# version 19
# baseline (speedup 1.0000x reference)
import numpy as np
import ml_dtypes
from contextlib import ExitStack

import concourse.bass as bass
import concourse.mybir as mybir
from concourse.bass_utils import run_bass_kernel_spmd

F32 = mybir.dt.float32
BF16 = mybir.dt.bfloat16
AF = mybir.ActivationFunctionType
ALU = mybir.AluOpType
AX = mybir.AxisListType
BF = ml_dtypes.bfloat16

B, SEQ, D = 2, 4096, 2048
NH, NKV, HD = 16, 4, 128
OQKV = 3072
INTER = 5632
NIC = INTER // 128
OWN = SEQ // 4
NT_ALL = SEQ // 128
NT_OWN = OWN // 128
RMS_EPS = 1.1920929e-07
LN_EPS = 1e-5
THR_C = 0.5009765625
INV_SQRT_HD = 0.08838834764831843


class Sem:
    def __init__(self, stack, nc, name):
        self.h = stack.enter_context(nc.semaphore(name))
        self.n = 0

    def inc(self, instr, k=1):
        instr.then_inc(self.h, k)
        self.n += k
        return instr

    def wait(self, eng, n=None):
        v = self.n if n is None else n
        if v > 0:
            eng.wait_ge(self.h, v)


def build_program(stop_after=None, ph2_taus=NT_OWN, ph2_quant=True, ph2_ytr=True, ph2_tail=True, debug=False):
    nc = bass.Bass("TRN2", target_bir_lowering=False)

    X_ALL = nc.dram_tensor("x_all", [SEQ, D], F32, kind="ExternalInput").ap()
    X_OWN = nc.dram_tensor("x_own", [OWN, D], F32, kind="ExternalInput").ap()
    WQ = nc.dram_tensor("wq", [OQKV, D], F32, kind="ExternalInput").ap()
    WP = nc.dram_tensor("wp", [D, D], F32, kind="ExternalInput").ap()
    GWT = nc.dram_tensor("gwt", [NIC, 128, 2048], F32, kind="ExternalInput").ap()
    UWT = nc.dram_tensor("uwt", [NIC, 128, 2048], F32, kind="ExternalInput").ap()
    DWT = nc.dram_tensor("dwt", [16, 128, NIC * 128], F32, kind="ExternalInput").ap()
    GS2 = nc.dram_tensor("gs2", [128, NIC], F32, kind="ExternalInput").ap()
    US2 = nc.dram_tensor("us2", [128, NIC], F32, kind="ExternalInput").ap()
    DS2 = nc.dram_tensor("ds2", [128, 16], F32, kind="ExternalInput").ap()
    QG2 = nc.dram_tensor("qg2", [128, 16], F32, kind="ExternalInput").ap()
    COSK = nc.dram_tensor("cosk", [128, NT_ALL, 64], BF16, kind="ExternalInput").ap()
    SINK = nc.dram_tensor("sink", [128, NT_ALL, 64], BF16, kind="ExternalInput").ap()
    COSQ = nc.dram_tensor("cosq", [128, NT_OWN, 64], BF16, kind="ExternalInput").ap()
    SINQ = nc.dram_tensor("sinq", [128, NT_OWN, 64], BF16, kind="ExternalInput").ap()
    MSK = nc.dram_tensor("msk", [128, 4, 512], BF16, kind="ExternalInput").ap()
    IDN = nc.dram_tensor("idn", [128, 128], BF16, kind="ExternalInput").ap()

    OUT = nc.dram_tensor("out_own", [OWN, D], F32, kind="ExternalOutput").ap()
    if debug:
        DBG_W = nc.dram_tensor("dbg_w", [128, 16, 1024], BF16, kind="ExternalOutput").ap()
        DBG_H = nc.dram_tensor("dbg_h", [128, NT_OWN, 16, 128], BF16, kind="ExternalOutput").ap()
        DBG_K = nc.dram_tensor("dbg_k", [128, NKV, SEQ], BF16, kind="ExternalOutput").ap()
        DBG_V = nc.dram_tensor("dbg_v", [128, NKV, NT_ALL, 132], BF16, kind="ExternalOutput").ap()
        DBG_Q = nc.dram_tensor("dbg_q", [128, NKV * NT_OWN * 4 * 128], BF16, kind="ExternalOutput").ap()
        DBG_R = nc.dram_tensor("dbg_r", [128, NT_ALL * 4], F32, kind="ExternalOutput").ap()
        DBG_Y = nc.dram_tensor("dbg_y", [128, NT_OWN, D], BF16, kind="ExternalOutput").ap()
        DBG_H2 = nc.dram_tensor("dbg_h2", [128, 16, OWN], BF16, kind="ExternalOutput").ap()

    WQ_EFF = nc.dram_tensor("wq_eff", [OQKV, D], BF16, kind="Internal").ap()
    WP_EFF = nc.dram_tensor("wp_eff", [D, D], BF16, kind="Internal").ap()

    st = ExitStack()
    with st:
        sb = lambda n, s, d: st.enter_context(nc.sbuf_tensor(n, s, d))
        ident = sb("ident", [128, 128], BF16)
        masks = sb("masks", [128, 4, 512], BF16)
        cosk = sb("cosk_t", [128, NT_ALL, 64], BF16)
        sink = sb("sink_t", [128, NT_ALL, 64], BF16)
        cosq = sb("cosq_t", [128, NT_OWN, 64], BF16)
        sinq = sb("sinq_t", [128, NT_OWN, 64], BF16)
        gs2 = sb("gs2_t", [128, NIC], F32)
        us2 = sb("us2_t", [128, NIC], F32)
        ds2 = sb("ds2_t", [128, 16], F32)
        qg2 = sb("qg2_t", [128, 16], F32)
        qgs = sb("qgs_t", [128, 16], F32)
        fence = sb("fence", [1, 16], F32)
        bn6 = sb("bn6", [128, 24, 6], F32)
        bn2 = sb("bn2", [128, 16, 2], F32)
        stV = sb("stV", [128, 4], F32)
        stLN = sb("stLN", [128, 4], F32)
        stR = sb("stR", [128, 4], F32)
        stMR = sb("stMR", [128, 4], F32)
        st1 = sb("st1", [128, 16], F32)
        st2 = sb("st2", [128, 16], F32)
        st3 = sb("st3", [128, 16], F32)
        st4 = sb("st4", [128, 512], BF16)
        st5 = sb("st5", [128, 16], F32)
        st6 = sb("st6", [128, 16], F32)
        st7 = sb("st7", [128, 16], F32)
        stb = sb("stb", [128, 16], BF16)
        st8 = sb("st8", [128, 16], F32)
        stL = sb("stL", [128, 8], F32)
        stK = sb("stK", [128, 16], F32)
        rstdk = sb("rstdk", [128, NT_ALL * 4], F32)

        sems = {}

        def mksem(name):
            sems[name] = Sem(st, nc, name)
            return sems[name]

        CONST = mksem("CONST")
        XL = mksem("XL")
        WTL = mksem("WTL")
        QBL = mksem("QBL")
        QST = mksem("QST")
        FWL = mksem("FWL")
        OST = mksem("OST")
        OACC = mksem("OACC")
        BAR = mksem("BAR")
        LNS = mksem("LNS")
        LNR = mksem("LNR")
        MRD = mksem("MRD")
        H1D = mksem("H1D")
        TRP = mksem("TRP")
        TRE = mksem("TRE")
        MMD = mksem("MMD")
        KEV = mksem("KEV")
        KBN = mksem("KBN")
        KRS = mksem("KRS")
        KRO = mksem("KRO")
        VEV = mksem("VEV")
        KTR = mksem("KTR")
        KTE = mksem("KTE")
        QED = mksem("QED")
        QTH = mksem("QTH")
        QNG = mksem("QNG")
        QPS = mksem("QPS")
        QWE = mksem("QWE")
        QKD = mksem("QKD")
        EXPD = mksem("EXPD")
        MSKD = mksem("MSKD")
        AVD = mksem("AVD")
        OEV = mksem("OEV")
        YST = mksem("YST")
        YRS = mksem("YRS")
        YND = mksem("YND")
        PJE = mksem("PJE")
        H2D = mksem("H2D")
        GUE = mksem("GUE")
        HDN = mksem("HDN")
        DSE = mksem("DSE")
        DTE = mksem("DTE")
        MMG = mksem("MMG")
        UBE = mksem("UBE")
        FWU = mksem("FWU")
        DWL = mksem("DWL")
        VSY = mksem("VSY")
        ASY = mksem("ASY")
        QAB = mksem("QAB")
        KSC = mksem("KSC")
        YTP = mksem("YTP")
        YTE = mksem("YTE")

        blk = st.enter_context(nc.Block())

        fence_ops = {
            "vector": lambda v: v.tensor_copy(fence[0:1, 0:1], fence[0:1, 1:2]),
            "scalar": lambda s: s.copy(fence[0:1, 2:3], fence[0:1, 3:4]),
            "gpsimd": lambda g: g.tensor_copy(fence[0:1, 4:5], fence[0:1, 5:6]),
            "tensor": lambda t: t.nop(),
        }

        def vs(v, instr):
            VSY.inc(instr)
            v.wait_ge(VSY.h, VSY.n)

        def asy(s, instr):
            ASY.inc(instr)
            s.wait_ge(ASY.h, ASY.n)

        def barrier():
            for e in ("vector", "scalar", "gpsimd", "tensor"):
                def emit_f(eng, e=e):
                    BAR.inc(fence_ops[e](eng))
                getattr(blk, e)(emit_f)
            tgt = BAR.n
            dma_targets = [(s, s.n) for s in (CONST, XL, WTL, QBL, QST, FWL,
                                              FWU, DWL, OST, OACC)]

            def waits(eng):
                eng.wait_ge(BAR.h, tgt)
                for s_, v in dma_targets:
                    if v > 0:
                        eng.wait_ge(s_.h, v)
            for e in ("sync", "gpsimd", "vector", "scalar", "tensor"):
                getattr(blk, e)(waits)

        LNG = [0, 8, 12, 16]

        def make_ln(idx, p, x_t, h_t, extra_dve_wait=None, extra_act_wait=None):
            g0 = LNG[p]

            def dve1(v, idx=idx, p=p, g0=g0):
                if extra_dve_wait is not None:
                    extra_dve_wait(v)
                for g4 in range(4):
                    i = v.bn_stats(bn6[:, g0 + g4, :],
                                   x_t[:, g4 * 512:(g4 + 1) * 512])
                vs(v, i)
                means = bn6[:, g0:g0 + 4, 1::3]
                ctvs = bn6[:, g0:g0 + 4, 2::3]
                st8v = st8[:, 0:8].rearrange("p (g t) -> p g t", g=4)
                v.tensor_reduce(stL[:, 0:1], means, AX.XY, ALU.add)
                v.tensor_reduce(stL[:, 1:2], ctvs, AX.XY, ALU.add)
                i = v.tensor_tensor(st8v, means, means, ALU.mult)
                vs(v, i)
                v.tensor_reduce(stL[:, 2:3], st8v, AX.XY, ALU.add)
                v.tensor_scalar(bn2[:, p, 0:1], stL[:, 0:1], 0.125, None,
                                ALU.mult)
                v.scalar_tensor_tensor(stL[:, 3:4], stL[:, 0:1],
                                       -0.125 * 0.125, stL[:, 0:1],
                                       ALU.mult, ALU.mult)
                i = v.tensor_scalar(stL[:, 4:5], stL[:, 1:2], 1.0 / 2048,
                                    LN_EPS, ALU.mult, ALU.add)
                vs(v, i)
                i = v.scalar_tensor_tensor(stL[:, 5:6], stL[:, 2:3], 0.125,
                                           stL[:, 3:4], ALU.mult, ALU.add)
                vs(v, i)
                LNS.inc(v.scalar_tensor_tensor(stV[:, p:p + 1], stL[:, 5:6],
                                               1.0, stL[:, 4:5],
                                               ALU.mult, ALU.add))

            def act1(s, idx=idx, p=p):
                LNS.wait(s, idx + 1)
                i = s.activation(stLN[:, p:p + 1], stV[:, p:p + 1], AF.Ln)
                asy(s, i)
                LNR.inc(s.activation(stR[:, p:p + 1], stLN[:, p:p + 1], AF.Exp,
                                     scale=-0.5))

            def dve2(v, idx=idx, p=p):
                LNR.wait(v, idx + 1)
                MRD.inc(v.scalar_tensor_tensor(stMR[:, p:p + 1], bn2[:, p, 0:1],
                                               -1.0, stR[:, p:p + 1],
                                               ALU.mult, ALU.mult))

            def act2(s, idx=idx, p=p):
                MRD.wait(s, idx + 1)
                if extra_act_wait is not None:
                    extra_act_wait(s)
                H1D.inc(s.activation(h_t[:, :], x_t[:, :], AF.Identity,
                                     bias=stMR[:, p:p + 1],
                                     scale=stR[:, p:p + 1]))
            return dve1, act1, dve2, act2

        def emit_ln(idx, p, x_t, h_t, extra_dve_wait=None, extra_act_wait=None):
            d1, a1, d2, a2 = make_ln(idx, p, x_t, h_t, extra_dve_wait,
                                     extra_act_wait)
            blk.vector(d1)
            blk.scalar(a1)
            blk.vector(d2)
            blk.scalar(a2)

        def emit_kstats(v, p2, seg):
            kb = 4 if p2 == 0 else 20
            mp = 8 * p2
            for h4 in range(4):
                i = v.bn_stats(bn6[:, kb + h4, :], seg[:, h4, :])
            vs(v, i)
            means = bn6[:, kb:kb + 4, 1::3]
            ctvs = bn6[:, kb:kb + 4, 2::3]
            st8v = st8[:, mp:mp + 8].rearrange("p (g t) -> p g t", g=4)
            v.tensor_reduce(stK[:, mp:mp + 4], ctvs, AX.X, ALU.add)
            i = v.tensor_tensor(st8v, means, means, ALU.mult)
            vs(v, i)
            i = v.tensor_reduce(stK[:, mp + 4:mp + 8], st8v, AX.X, ALU.add)
            vs(v, i)
            i = v.scalar_tensor_tensor(st8[:, mp:mp + 4], stK[:, mp + 4:mp + 8],
                                       64.0, stK[:, mp:mp + 4],
                                       ALU.mult, ALU.add)
            vs(v, i)
            KBN.inc(v.tensor_scalar(st1[:, 4 + 4 * p2:8 + 4 * p2],
                                    st8[:, mp:mp + 4], 1.0 / 128, RMS_EPS,
                                    ALU.mult, ALU.add))

        def emit_tr16(src_t, dst_fn, ps_tr, h1d_idx, trp_base, tre_base, gbase,
                      pe_extra=None):
            def pe1(t):
                H1D.wait(t, h1d_idx + 1)
                if pe_extra is not None:
                    pe_extra(t)
                for grp in range(4):
                    gi = gbase + grp
                    TRE.wait(t, tre_base + max(0, gi - 1))
                    last = None
                    for k in range(4):
                        dc = grp * 4 + k
                        last = t.transpose(ps_tr[gi % 2][:, k * 128:(k + 1) * 128],
                                           src_t[:, dc * 128:(dc + 1) * 128],
                                           ident[:, :])
                    TRP.inc(last)
            blk.tensor(pe1)

            def dve3(v):
                for grp in range(4):
                    gi = gbase + grp
                    TRP.wait(v, trp_base + gi + 1)
                    TRE.inc(v.tensor_copy(
                        dst_fn(grp),
                        ps_tr[gi % 2][:, 0:512].rearrange("p (k f) -> p k f", k=4)))
            blk.vector(dve3)

        def emit_quant(k, w_bf, weff_dst):
            def qdve(v, k=k):
                QBL.wait(v, 16 * (k + 1))
                i = v.tensor_reduce(st5[:, :],
                                    w_bf[:, :].rearrange("p (g d) -> p g d", g=16),
                                    AX.X, ALU.add, apply_absolute_value=True)
                vs(v, i)
                i = v.tensor_scalar(stb[:, :], st5[:, :], 1.0 / 128.0, None,
                                    ALU.mult)
                vs(v, i)
                i = v.tensor_copy(st6[:, :], stb[:, :])
                vs(v, i)
                v.tensor_scalar(st7[:, :], st6[:, :], THR_C, None, ALU.mult)
                i = v.tensor_scalar(st5[:, :], st6[:, :], -THR_C, None, ALU.mult)
                vs(v, i)
                v.tensor_copy(qbuf['thrf'][:, :, :],
                              st7[:, :].unsqueeze(2).broadcast_to((128, 16, 128)))
                v.tensor_copy(qbuf['thrn'][:, :, :],
                              st5[:, :].unsqueeze(2).broadcast_to((128, 16, 128)))
                i = v.tensor_copy(qbuf['scf'][:, :, :],
                                  stb[:, :].unsqueeze(2).broadcast_to((128, 16, 128)))
                vs(v, i)
                wv = w_bf[:, :].rearrange("p (g d) -> p g d", g=16)
                awv = qbuf['aw'][k % 2][:, :].rearrange("p (g d) -> p g d", g=16)
                sgv = qbuf['sg'][k % 2][:, :].rearrange("p (g d) -> p g d", g=16)
                v.tensor_tensor(awv, wv, qbuf['thrf'][:, :, :], ALU.is_gt)
                i = v.tensor_tensor(sgv, wv, qbuf['thrn'][:, :, :], ALU.is_lt)
                vs(v, i)
                i = v.tensor_tensor(qbuf['aw'][k % 2][:, :], qbuf['aw'][k % 2][:, :],
                                    qbuf['sg'][k % 2][:, :], ALU.subtract)
                vs(v, i)
                QWE.inc(v.tensor_tensor(
                    weff_dst[:, :].rearrange("p (g d) -> p g d", g=16),
                    qbuf['aw'][k % 2][:, :].rearrange("p (g d) -> p g d", g=16),
                    qbuf['scf'][:, :, :], ALU.mult))
            blk.vector(qdve)

        def ph0_sync(sy):
            for dst, src in ((ident, IDN), (masks, MSK), (cosk, COSK),
                             (sink, SINK), (cosq, COSQ), (sinq, SINQ),
                             (gs2, GS2), (us2, US2), (ds2, DS2), (qg2, QG2)):
                CONST.inc(sy.dma_start(out=dst[...], in_=src[...]), 16)
        blk.sync(ph0_sync)

        def ph0_dve(v):
            v.memset(fence[0:1, 0:2], 0.0)
            CONST.wait(v)
            v.tensor_scalar(qgs[:, :], qg2[:, :], INV_SQRT_HD, None, ALU.mult)
        blk.vector(ph0_dve)

        def ph0_act(s):
            s.memzero(fence[0:1, 2:4])
        blk.scalar(ph0_act)

        def ph0_pool(g):
            g.memset(fence[0:1, 4:6], 0.0)
        blk.gpsimd(ph0_pool)
        barrier()
        if stop_after == 0:
            return nc

        sA = ExitStack()
        K_fm = sA.enter_context(nc.sbuf_tensor("K_fm", [128, NKV, SEQ], BF16))
        V_aug = sA.enter_context(nc.sbuf_tensor("V_aug", [128, NKV, NT_ALL, 132], BF16))

        qbuf = {}

        def alloc_quant_scratch(scope, tag):
            qbuf['aw'] = [scope.enter_context(
                nc.sbuf_tensor(f"qaw{tag}", [128, D], BF16))] * 2
            qbuf['sg'] = [scope.enter_context(
                nc.sbuf_tensor(f"qsg{tag}", [128, D], BF16))] * 2
            qbuf['wbf'] = [scope.enter_context(
                nc.sbuf_tensor(f"qwbf{tag}{i}", [128, D], BF16)) for i in range(2)]
            qbuf['thrf'] = scope.enter_context(
                nc.sbuf_tensor(f"qthrf{tag}", [128, 16, 128], F32))
            qbuf['thrn'] = scope.enter_context(
                nc.sbuf_tensor(f"qthrn{tag}", [128, 16, 128], F32))
            qbuf['scf'] = scope.enter_context(
                nc.sbuf_tensor(f"qscf{tag}", [128, 16, 128], BF16))

        sQ = ExitStack()
        alloc_quant_scratch(sQ, "a")

        def quant_src_dst(k):
            if k < 8:
                wt = 16 + k
                return (WQ[wt * 128:(wt + 1) * 128, :],
                        WQ_EFF[wt * 128:(wt + 1) * 128, :])
            if k < 24:
                wt = k - 8
                return (WQ[wt * 128:(wt + 1) * 128, :],
                        WQ_EFF[wt * 128:(wt + 1) * 128, :])
            wt = k - 24
            return (WP[wt * 128:(wt + 1) * 128, :],
                    WP_EFF[wt * 128:(wt + 1) * 128, :])

        def emit_quant_tile(k):
            srcap, dstap = quant_src_dst(k)

            def pload(g, k=k, srcap=srcap):
                QST.wait(g, 16 * max(0, k - 1))
                QBL.inc(g.dma_start(out=qbuf['wbf'][k % 2][:, :], in_=srcap), 16)
            blk.gpsimd(pload)
            emit_quant(k, qbuf['wbf'][k % 2], qbuf['wbf'][k % 2])

            def pstore(g, k=k, dstap=dstap):
                QWE.wait(g, k + 1)
                QST.inc(g.dma_start(out=dstap, in_=qbuf['wbf'][k % 2][:, :]), 16)
            blk.gpsimd(pstore)

        for k in range(8):
            emit_quant_tile(k)
        barrier()

        with ExitStack() as s1b:
            sb1 = lambda n, s, d: s1b.enter_context(nc.sbuf_tensor(n, s, d))
            ps1 = lambda n, s, d: s1b.enter_context(nc.psum_tensor(n, s, d))
            x_f = [sb1(f"x{i}", [128, D], F32) for i in range(3)]
            h1 = [sb1(f"h1{i}", [128, D], BF16) for i in range(2)]
            h1T = [sb1(f"h1T{i}", [128, 16, 128], BF16) for i in range(2)]
            wkvT = sb1("wkvT", [128, 16, 1024], BF16)
            kbf = [sb1(f"kbf{i}", [128, 512], BF16) for i in range(2)]
            krope = [sb1(f"krope{i}", [128, 512], BF16) for i in range(2)]
            ps_tr = [ps1(f"pstr{i}", [128, 1024], BF16) for i in range(2)]
            ps_ktr = [ps1(f"psktr{i}", [128, 1024], BF16) for i in range(2)]
            ps_k = [ps1(f"psk{i}", [128, 512], F32) for i in range(2)]
            ps_v = [ps1(f"psv{i}", [128, 512], F32) for i in range(2)]

            def dve_init(v):
                v.memset(V_aug[:, :, :, 128:129], 1.0)
            blk.vector(dve_init)

            def sync_wkvT(sy):
                QST.wait(sy, 16 * 8)
                for dc in range(16):
                    for half in range(2):
                        WTL.inc(sy.dma_start(
                            out=wkvT[:, dc, half * 512:(half + 1) * 512],
                            in_=WQ_EFF[2048 + half * 512:2048 + (half + 1) * 512,
                                       dc * 128:(dc + 1) * 128],
                            transpose=True), 16)
            blk.sync(sync_wkvT)
            wkvT_t = WTL.n

            def st_load(tt):
                def sy(sy_, tt=tt):
                    if tt >= 3:
                        H1D.wait(sy_, tt - 2)
                    XL.inc(sy_.dma_start(out=x_f[tt % 3][:, :],
                                         in_=X_ALL[tt * 128:(tt + 1) * 128, :]), 16)
                blk.sync(sy)

            def st_tr(tt):
                def pe1(t, tt=tt):
                    H1D.wait(t, tt + 1)
                    if tt >= 2:
                        MMD.wait(t, 2 * (tt - 1))
                    TRE.wait(t, 2 * tt)
                    for half in range(2):
                        last = None
                        for kk in range(8):
                            dc = half * 8 + kk
                            last = t.transpose(
                                ps_tr[half][:, kk * 128:(kk + 1) * 128],
                                h1[tt % 2][:, dc * 128:(dc + 1) * 128],
                                ident[:, :])
                        TRP.inc(last)
                blk.tensor(pe1)

            def st_trev(tt):
                def dve3(v, tt=tt):
                    for half in range(2):
                        TRP.wait(v, 2 * tt + half + 1)
                        dst = h1T[tt % 2][:, half * 8:(half + 1) * 8, :]
                        TRE.inc(v.tensor_copy(
                            dst,
                            ps_tr[half][:, :].rearrange("p (k f) -> p k f",
                                                        k=8)))
                blk.vector(dve3)

            def st_kvmm(tt):
                def pe2(t, tt=tt):
                    if tt == 0:
                        WTL.wait(t, wkvT_t)
                    TRE.wait(t, 2 * (tt + 1))
                    KEV.wait(t, max(0, tt - 1))
                    VEV.wait(t, max(0, tt - 1))
                    last_v = None
                    for dc in range(16):
                        t.matmul(ps_k[tt % 2][:, :], h1T[tt % 2][:, dc, :],
                                 wkvT[:, dc, 0:512],
                                 start=(dc == 0), stop=(dc == 15))
                        last_v = t.matmul(ps_v[tt % 2][:, :], h1T[tt % 2][:, dc, :],
                                          wkvT[:, dc, 512:1024],
                                          start=(dc == 0), stop=(dc == 15))
                    MMD.inc(last_v, 2)
                blk.tensor(pe2)

            def st_kevac(tt):
                def act3(s, tt=tt):
                    MMD.wait(s, 2 * (tt + 1))
                    KRO.wait(s, max(0, tt - 1))
                    KEV.inc(s.activation(kbf[tt % 2][:, :], ps_k[tt % 2][:, :],
                                         AF.Copy))
                blk.scalar(act3)

            def st_kstats(tt):
                def dve4(v, tt=tt):
                    KEV.wait(v, tt + 1)
                    KRS.wait(v, max(0, tt - 1))
                    kseg = kbf[tt % 2][:, :].rearrange("p (h d) -> p h d", h=4)
                    emit_kstats(v, tt % 2, kseg)
                    KTR.wait(v, max(0, tt - 1))
                    k1 = kseg[:, :, 0:64]
                    k2 = kseg[:, :, 64:128]
                    co = cosk[:, tt, :].unsqueeze(1).broadcast_to((128, 4, 64))
                    si = sink[:, tt, :].unsqueeze(1).broadcast_to((128, 4, 64))
                    kr = krope[tt % 2][:, :].rearrange("p (h d) -> p h d", h=4)
                    t1 = st4[:, :].rearrange("p (h d) -> p h d", h=4)[:, :, 0:64]
                    t2 = st4[:, :].rearrange("p (h d) -> p h d", h=4)[:, :, 64:128]
                    v.tensor_tensor(t1, k1, co, ALU.mult)
                    v.tensor_tensor(t2, k2, si, ALU.mult)
                    v.tensor_tensor(kr[:, :, 0:64], t1, t2, ALU.add)
                    v.tensor_tensor(t1, k2, co, ALU.mult)
                    v.tensor_tensor(t2, k1, si, ALU.mult)
                    KRO.inc(v.tensor_tensor(kr[:, :, 64:128], t1, t2,
                                            ALU.subtract))
                    MMD.wait(v, 2 * (tt + 1))
                    last = v.tensor_copy(
                        V_aug[:, :, tt, 0:128].rearrange("p h d -> p h d"),
                        ps_v[tt % 2][:, :].rearrange("p (h d) -> p h d", h=4))
                    VEV.inc(last)
                blk.vector(dve4)

            def st_krstd(tt):
                def act4(s, tt=tt):
                    p2 = tt % 2
                    KBN.wait(s, tt + 1)
                    i = s.activation(st2[:, 4 + 4 * p2:8 + 4 * p2],
                                     st1[:, 4 + 4 * p2:8 + 4 * p2], AF.Ln)
                    asy(s, i)
                    KRS.inc(s.activation(rstdk[:, tt * 4:(tt + 1) * 4],
                                         st2[:, 4 + 4 * p2:8 + 4 * p2],
                                         AF.Exp, scale=-0.5))
                blk.scalar(act4)

            def st_kscale(tt):
                def dvf(v, tt=tt):
                    KRS.wait(v, tt + 1)
                    last = None
                    for h in range(4):
                        last = v.tensor_scalar(
                            krope[tt % 2][:, h * 128:(h + 1) * 128],
                            krope[tt % 2][:, h * 128:(h + 1) * 128],
                            rstdk[:, tt * 4 + h:tt * 4 + h + 1], None, ALU.mult)
                    KSC.inc(last)
                blk.vector(dvf)

            def st_ktr(tt):
                def pe3(t, tt=tt):
                    KSC.wait(t, tt + 1)
                    KTE.wait(t, max(0, tt - 1))
                    last = None
                    for h in range(4):
                        last = t.transpose(ps_ktr[tt % 2][:, h * 128:(h + 1) * 128],
                                           krope[tt % 2][:, h * 128:(h + 1) * 128],
                                           ident[:, :])
                    KTR.inc(last)
                blk.tensor(pe3)

            def st_ktrev(tt):
                def dve5(v, tt=tt):
                    KTR.wait(v, tt + 1)
                    KTE.inc(v.tensor_copy(
                        V_aug[:, 0, 0, 0:1] if False else
                        K_fm[:, :, tt * 128:(tt + 1) * 128].rearrange(
                            "p h d -> p h d"),
                        ps_ktr[tt % 2][:, 0:512].rearrange("p (h d) -> p h d",
                                                           h=4)))
                blk.vector(dve5)

            def in_range(tt):
                return 0 <= tt < NT_ALL

            ln_stash = {}
            for m in range(NT_ALL + 8):
                if in_range(m + 1) or m == 0:
                    if m == 0:
                        st_load(0)
                    if in_range(m + 1):
                        st_load(m + 1)
                if in_range(m):
                    lns = make_ln(m, m % 4, x_f[m % 3], h1[m % 2],
                                  extra_dve_wait=(
                                      lambda v, tt=m: XL.wait(v, 16 * (tt + 1))),
                                  extra_act_wait=(
                                      lambda s, tt=m: TRP.wait(s, 2 * (tt - 1))
                                      if tt >= 2 else None))
                else:
                    lns = None
                if in_range(m - 4):
                    st_trev(m - 4)
                if in_range(m - 7):
                    st_ktrev(m - 7)
                if lns:
                    ln_stash[m % 4] = lns
                    blk.vector(lns[0])
                if in_range(m - 1):
                    blk.scalar(ln_stash[(m - 1) % 4][1])
                if in_range(m - 2):
                    blk.vector(ln_stash[(m - 2) % 4][2])
                    blk.scalar(ln_stash[(m - 2) % 4][3])
                if in_range(m - 3):
                    st_tr(m - 3)
                if in_range(m - 4):
                    st_kvmm(m - 4)
                if in_range(m - 6):
                    st_krstd(m - 6)
                if in_range(m - 5):
                    st_kevac(m - 5)
                    st_kstats(m - 5)
                if in_range(m - 6):
                    st_kscale(m - 6)
                    st_ktr(m - 6)
                if m >= 7 and (m - 7) % 2 == 0 and (m - 7) // 2 < 16:
                    emit_quant_tile(8 + (m - 7) // 2)
            barrier()
            if debug:
                def dbgw(g):
                    OST.inc(g.dma_start(out=DBG_W[...], in_=wkvT[...]), 16)
                barrier()
                blk.gpsimd(dbgw)
            barrier()
        sQ.close()
        Q4 = sA.enter_context(
            nc.sbuf_tensor("Q4", [128, NKV * NT_OWN * 4 * 128], BF16))
        if stop_after == 2:
            sA.close()
            return nc

        with ExitStack() as s1c:
            sb1 = lambda n, s, d: s1c.enter_context(nc.sbuf_tensor(n, s, d))
            ps1 = lambda n, s, d: s1c.enter_context(nc.psum_tensor(n, s, d))
            x_f = [sb1(f"xo{i}", [128, D], F32) for i in range(3)]
            h1 = [sb1(f"h1o{i}", [128, D], BF16) for i in range(2)]
            h1ownT = sb1("h1ownT", [128, NT_OWN, 16, 128], BF16)
            wqT = sb1("wqT", [128, 16, 512], BF16)
            qbf = [sb1(f"qbf{i}", [128, 512], BF16) for i in range(2)]
            qrope = [sb1(f"qrope{i}", [128, 512], BF16) for i in range(3)]
            qfin = [sb1(f"qfin{i}", [128, 512], BF16) for i in range(2)]
            ps_tr = [ps1(f"pstrc{i}", [128, 1024], BF16) for i in range(2)]
            ps_qtr = [ps1(f"psqtr{i}", [128, 1024], BF16) for i in range(2)]
            ps_q = [ps1(f"psq{i}", [128, 512], F32) for i in range(2)]

            trp0, tre0 = TRP.n, TRE.n
            ktr0, kte0 = KTR.n, KTE.n
            mmd0, kev0, kbn0, krs0, kro0 = MMD.n, KEV.n, KBN.n, KRS.n, KRO.n
            h1d0, xl0, qed0 = H1D.n, XL.n, QED.n

            def st_load(tt):
                def sy(sy_, tt=tt):
                    if tt >= 3:
                        H1D.wait(sy_, h1d0 + tt - 2)
                    XL.inc(sy_.dma_start(out=x_f[tt % 3][:, :],
                                         in_=X_OWN[tt * 128:(tt + 1) * 128, :]), 16)
                blk.sync(sy)

            def st_tr(tt):
                def pe1(t, tt=tt):
                    H1D.wait(t, h1d0 + tt + 1)
                    TRE.wait(t, tre0 + 2 * tt)
                    for half in range(2):
                        last = None
                        for kk in range(8):
                            dc = half * 8 + kk
                            last = t.transpose(
                                ps_tr[half][:, kk * 128:(kk + 1) * 128],
                                h1[tt % 2][:, dc * 128:(dc + 1) * 128],
                                ident[:, :])
                        TRP.inc(last)
                blk.tensor(pe1)

            def st_trev(tt):
                def dve3(v, tt=tt):
                    for half in range(2):
                        TRP.wait(v, trp0 + 2 * tt + half + 1)
                        dst = h1ownT[:, tt, half * 8:(half + 1) * 8, :]
                        TRE.inc(v.tensor_copy(
                            dst,
                            ps_tr[half][:, :].rearrange("p (k f) -> p k f", k=8)))
                blk.vector(dve3)

            ln_stash = {}
            for m in range(NT_OWN + 4):
                if m == 0:
                    st_load(0)
                if m + 1 < NT_OWN:
                    st_load(m + 1)
                if m < NT_OWN:
                    lns = make_ln(h1d0 + m, m % 4, x_f[m % 3], h1[m % 2],
                                  extra_dve_wait=(
                                      lambda v, tt=m: XL.wait(
                                          v, xl0 + 16 * (tt + 1))),
                                  extra_act_wait=(
                                      lambda s, tt=m: TRP.wait(
                                          s, trp0 + 2 * (tt - 1))
                                      if tt >= 2 else None))
                    ln_stash[m % 4] = lns
                    blk.vector(lns[0])
                if 0 <= m - 1 < NT_OWN:
                    blk.scalar(ln_stash[(m - 1) % 4][1])
                if 0 <= m - 2 < NT_OWN:
                    blk.vector(ln_stash[(m - 2) % 4][2])
                    blk.scalar(ln_stash[(m - 2) % 4][3])
                if 0 <= m - 3 < NT_OWN:
                    st_tr(m - 3)
                if 0 <= m - 4 < NT_OWN:
                    st_trev(m - 4)

            wq_t = {}

            def st_wqload(ob):
                def syw(sy_, ob=ob):
                    QST.wait(sy_, 16 * 24)
                    if ob >= 1:
                        MMD.wait(sy_, mmd0 + 8 * ob)
                    for dc in range(16):
                        WTL.inc(sy_.dma_start(
                            out=wqT[:, dc, :],
                            in_=WQ_EFF[ob * 512:(ob + 1) * 512,
                                       dc * 128:(dc + 1) * 128],
                            transpose=True), 16)
                blk.sync(syw)
                wq_t[ob] = WTL.n

            def st_qmm(n):
                ob, tt = n // 8, n % 8

                def pe2(t, ob=ob, tt=tt, n=n):
                    if tt == 0:
                        WTL.wait(t, wq_t[ob])
                    TRE.wait(t, tre0 + 2 * (tt + 1))
                    KEV.wait(t, kev0 + max(0, n - 1))
                    last = None
                    for dc in range(16):
                        last = t.matmul(ps_q[n % 2][:, :], h1ownT[:, tt, dc, :],
                                        wqT[:, dc, :],
                                        start=(dc == 0), stop=(dc == 15))
                    MMD.inc(last)
                blk.tensor(pe2)

            def st_qevac(n):
                def act3(s, n=n):
                    MMD.wait(s, mmd0 + n + 1)
                    KRO.wait(s, kro0 + max(0, n - 1))
                    KEV.inc(s.activation(qbf[n % 2][:, :], ps_q[n % 2][:, :],
                                         AF.Copy))
                blk.scalar(act3)

            def st_qstats(n):
                ob, tt = n // 8, n % 8

                def dve4(v, ob=ob, tt=tt, n=n):
                    KEV.wait(v, kev0 + n + 1)
                    KRS.wait(v, krs0 + max(0, n - 1))
                    qseg = qbf[n % 2][:, :].rearrange("p (h d) -> p h d", h=4)
                    emit_kstats(v, n % 2, qseg)
                    KTR.wait(v, ktr0 + max(0, n - 2))
                    q1 = qseg[:, :, 0:64]
                    q2 = qseg[:, :, 64:128]
                    co = cosq[:, tt, :].unsqueeze(1).broadcast_to((128, 4, 64))
                    si = sinq[:, tt, :].unsqueeze(1).broadcast_to((128, 4, 64))
                    qr = qrope[n % 3][:, :].rearrange("p (h d) -> p h d", h=4)
                    t1 = st4[:, :].rearrange("p (h d) -> p h d", h=4)[:, :, 0:64]
                    t2 = st4[:, :].rearrange("p (h d) -> p h d", h=4)[:, :, 64:128]
                    v.tensor_tensor(t1, q1, co, ALU.mult)
                    v.tensor_tensor(t2, q2, si, ALU.mult)
                    v.tensor_tensor(qr[:, :, 0:64], t1, t2, ALU.add)
                    v.tensor_tensor(t1, q2, co, ALU.mult)
                    v.tensor_tensor(t2, q1, si, ALU.mult)
                    KRO.inc(v.tensor_tensor(qr[:, :, 64:128], t1, t2,
                                            ALU.subtract))
                blk.vector(dve4)

            def st_qrstd(n):
                def act4(s, n=n):
                    p2 = n % 2
                    p3 = n % 3
                    KBN.wait(s, kbn0 + n + 1)
                    i = s.activation(st2[:, 4 + 4 * p2:8 + 4 * p2],
                                     st1[:, 4 + 4 * p2:8 + 4 * p2], AF.Ln)
                    asy(s, i)
                    KRS.inc(s.activation(st3[:, 4 * p3:4 * p3 + 4],
                                         st2[:, 4 + 4 * p2:8 + 4 * p2],
                                         AF.Exp, scale=-0.5))
                blk.scalar(act4)

            def st_qscale(n):
                ob = n // 8

                def dve5(v, ob=ob, n=n):
                    p3 = n % 3
                    KRS.wait(v, krs0 + n + 1)
                    KTR.wait(v, ktr0 + max(0, n - 1))
                    i = v.tensor_tensor(st3[:, 12:16],
                                        st3[:, 4 * p3:4 * p3 + 4],
                                        qgs[:, ob * 4:(ob + 1) * 4], ALU.mult)
                    vs(v, i)
                    last = None
                    for g in range(4):
                        last = v.tensor_scalar(qfin[n % 2][:, g * 128:(g + 1) * 128],
                                               qrope[n % 3][:, g * 128:(g + 1) * 128],
                                               st3[:, 12 + g:13 + g],
                                               None, ALU.mult)
                    QED.inc(last)
                blk.vector(dve5)

            def st_qtr(n):
                def pe3(t, n=n):
                    QED.wait(t, qed0 + n + 1)
                    KTE.wait(t, kte0 + max(0, n - 1))
                    last = None
                    for g in range(4):
                        last = t.transpose(ps_qtr[n % 2][:, g * 128:(g + 1) * 128],
                                           qfin[n % 2][:, g * 128:(g + 1) * 128],
                                           ident[:, :])
                    KTR.inc(last)
                blk.tensor(pe3)

            def st_qtrev(n):
                ob, tt = n // 8, n % 8

                def dve6(v, ob=ob, tt=tt, n=n):
                    KTR.wait(v, ktr0 + n + 1)
                    off = ((ob * NT_OWN + tt) * 4) * 128
                    KTE.inc(v.tensor_copy(Q4[:, off:off + 512],
                                          ps_qtr[n % 2][:, 0:512]))
                blk.vector(dve6)

            NQ = 32
            for m in range(NQ + 6):
                if m % 8 == 0 and m // 8 < 4:
                    st_wqload(m // 8)
                if 0 <= m - 6 < NQ:
                    st_qtrev(m - 6)
                if m < NQ:
                    st_qmm(m)
                if 0 <= m - 1 < NQ:
                    st_qevac(m - 1)
                if 0 <= m - 2 < NQ:
                    st_qstats(m - 2)
                if 0 <= m - 3 < NQ:
                    st_qrstd(m - 3)
                if 0 <= m - 4 < NQ:
                    st_qscale(m - 4)
                if 0 <= m - 5 < NQ:
                    st_qtr(m - 5)
            if debug:
                def dbgh(g):
                    OST.inc(g.dma_start(out=DBG_H[...], in_=h1ownT[...]), 16)
                barrier()
                blk.gpsimd(dbgh)
            barrier()
        if debug:
            def dbg1(g):
                OST.inc(g.dma_start(out=DBG_K[...], in_=K_fm[...]), 16)
                OST.inc(g.dma_start(out=DBG_V[...], in_=V_aug[...]), 16)
                OST.inc(g.dma_start(out=DBG_Q[...], in_=Q4[...]), 16)
                OST.inc(g.dma_start(out=DBG_R[...], in_=rstdk[...]), 16)
            blk.gpsimd(dbg1)
            barrier()
        if stop_after == 3:
            sA.close()
            return nc

        sB = ExitStack()
        y_nB = sB.enter_context(nc.sbuf_tensor("y_nB", [128, NT_OWN, D], BF16,
                                               side="right"))

        with ExitStack() as s2:
            sb1 = lambda n, s, d: s2.enter_context(nc.sbuf_tensor(n, s, d))
            ps1 = lambda n, s, d: s2.enter_context(nc.psum_tensor(n, s, d))
            alloc_quant_scratch(s2, "b")
            P_buf = sb1("P_buf", [128, 8 * 512], BF16)
            y_tm = [sb1(f"ytm{i}", [128, D], F32) for i in range(2)]
            linv = sb1("linv", [128, 4], F32)
            ps_s = ps1("ps_s", [128, 4 * 512], F32)
            ps_o = [ps1(f"pso{i}", [128, 512], F32) for i in range(4)]

            qkd0, expd0, mskd0, avd0, oev0 = QKD.n, EXPD.n, MSKD.n, AVD.n, OEV.n
            yst0, yrs0, ynd0 = YST.n, YRS.n, YND.n
            trp0, tre0 = TRP.n, TRE.n
            qbases = (QBL.n, QTH.n, QNG.n, QPS.n, QWE.n, QST.n)
            qwe0, qst0, qps0 = QWE.n, QST.n, QPS.n

            nj_of = [4 * tau + 4 for tau in range(NT_OWN)]
            gc_of = [0] * NT_OWN
            dg_of = [0] * NT_OWN
            for tau in range(1, NT_OWN):
                gc_of[tau] = gc_of[tau - 1] + 4 * nj_of[tau - 1]
                dg_of[tau] = dg_of[tau - 1] + 16

            pend_rms = [None]
            pend_ynd = [None]

            for tau in range(ph2_taus):
                nj = nj_of[tau]
                gc_tau = gc_of[tau]
                dg_tau = dg_of[tau]

                LA = 4

                def emit_qk(t, tau, h, j, c):
                    EXPD.wait(t, expd0 + max(0, c // 2 - 1))
                    qoff = (h * NT_OWN + tau) * 4 * 128
                    so = (c % 4) * 512
                    QKD.inc(t.matmul(ps_s[:, so:so + 512],
                                     K_fm[:, h, j * 128:(j + 1) * 128],
                                     Q4[:, qoff:qoff + 512],
                                     start=True, stop=True))

                def emit_av(t, tau, h, j, c, nj, dcnt, s_i):
                    if j == 0 and s_i > 0:
                        OEV.wait(t, oev0 + s_i)
                    if j >= nj - 4:
                        MSKD.wait(t, mskd0 + dcnt)
                    else:
                        EXPD.wait(t, expd0 + c // 2 + 1)
                    last = None
                    po = (c % 8) * 512
                    for g in range(4):
                        last = t.matmul(ps_o[g][:, 0:129],
                                        P_buf[:, po + g * 128:po + (g + 1) * 128],
                                        V_aug[:, h, j, 0:129],
                                        start=(j == 0), stop=(j == nj - 1))
                    AVD.inc(last)

                def pe(t, tau=tau, nj=nj, gc_tau=gc_tau, dg_tau=dg_tau):
                    if tau == 0:
                        KTE.wait(t)
                        VEV.wait(t)
                    items = []
                    dcnt = dg_tau
                    for h in range(4):
                        for j in range(nj):
                            c = gc_tau + h * nj + j
                            if j >= nj - 4:
                                dcnt += 1
                            items.append((h, j, c, dcnt, 4 * tau + h))
                    for k, (h, j, c, dc_, s_i) in enumerate(items):
                        emit_qk(t, tau, h, j, c)
                        if k >= LA:
                            h2_, j2_, c2_, d2_, s2_ = items[k - LA]
                            emit_av(t, tau, h2_, j2_, c2_, nj, d2_, s2_)
                    for k in range(len(items) - LA, len(items)):
                        h2_, j2_, c2_, d2_, s2_ = items[k]
                        emit_av(t, tau, h2_, j2_, c2_, nj, d2_, s2_)
                blk.tensor(pe)

                def act(s, tau=tau, nj=nj, gc_tau=gc_tau, prev=pend_rms[0]):
                    for idx, i in enumerate(range(gc_tau // 2,
                                                  (gc_tau + 4 * nj) // 2)):
                        QKD.wait(s, qkd0 + 2 * i + 2)
                        AVD.wait(s, avd0 + max(0, 2 * i - 6))
                        po = (2 * i % 8) * 512
                        so = (2 * i % 4) * 512
                        EXPD.inc(s.activation(P_buf[:, po:po + 1024],
                                              ps_s[:, so:so + 1024], AF.Exp))
                        if idx == 3 and prev is not None:
                            prev(s)
                blk.scalar(act)

                def mk_rms(tau):
                    p = tau % 2

                    def rms(s, tau=tau, p=p):
                        YST.wait(s, yst0 + tau + 1)
                        i = s.activation(st1[:, 2 * p + 1:2 * p + 2],
                                         st1[:, 2 * p:2 * p + 1], AF.Ln)
                        asy(s, i)
                        YRS.inc(s.activation(st2[:, p:p + 1],
                                             st1[:, 2 * p + 1:2 * p + 2],
                                             AF.Exp, scale=-0.5))
                    return rms
                pend_rms[0] = mk_rms(tau)

                def dve(v, tau=tau, nj=nj, gc_tau=gc_tau, dg_tau=dg_tau,
                        prev=pend_ynd[0]):
                    c = gc_tau
                    dcnt = dg_tau
                    for h in range(4):
                        for j in range(nj):
                            if j >= nj - 4:
                                EXPD.wait(v, expd0 + c // 2 + 1)
                                dcnt += 1
                                po = (c % 8) * 512
                                MSKD.inc(v.tensor_tensor(
                                    P_buf[:, po:po + 512], P_buf[:, po:po + 512],
                                    masks[:, j - (nj - 4), :], ALU.mult))
                            c += 1
                        AVD.wait(v, avd0 + c)
                        i = None
                        for g in range(4):
                            i = v.reciprocal(linv[:, g:g + 1], ps_o[g][:, 128:129])
                        vs(v, i)
                        last = None
                        for g in range(4):
                            hh = 4 * h + g
                            last = v.tensor_scalar(
                                y_tm[tau % 2][:, hh * 128:(hh + 1) * 128],
                                ps_o[g][:, 0:128], linv[:, g:g + 1], None, ALU.mult)
                        OEV.inc(last)
                        if h == 0 and prev is not None:
                            prev(v)
                    p = tau % 2
                    v.wait_ge(OEV.h, OEV.n)
                    for g4 in range(4):
                        i = v.bn_stats(bn6[:, g4, :],
                                       y_tm[tau % 2][:, g4 * 512:(g4 + 1) * 512])
                    vs(v, i)
                    means = bn6[:, 0:4, 1::3]
                    ctvs = bn6[:, 0:4, 2::3]
                    st8v = st8[:, 0:8].rearrange("p (g t) -> p g t", g=4)
                    v.tensor_reduce(stL[:, 1:2], ctvs, AX.XY, ALU.add)
                    i = v.tensor_tensor(st8v, means, means, ALU.mult)
                    vs(v, i)
                    i = v.tensor_reduce(stL[:, 2:3], st8v, AX.XY, ALU.add)
                    vs(v, i)
                    i = v.tensor_scalar(stL[:, 3:4], stL[:, 1:2],
                                        1.0 / 2048, RMS_EPS, ALU.mult, ALU.add)
                    vs(v, i)
                    YST.inc(v.scalar_tensor_tensor(st1[:, 2 * p:2 * p + 1],
                                                   stL[:, 2:3],
                                                   0.125, stL[:, 3:4],
                                                   ALU.mult, ALU.add))
                blk.vector(dve)

                def mk_ynd(tau):
                    p = tau % 2

                    def ynd(v, tau=tau, p=p):
                        YRS.wait(v, yrs0 + tau + 1)
                        YND.inc(v.tensor_scalar(y_nB[:, tau, :],
                                                y_tm[tau % 2][:, :],
                                                st2[:, p:p + 1], None, ALU.mult))
                    return ynd
                pend_ynd[0] = mk_ynd(tau)

                if ph2_quant:
                    for k2 in range(2):
                        emit_quant_tile(24 + tau * 2 + k2)

            if pend_rms[0] is not None:
                blk.scalar(pend_rms[0])
            if pend_ynd[0] is not None:
                blk.vector(pend_ynd[0])
            barrier()
        if debug:
            def dbg2(g):
                OST.inc(g.dma_start(out=DBG_Y[...], in_=y_nB[...]), 16)
            blk.gpsimd(dbg2)
            barrier()
        if stop_after == 4 or ph2_taus < NT_OWN:
            sA.close()
            sB.close()
            return nc

        sA.close()

        sC = ExitStack()
        h2T = sC.enter_context(nc.sbuf_tensor("h2T", [128, 16, OWN], BF16))

        with ExitStack() as s3:
            sb1 = lambda n, s, d: s3.enter_context(nc.sbuf_tensor(n, s, d))
            ps1 = lambda n, s, d: s3.enter_context(nc.psum_tensor(n, s, d))
            wpT = sb1("wpT", [128, 16, D], BF16)
            x_f = [sb1(f"x3{i}", [128, D], F32) for i in range(2)]
            x2 = [sb1(f"x2{i}", [128, D], F32) for i in range(2)]
            h2 = [sb1(f"h2b{i}", [128, D], BF16) for i in range(2)]
            yT = [sb1(f"yT{i}", [128, D], BF16) for i in range(2)]
            ps_p = [ps1(f"psp{i}", [128, 512], F32) for i in range(4)]
            ps_tr = [ps1(f"pstr3{i}", [128, 1024], BF16) for i in range(2)]
            ps_try = ps1("ps_try", [128, D], BF16)

            def syw(sy_):
                QST.wait(sy_)
                for dc in range(16):
                    WTL.inc(sy_.dma_start(out=wpT[:, dc, :],
                                          in_=WP_EFF[0:D, dc * 128:(dc + 1) * 128],
                                          transpose=True), 16)
            blk.sync(syw)
            wpT_t = WTL.n

            xl0, mmd0, pje0, h2d0 = XL.n, MMD.n, PJE.n, H1D.n
            trp0, tre0, ost0 = TRP.n, TRE.n, OST.n
            ytp0, yte0 = YTP.n, YTE.n

            def emit_ytr3(t, tt):
                if tt >= 1:
                    YTE.wait(t, yte0 + tt)
                last = None
                for dc in range(16):
                    last = t.transpose(ps_try[:, dc * 128:(dc + 1) * 128],
                                       y_nB[:, tt, dc * 128:(dc + 1) * 128],
                                       ident[:, :])
                YTP.inc(last)

            def pe_ytr0(t):
                emit_ytr3(t, 0)
            blk.tensor(pe_ytr0)

            for tt in range(NT_OWN):
                bi = tt % 2

                def sy(sy_, tt=tt, bi=bi):
                    if tt >= 2:
                        PJE.wait(sy_, pje0 + 4 * (tt - 1))
                    XL.inc(sy_.dma_start(out=x_f[bi][:, :],
                                         in_=X_OWN[tt * 128:(tt + 1) * 128, :]), 16)
                blk.sync(sy)

                def pe(t, tt=tt, bi=bi):
                    if tt + 1 < NT_OWN:
                        emit_ytr3(t, tt + 1)
                    if tt == 0:
                        WTL.wait(t, wpT_t)
                    if tt >= 1:
                        PJE.wait(t, pje0 + 4 * tt)
                    YTE.wait(t, yte0 + tt + 1)
                    last = None
                    for dc in range(16):
                        for ob in range(4):
                            last = t.matmul(ps_p[ob][:, :],
                                            yT[bi][:, dc * 128:(dc + 1) * 128],
                                            wpT[:, dc, ob * 512:(ob + 1) * 512],
                                            start=(dc == 0), stop=(dc == 15))
                    MMD.inc(last)
                blk.tensor(pe)

                def act_yev(s, tt=tt, bi=bi):
                    YTP.wait(s, ytp0 + tt + 1)
                    if tt >= 2:
                        MMD.wait(s, mmd0 + tt - 1)
                    YTE.inc(s.activation(yT[bi][:, :], ps_try[:, :], AF.Identity))
                blk.scalar(act_yev)

                def dve1(v, tt=tt, bi=bi):
                    MMD.wait(v, mmd0 + tt + 1)
                    XL.wait(v, xl0 + 16 * (tt + 1))
                    if tt >= 2:
                        OST.wait(v, ost0 + 16 * (tt - 1))
                    for ob in range(4):
                        PJE.inc(v.tensor_tensor(x2[bi][:, ob * 512:(ob + 1) * 512],
                                                ps_p[ob][:, :],
                                                x_f[bi][:, ob * 512:(ob + 1) * 512],
                                                ALU.add))
                blk.vector(dve1)

                def h2guard(s, tt=tt):
                    if tt >= 2:
                        TRP.wait(s, trp0 + 4 * (tt - 1))

                def x2w(v):
                    v.wait_ge(PJE.h, PJE.n)
                emit_ln(h2d0 + tt, tt % 4, x2[bi], h2[bi], extra_dve_wait=x2w,
                        extra_act_wait=h2guard)

                def pst(g, tt=tt, bi=bi):
                    H1D.wait(g, h2d0 + tt + 1)
                    OST.inc(g.dma_start(out=OUT[tt * 128:(tt + 1) * 128, :],
                                        in_=x2[bi][:, :]), 16)
                blk.gpsimd(pst)

                emit_tr16(h2[bi],
                          lambda grp, tt=tt: h2T[:, grp * 4:(grp + 1) * 4,
                                                 tt * 128:(tt + 1) * 128],
                          ps_tr, h2d0 + tt, trp0, tre0, 4 * tt)
            barrier()
        if debug:
            def dbg3(g):
                OST.inc(g.dma_start(out=DBG_H2[...], in_=h2T[...]), 16)
            blk.gpsimd(dbg3)
            barrier()
        if stop_after == 5:
            sB.close()
            sC.close()
            return nc

        sB.close()

        with ExitStack() as s4:
            sb0 = lambda n, s, d: s4.enter_context(nc.sbuf_tensor(n, s, d))
            hdn = sb0("hdn", [128, NIC, 1024], BF16)
            dw = [sb0(f"dw{i}", [128, NIC, 128], BF16) for i in range(3)]
            ddb = [sb0(f"ddb{i}", [128, 1024], BF16) for i in range(2)]
            ddf = sb0("ddf", [128, 2, 1024], F32)

            hdn0 = HDN.n
            dwl0, mmdd0 = DWL.n, MMD.n
            dw_t = {}

            DWPRE = {28: (0, 0), 30: (0, 1), 32: (0, 2), 34: (0, 3),
                     36: (1, 0), 38: (1, 1), 40: (1, 2), 42: (1, 3)}

            def dw_chunk(g, oc, c):
                i0, i1 = c * 11, (c + 1) * 11
                DWL.inc(g.dma_start(
                    out=dw[oc % 3][:, i0:i1, :],
                    in_=DWT[oc].rearrange("p (c f) -> p c f", c=NIC)[:, i0:i1, :]),
                    16)
                if c == 3:
                    dw_t[oc] = DWL.n

            with ExitStack() as s4a:
                sb1 = lambda n, s, d: s4a.enter_context(nc.sbuf_tensor(n, s, d))
                ps1 = lambda n, s, d: s4a.enter_context(nc.psum_tensor(n, s, d))
                gw = [sb1(f"gw{i}", [128, 16, 128], BF16) for i in range(2)]
                uw = [sb1(f"uw{i}", [128, 16, 128], BF16) for i in range(2)]
                sgb = [sb1(f"sgb{i}", [128, 1024], BF16) for i in range(2)]
                ubb = [sb1(f"ubb{i}", [128, 1024], BF16) for i in range(2)]
                pool = [ps1(f"pp{i}", [128, 512], F32) for i in range(6)]

                fwl0, fwu0 = FWL.n, FWU.n
                mmg0, gue0, ube0 = MMG.n, GUE.n, UBE.n
                bank_owner = [None] * 6

                for ic in range(NIC):
                    def pload(g, ic=ic):
                        if ic >= 2:
                            MMG.wait(g, mmg0 + 4 * (ic - 2) + 2)
                        FWL.inc(g.dma_start(
                            out=gw[ic % 2][...],
                            in_=GWT[ic].rearrange("p (c f) -> p c f", c=16)), 16)
                        if ic >= 2:
                            MMG.wait(g, mmg0 + 4 * (ic - 2) + 4)
                        FWU.inc(g.dma_start(
                            out=uw[ic % 2][...],
                            in_=UWT[ic].rearrange("p (c f) -> p c f", c=16)), 16)
                        if ic in DWPRE:
                            dw_chunk(g, *DWPRE[ic])
                    blk.gpsimd(pload)

                    owner_snap = [bank_owner[(4 * ic + qi) % 6] for qi in range(4)]

                    def pe(t, ic=ic, owner_snap=owner_snap):
                        FWL.wait(t, fwl0 + 16 * (ic + 1))
                        for qi in range(4):
                            if qi == 2:
                                FWU.wait(t, fwu0 + 16 * (ic + 1))
                            b = (4 * ic + qi) % 6
                            own = owner_snap[qi]
                            if own is not None:
                                own[0].wait(t, own[1])
                            w = (gw if qi < 2 else uw)[ic % 2]
                            half = qi % 2
                            last = None
                            for dc in range(16):
                                last = t.matmul(
                                    pool[b][:, :], w[:, dc, :],
                                    h2T[:, dc, half * 512:(half + 1) * 512],
                                    start=(dc == 0), stop=(dc == 15))
                            MMG.inc(last)
                    blk.tensor(pe)

                    def act(s, ic=ic):
                        if ic >= 2:
                            HDN.wait(s, hdn0 + ic - 1)
                        for half in range(2):
                            MMG.wait(s, mmg0 + 4 * ic + 1 + half)
                            GUE.inc(s.activation(
                                sgb[ic % 2][:, half * 512:(half + 1) * 512],
                                pool[(4 * ic + half) % 6][:, :], AF.Silu,
                                scale=gs2[:, ic:ic + 1]))
                    blk.scalar(act)
                    bank_owner[(4 * ic) % 6] = (GUE, gue0 + 2 * ic + 1)
                    bank_owner[(4 * ic + 1) % 6] = (GUE, gue0 + 2 * ic + 2)

                    def dve(v, ic=ic):
                        for half in range(2):
                            MMG.wait(v, mmg0 + 4 * ic + 3 + half)
                            UBE.inc(v.tensor_scalar(
                                ubb[ic % 2][:, half * 512:(half + 1) * 512],
                                pool[(4 * ic + 2 + half) % 6][:, :],
                                us2[:, ic:ic + 1], None, ALU.mult))
                        UBE.wait(v, ube0 + 2 * ic + 2)
                        GUE.wait(v, gue0 + 2 * ic + 2)
                        HDN.inc(v.tensor_tensor(hdn[:, ic, :], sgb[ic % 2][:, :],
                                                ubb[ic % 2][:, :], ALU.mult))
                    blk.vector(dve)
                    bank_owner[(4 * ic + 2) % 6] = (UBE, ube0 + 2 * ic + 1)
                    bank_owner[(4 * ic + 3) % 6] = (UBE, ube0 + 2 * ic + 2)

            with ExitStack() as s4d:
                ps1 = lambda n, s, d: s4d.enter_context(nc.psum_tensor(n, s, d))
                ps_d = [ps1(f"psd{i}", [128, 1024], F32) for i in range(2)]
                ps_tro = [ps1(f"pstro{i}", [128, 1024], BF16) for i in range(2)]

                dse0, dte0, ktr0 = DSE.n, DTE.n, KTR.n
                oacc0 = OACC.n

                def dwload(oc):
                    def gld(g, oc=oc):
                        if oc >= 3:
                            MMD.wait(g, mmdd0 + oc - 2)
                        DWL.inc(g.dma_start(
                            out=dw[oc % 3][...],
                            in_=DWT[oc].rearrange("p (c f) -> p c f", c=NIC)), 16)
                        dw_t[oc] = DWL.n
                    blk.gpsimd(gld)

                def mmdn(oc):
                    def pe(t, oc=oc):
                        DWL.wait(t, dw_t[oc])
                        if oc == 0:
                            HDN.wait(t, hdn0 + NIC)
                        if oc >= 2:
                            DSE.wait(t, dse0 + oc - 1)
                        last = None
                        for icc in range(NIC):
                            for half in range(2):
                                last = t.matmul(
                                    ps_d[oc % 2][:, half * 512:(half + 1) * 512],
                                    dw[oc % 3][:, icc, :],
                                    hdn[:, icc, half * 512:(half + 1) * 512],
                                    start=(icc == 0), stop=(icc == NIC - 1))
                        MMD.inc(last)
                    blk.tensor(pe)

                def ddbev(oc):
                    def dve1(v, oc=oc):
                        MMD.wait(v, mmdd0 + oc + 1)
                        if oc >= 2:
                            KTR.wait(v, ktr0 + oc - 1)
                        DSE.inc(v.tensor_scalar(ddb[oc % 2][:, :],
                                                ps_d[oc % 2][:, :],
                                                ds2[:, oc:oc + 1], None, ALU.mult))
                    blk.vector(dve1)

                OUT_T = OUT.rearrange("(t p) c -> p t c", p=128)

                def trout(oc):
                    def pe2(t, oc=oc):
                        DSE.wait(t, dse0 + oc + 1)
                        if oc >= 2:
                            DTE.wait(t, dte0 + oc - 1)
                        last = None
                        for tt in range(NT_OWN):
                            last = t.transpose(
                                ps_tro[oc % 2][:, tt * 128:(tt + 1) * 128],
                                ddb[oc % 2][:, tt * 128:(tt + 1) * 128],
                                ident[:, :])
                        KTR.inc(last)
                    blk.tensor(pe2)

                    def dve2(v, oc=oc):
                        KTR.wait(v, ktr0 + oc + 1)
                        if oc >= 2:
                            OACC.wait(v, oacc0 + 16 * (oc - 1))
                        DTE.inc(v.tensor_copy(ddf[:, oc % 2, :],
                                              ps_tro[oc % 2][:, :]))
                    blk.vector(dve2)

                def pacc_emit(oc):
                    def pacc(g, oc=oc):
                        DTE.wait(g, dte0 + oc + 1)
                        OST.wait(g, 16 * 8)
                        OACC.inc(g.dma_start(
                            out=OUT_T[:, :, oc * 128:(oc + 1) * 128],
                            in_=ddf[:, oc % 2, :].rearrange("p (t c) -> p t c",
                                                            t=NT_OWN),
                            accum_op=ALU.add), 16)
                    blk.gpsimd(pacc)

                mmdn(0)
                mmdn(1)
                for oc in range(16):
                    if oc + 2 < 16:
                        dwload(oc + 2)
                    ddbev(oc)
                    trout(oc)
                    if oc + 2 < 16:
                        mmdn(oc + 2)
                    if oc >= 1:
                        pacc_emit(oc - 1)
                pacc_emit(15)

            def fin(g):
                OACC.wait(g)
                OST.wait(g)
            blk.gpsimd(fin)

        sC.close()

    return nc


def _host_prepare(L0_attn_x, L0_attn_w_qkv, L0_attn_w_proj, L0_attn_q_gain,
                  L0_ffn_gate_w, L0_ffn_up_w, L0_ffn_down_w,
                  L0_ffn_gate_s, L0_ffn_up_s, L0_ffn_down_s):
    f32 = np.float32
    x = np.asarray(L0_attn_x, f32)
    wq = np.ascontiguousarray(np.asarray(L0_attn_w_qkv, f32))
    wp = np.ascontiguousarray(np.asarray(L0_attn_w_proj, f32))
    qg = np.asarray(L0_attn_q_gain, f32)
    gwm = np.asarray(L0_ffn_gate_w, f32)
    uwm = np.asarray(L0_ffn_up_w, f32)
    dwm = np.asarray(L0_ffn_down_w, f32)
    gs = np.asarray(L0_ffn_gate_s, f32)
    us = np.asarray(L0_ffn_up_s, f32)
    ds = np.asarray(L0_ffn_down_s, f32)

    gwt = np.ascontiguousarray(
        gwm.reshape(NIC, 128, 16, 128).transpose(0, 3, 2, 1)).reshape(NIC, 128, 2048)
    uwt = np.ascontiguousarray(
        uwm.reshape(NIC, 128, 16, 128).transpose(0, 3, 2, 1)).reshape(NIC, 128, 2048)
    dwt = np.ascontiguousarray(
        dwm.reshape(16, 128, NIC, 128).transpose(0, 3, 2, 1)).reshape(16, 128,
                                                                      NIC * 128)
    gs2 = np.ascontiguousarray(gs.reshape(NIC, 128).T)
    us2 = np.ascontiguousarray(us.reshape(NIC, 128).T)
    ds2 = np.ascontiguousarray(ds.reshape(16, 128).T)
    qg2 = np.ascontiguousarray(np.tile(qg.reshape(1, NH), (128, 1)))

    inv_freq = (1.0 / (10000.0 ** (np.arange(0, HD, 2, dtype=f32) / HD))).astype(f32)
    t_all = np.arange(SEQ, dtype=f32)
    freqs = np.outer(t_all, inv_freq).astype(f32)
    cos_all = np.cos(freqs).astype(BF)
    sin_all = np.sin(freqs).astype(BF)
    cosk = np.ascontiguousarray(cos_all.reshape(NT_ALL, 128, 64).transpose(1, 0, 2))
    sink = np.ascontiguousarray(sin_all.reshape(NT_ALL, 128, 64).transpose(1, 0, 2))
    ident = np.eye(128, dtype=BF)

    per_core = []
    for core in range(8):
        b, lane = core // 4, core % 4
        x_all = np.ascontiguousarray(x[b])
        x_own = np.ascontiguousarray(x[b, lane::4])
        cos_own = np.ascontiguousarray(
            cos_all[lane::4].reshape(NT_OWN, 128, 64).transpose(1, 0, 2))
        sin_own = np.ascontiguousarray(
            sin_all[lane::4].reshape(NT_OWN, 128, 64).transpose(1, 0, 2))
        kk = np.arange(128)[:, None]
        qq = np.arange(128)[None, :]
        m = np.zeros((128, 4, 512), dtype=BF)
        for i in range(4):
            mi = ((128 * i + kk) <= (4 * qq + lane)).astype(BF)
            m[:, i, :] = np.tile(mi, (1, 4))
        per_core.append(dict(
            x_all=x_all, x_own=x_own, wq=wq, wp=wp,
            gwt=gwt, uwt=uwt, dwt=dwt, gs2=gs2, us2=us2, ds2=ds2, qg2=qg2,
            cosk=cosk, sink=sink, cosq=cos_own, sinq=sin_own,
            msk=np.ascontiguousarray(m), idn=ident,
        ))
    return per_core


def kernel(L0_attn_x, L0_attn_w_qkv, L0_attn_w_proj, L0_attn_q_gain,
           L0_ffn_gate_w, L0_ffn_up_w, L0_ffn_down_w,
           L0_ffn_gate_s, L0_ffn_up_s, L0_ffn_down_s):
    in_maps = _host_prepare(L0_attn_x, L0_attn_w_qkv, L0_attn_w_proj,
                            L0_attn_q_gain, L0_ffn_gate_w, L0_ffn_up_w,
                            L0_ffn_down_w, L0_ffn_gate_s, L0_ffn_up_s,
                            L0_ffn_down_s)
    nc = build_program()
    res = run_bass_kernel_spmd(nc, in_maps, core_ids=list(range(8)))
    out = np.empty((B, SEQ, D), np.float32)
    for core in range(8):
        b, lane = core // 4, core % 4
        out[b, lane::4] = res.results[core]["out_own"]
    return out


# version 21
# speedup vs baseline: 1.0131x; 1.0131x over previous
import numpy as np
import ml_dtypes
from contextlib import ExitStack

import concourse.bass as bass
import concourse.mybir as mybir
from concourse.bass_utils import run_bass_kernel_spmd

F32 = mybir.dt.float32
BF16 = mybir.dt.bfloat16
AF = mybir.ActivationFunctionType
ALU = mybir.AluOpType
AX = mybir.AxisListType
BF = ml_dtypes.bfloat16

B, SEQ, D = 2, 4096, 2048
NH, NKV, HD = 16, 4, 128
OQKV = 3072
INTER = 5632
NIC = INTER // 128
OWN = SEQ // 4
NT_ALL = SEQ // 128
NT_OWN = OWN // 128
RMS_EPS = 1.1920929e-07
LN_EPS = 1e-5
THR_C = 0.5009765625
INV_SQRT_HD = 0.08838834764831843


class Sem:
    def __init__(self, stack, nc, name):
        self.h = stack.enter_context(nc.semaphore(name))
        self.n = 0

    def inc(self, instr, k=1):
        instr.then_inc(self.h, k)
        self.n += k
        return instr

    def wait(self, eng, n=None):
        v = self.n if n is None else n
        if v > 0:
            eng.wait_ge(self.h, v)


def build_program(stop_after=None, ph2_taus=NT_OWN, ph2_quant=True, ph2_ytr=True, ph2_tail=True, debug=False):
    nc = bass.Bass("TRN2", target_bir_lowering=False)

    X_ALL = nc.dram_tensor("x_all", [SEQ, D], F32, kind="ExternalInput").ap()
    X_OWN = nc.dram_tensor("x_own", [OWN, D], F32, kind="ExternalInput").ap()
    WQ = nc.dram_tensor("wq", [OQKV, D], F32, kind="ExternalInput").ap()
    WP = nc.dram_tensor("wp", [D, D], F32, kind="ExternalInput").ap()
    GWT = nc.dram_tensor("gwt", [NIC, 128, 2048], F32, kind="ExternalInput").ap()
    UWT = nc.dram_tensor("uwt", [NIC, 128, 2048], F32, kind="ExternalInput").ap()
    DWT = nc.dram_tensor("dwt", [16, 128, NIC * 128], F32, kind="ExternalInput").ap()
    GS2 = nc.dram_tensor("gs2", [128, NIC], F32, kind="ExternalInput").ap()
    US2 = nc.dram_tensor("us2", [128, NIC], F32, kind="ExternalInput").ap()
    DS2 = nc.dram_tensor("ds2", [128, 16], F32, kind="ExternalInput").ap()
    QG2 = nc.dram_tensor("qg2", [128, 16], F32, kind="ExternalInput").ap()
    COSK = nc.dram_tensor("cosk", [128, NT_ALL, 64], BF16, kind="ExternalInput").ap()
    SINK = nc.dram_tensor("sink", [128, NT_ALL, 64], BF16, kind="ExternalInput").ap()
    COSQ = nc.dram_tensor("cosq", [128, NT_OWN, 64], BF16, kind="ExternalInput").ap()
    SINQ = nc.dram_tensor("sinq", [128, NT_OWN, 64], BF16, kind="ExternalInput").ap()
    MSK = nc.dram_tensor("msk", [128, 4, 512], BF16, kind="ExternalInput").ap()
    IDN = nc.dram_tensor("idn", [128, 128], BF16, kind="ExternalInput").ap()

    OUT = nc.dram_tensor("out_own", [OWN, D], F32, kind="ExternalOutput").ap()
    if debug:
        DBG_W = nc.dram_tensor("dbg_w", [128, 16, 1024], BF16, kind="ExternalOutput").ap()
        DBG_H = nc.dram_tensor("dbg_h", [128, NT_OWN, 16, 128], BF16, kind="ExternalOutput").ap()
        DBG_K = nc.dram_tensor("dbg_k", [128, NKV, SEQ], BF16, kind="ExternalOutput").ap()
        DBG_V = nc.dram_tensor("dbg_v", [128, NKV, NT_ALL, 132], BF16, kind="ExternalOutput").ap()
        DBG_Q = nc.dram_tensor("dbg_q", [128, NKV * NT_OWN * 4 * 128], BF16, kind="ExternalOutput").ap()
        DBG_R = nc.dram_tensor("dbg_r", [128, NT_ALL * 4], F32, kind="ExternalOutput").ap()
        DBG_Y = nc.dram_tensor("dbg_y", [128, NT_OWN, D], BF16, kind="ExternalOutput").ap()
        DBG_H2 = nc.dram_tensor("dbg_h2", [128, 16, OWN], BF16, kind="ExternalOutput").ap()

    WQ_EFF = nc.dram_tensor("wq_eff", [OQKV, D], BF16, kind="Internal").ap()
    WP_EFF = nc.dram_tensor("wp_eff", [D, D], BF16, kind="Internal").ap()

    st = ExitStack()
    with st:
        sb = lambda n, s, d: st.enter_context(nc.sbuf_tensor(n, s, d))
        ident = sb("ident", [128, 128], BF16)
        masks = sb("masks", [128, 4, 512], BF16)
        cosk = sb("cosk_t", [128, NT_ALL, 64], BF16)
        sink = sb("sink_t", [128, NT_ALL, 64], BF16)
        cosq = sb("cosq_t", [128, NT_OWN, 64], BF16)
        sinq = sb("sinq_t", [128, NT_OWN, 64], BF16)
        gs2 = sb("gs2_t", [128, NIC], F32)
        us2 = sb("us2_t", [128, NIC], F32)
        ds2 = sb("ds2_t", [128, 16], F32)
        qg2 = sb("qg2_t", [128, 16], F32)
        qgs = sb("qgs_t", [128, 16], F32)
        fence = sb("fence", [1, 16], F32)
        bn6 = sb("bn6", [128, 24, 6], F32)
        bn2 = sb("bn2", [128, 16, 2], F32)
        stV = sb("stV", [128, 4], F32)
        stLN = sb("stLN", [128, 4], F32)
        stR = sb("stR", [128, 4], F32)
        stMR = sb("stMR", [128, 4], F32)
        st1 = sb("st1", [128, 16], F32)
        st2 = sb("st2", [128, 16], F32)
        st3 = sb("st3", [128, 16], F32)
        st4 = sb("st4", [128, 512], BF16)
        st5 = sb("st5", [128, 16], F32)
        st6 = sb("st6", [128, 16], F32)
        st7 = sb("st7", [128, 16], F32)
        stb = sb("stb", [128, 16], BF16)
        st8 = sb("st8", [128, 16], F32)
        stL = sb("stL", [128, 8], F32)
        stK = sb("stK", [128, 16], F32)
        rstdk = sb("rstdk", [128, NT_ALL * 4], F32)

        sems = {}

        def mksem(name):
            sems[name] = Sem(st, nc, name)
            return sems[name]

        CONST = mksem("CONST")
        XL = mksem("XL")
        WTL = mksem("WTL")
        QBL = mksem("QBL")
        QST = mksem("QST")
        FWL = mksem("FWL")
        OST = mksem("OST")
        OACC = mksem("OACC")
        BAR = mksem("BAR")
        LNS = mksem("LNS")
        LNR = mksem("LNR")
        MRD = mksem("MRD")
        H1D = mksem("H1D")
        TRP = mksem("TRP")
        TRE = mksem("TRE")
        MMD = mksem("MMD")
        KEV = mksem("KEV")
        KBN = mksem("KBN")
        KRS = mksem("KRS")
        KRO = mksem("KRO")
        VEV = mksem("VEV")
        KTR = mksem("KTR")
        KTE = mksem("KTE")
        QED = mksem("QED")
        QTH = mksem("QTH")
        QNG = mksem("QNG")
        QPS = mksem("QPS")
        QWE = mksem("QWE")
        QKD = mksem("QKD")
        EXPD = mksem("EXPD")
        MSKD = mksem("MSKD")
        AVD = mksem("AVD")
        OEV = mksem("OEV")
        YST = mksem("YST")
        YRS = mksem("YRS")
        YND = mksem("YND")
        PJE = mksem("PJE")
        H2D = mksem("H2D")
        GUE = mksem("GUE")
        HDN = mksem("HDN")
        DSE = mksem("DSE")
        DTE = mksem("DTE")
        MMG = mksem("MMG")
        UBE = mksem("UBE")
        FWU = mksem("FWU")
        DWL = mksem("DWL")
        VSY = mksem("VSY")
        ASY = mksem("ASY")
        QAB = mksem("QAB")
        KSC = mksem("KSC")
        YTP = mksem("YTP")
        YTE = mksem("YTE")

        blk = st.enter_context(nc.Block())

        fence_ops = {
            "vector": lambda v: v.tensor_copy(fence[0:1, 0:1], fence[0:1, 1:2]),
            "scalar": lambda s: s.copy(fence[0:1, 2:3], fence[0:1, 3:4]),
            "gpsimd": lambda g: g.tensor_copy(fence[0:1, 4:5], fence[0:1, 5:6]),
            "tensor": lambda t: t.nop(),
        }

        def vs(v, instr):
            VSY.inc(instr)
            v.wait_ge(VSY.h, VSY.n)

        def asy(s, instr):
            ASY.inc(instr)
            s.wait_ge(ASY.h, ASY.n)

        def barrier():
            for e in ("vector", "scalar", "gpsimd", "tensor"):
                def emit_f(eng, e=e):
                    BAR.inc(fence_ops[e](eng))
                getattr(blk, e)(emit_f)
            tgt = BAR.n
            dma_targets = [(s, s.n) for s in (CONST, XL, WTL, QBL, QST, FWL,
                                              FWU, DWL, OST, OACC)]

            def waits(eng):
                eng.wait_ge(BAR.h, tgt)
                for s_, v in dma_targets:
                    if v > 0:
                        eng.wait_ge(s_.h, v)
            for e in ("sync", "gpsimd", "vector", "scalar", "tensor"):
                getattr(blk, e)(waits)

        LNG = [0, 8, 12, 16]

        def make_ln(idx, p, x_t, h_t, extra_dve_wait=None, extra_act_wait=None):
            g0 = LNG[p]

            def dve1(v, idx=idx, p=p, g0=g0):
                if extra_dve_wait is not None:
                    extra_dve_wait(v)
                for g4 in range(4):
                    i = v.bn_stats(bn6[:, g0 + g4, :],
                                   x_t[:, g4 * 512:(g4 + 1) * 512])
                vs(v, i)
                means = bn6[:, g0:g0 + 4, 1::3]
                ctvs = bn6[:, g0:g0 + 4, 2::3]
                st8v = st8[:, 0:8].rearrange("p (g t) -> p g t", g=4)
                v.tensor_reduce(stL[:, 0:1], means, AX.XY, ALU.add)
                v.tensor_reduce(stL[:, 1:2], ctvs, AX.XY, ALU.add)
                i = v.tensor_tensor(st8v, means, means, ALU.mult)
                vs(v, i)
                v.tensor_reduce(stL[:, 2:3], st8v, AX.XY, ALU.add)
                v.tensor_scalar(bn2[:, p, 0:1], stL[:, 0:1], 0.125, None,
                                ALU.mult)
                v.scalar_tensor_tensor(stL[:, 3:4], stL[:, 0:1],
                                       -0.125 * 0.125, stL[:, 0:1],
                                       ALU.mult, ALU.mult)
                i = v.tensor_scalar(stL[:, 4:5], stL[:, 1:2], 1.0 / 2048,
                                    LN_EPS, ALU.mult, ALU.add)
                vs(v, i)
                i = v.scalar_tensor_tensor(stL[:, 5:6], stL[:, 2:3], 0.125,
                                           stL[:, 3:4], ALU.mult, ALU.add)
                vs(v, i)
                LNS.inc(v.scalar_tensor_tensor(stV[:, p:p + 1], stL[:, 5:6],
                                               1.0, stL[:, 4:5],
                                               ALU.mult, ALU.add))

            def act1(s, idx=idx, p=p):
                LNS.wait(s, idx + 1)
                i = s.activation(stLN[:, p:p + 1], stV[:, p:p + 1], AF.Ln)
                asy(s, i)
                LNR.inc(s.activation(stR[:, p:p + 1], stLN[:, p:p + 1], AF.Exp,
                                     scale=-0.5))

            def dve2(v, idx=idx, p=p):
                LNR.wait(v, idx + 1)
                MRD.inc(v.scalar_tensor_tensor(stMR[:, p:p + 1], bn2[:, p, 0:1],
                                               -1.0, stR[:, p:p + 1],
                                               ALU.mult, ALU.mult))

            def act2(s, idx=idx, p=p):
                MRD.wait(s, idx + 1)
                if extra_act_wait is not None:
                    extra_act_wait(s)
                H1D.inc(s.activation(h_t[:, :], x_t[:, :], AF.Identity,
                                     bias=stMR[:, p:p + 1],
                                     scale=stR[:, p:p + 1]))
            return dve1, act1, dve2, act2

        def emit_ln(idx, p, x_t, h_t, extra_dve_wait=None, extra_act_wait=None):
            d1, a1, d2, a2 = make_ln(idx, p, x_t, h_t, extra_dve_wait,
                                     extra_act_wait)
            blk.vector(d1)
            blk.scalar(a1)
            blk.vector(d2)
            blk.scalar(a2)

        def emit_kstats(v, p2, seg):
            kb = 4 if p2 == 0 else 20
            mp = 8 * p2
            for h4 in range(4):
                i = v.bn_stats(bn6[:, kb + h4, :], seg[:, h4, :])
            vs(v, i)
            means = bn6[:, kb:kb + 4, 1::3]
            ctvs = bn6[:, kb:kb + 4, 2::3]
            st8v = st8[:, mp:mp + 8].rearrange("p (g t) -> p g t", g=4)
            v.tensor_reduce(stK[:, mp:mp + 4], ctvs, AX.X, ALU.add)
            i = v.tensor_tensor(st8v, means, means, ALU.mult)
            vs(v, i)
            i = v.tensor_reduce(stK[:, mp + 4:mp + 8], st8v, AX.X, ALU.add)
            vs(v, i)
            i = v.scalar_tensor_tensor(st8[:, mp:mp + 4], stK[:, mp + 4:mp + 8],
                                       64.0, stK[:, mp:mp + 4],
                                       ALU.mult, ALU.add)
            vs(v, i)
            KBN.inc(v.tensor_scalar(st1[:, 4 + 4 * p2:8 + 4 * p2],
                                    st8[:, mp:mp + 4], 1.0 / 128, RMS_EPS,
                                    ALU.mult, ALU.add))

        def emit_tr16(src_t, dst_fn, ps_tr, h1d_idx, trp_base, tre_base, gbase,
                      pe_extra=None):
            def pe1(t):
                H1D.wait(t, h1d_idx + 1)
                if pe_extra is not None:
                    pe_extra(t)
                for grp in range(4):
                    gi = gbase + grp
                    TRE.wait(t, tre_base + max(0, gi - 1))
                    last = None
                    for k in range(4):
                        dc = grp * 4 + k
                        last = t.transpose(ps_tr[gi % 2][:, k * 128:(k + 1) * 128],
                                           src_t[:, dc * 128:(dc + 1) * 128],
                                           ident[:, :])
                    TRP.inc(last)
            blk.tensor(pe1)

            def dve3(v):
                for grp in range(4):
                    gi = gbase + grp
                    TRP.wait(v, trp_base + gi + 1)
                    TRE.inc(v.tensor_copy(
                        dst_fn(grp),
                        ps_tr[gi % 2][:, 0:512].rearrange("p (k f) -> p k f", k=4)))
            blk.vector(dve3)

        def emit_quant(k, w_bf, weff_dst):
            def qact(s, k=k):
                QBL.wait(s, 16 * (k + 1))
                QWE.wait(s, k)
                s.activation(qbuf['aw'][k % 2][:, :], w_bf[:, :], AF.Abs)
                QAB.inc(s.activation(qbuf['sg'][k % 2][:, :], w_bf[:, :], AF.Sign))
            blk.scalar(qact)

            def qdve(v, k=k):
                QBL.wait(v, 16 * (k + 1))
                i = v.tensor_reduce(st5[:, :],
                                    w_bf[:, :].rearrange("p (g d) -> p g d", g=16),
                                    AX.X, ALU.add, apply_absolute_value=True)
                vs(v, i)
                i = v.tensor_scalar(stb[:, :], st5[:, :], 1.0 / 128.0, None,
                                    ALU.mult)
                vs(v, i)
                i = v.tensor_copy(st6[:, :], stb[:, :])
                vs(v, i)
                i = v.tensor_scalar(st7[:, :], st6[:, :], THR_C, None, ALU.mult)
                vs(v, i)
                i = v.tensor_copy(qbuf['thrf'][:, :, :],
                                  st7[:, :].unsqueeze(2).broadcast_to((128, 16, 128)))
                v.tensor_copy(qbuf['scf'][:, :, :],
                              stb[:, :].unsqueeze(2).broadcast_to((128, 16, 128)))
                vs(v, i)
                QAB.wait(v, k + 1)
                awv = qbuf['aw'][k % 2][:, :].rearrange("p (g d) -> p g d", g=16)
                i = v.tensor_tensor(awv, awv, qbuf['thrf'][:, :, :], ALU.is_gt)
                vs(v, i)
                i = v.tensor_tensor(qbuf['aw'][k % 2][:, :], qbuf['aw'][k % 2][:, :],
                                    qbuf['sg'][k % 2][:, :], ALU.mult)
                vs(v, i)
                QWE.inc(v.tensor_tensor(
                    weff_dst[:, :].rearrange("p (g d) -> p g d", g=16),
                    qbuf['aw'][k % 2][:, :].rearrange("p (g d) -> p g d", g=16),
                    qbuf['scf'][:, :, :], ALU.mult))
            blk.vector(qdve)

        def ph0_sync(sy):
            for dst, src in ((ident, IDN), (masks, MSK), (cosk, COSK),
                             (sink, SINK), (cosq, COSQ), (sinq, SINQ),
                             (gs2, GS2), (us2, US2), (ds2, DS2), (qg2, QG2)):
                CONST.inc(sy.dma_start(out=dst[...], in_=src[...]), 16)
        blk.sync(ph0_sync)

        def ph0_dve(v):
            v.memset(fence[0:1, 0:2], 0.0)
            CONST.wait(v)
            v.tensor_scalar(qgs[:, :], qg2[:, :], INV_SQRT_HD, None, ALU.mult)
        blk.vector(ph0_dve)

        def ph0_act(s):
            s.memzero(fence[0:1, 2:4])
        blk.scalar(ph0_act)

        def ph0_pool(g):
            g.memset(fence[0:1, 4:6], 0.0)
        blk.gpsimd(ph0_pool)
        barrier()
        if stop_after == 0:
            return nc

        sA = ExitStack()
        K_fm = sA.enter_context(nc.sbuf_tensor("K_fm", [128, NKV, SEQ], BF16))
        V_aug = sA.enter_context(nc.sbuf_tensor("V_aug", [128, NKV, NT_ALL, 132], BF16))

        qbuf = {}

        def alloc_quant_scratch(scope, tag):
            qbuf['aw'] = [scope.enter_context(
                nc.sbuf_tensor(f"qaw{tag}", [128, D], BF16))] * 2
            qbuf['sg'] = [scope.enter_context(
                nc.sbuf_tensor(f"qsg{tag}", [128, D], BF16))] * 2
            qbuf['wbf'] = [scope.enter_context(
                nc.sbuf_tensor(f"qwbf{tag}{i}", [128, D], BF16)) for i in range(2)]
            qbuf['thrf'] = scope.enter_context(
                nc.sbuf_tensor(f"qthrf{tag}", [128, 16, 128], F32))
            qbuf['scf'] = scope.enter_context(
                nc.sbuf_tensor(f"qscf{tag}", [128, 16, 128], BF16))

        sQ = ExitStack()
        alloc_quant_scratch(sQ, "a")

        def quant_src_dst(k):
            if k < 8:
                wt = 16 + k
                return (WQ[wt * 128:(wt + 1) * 128, :],
                        WQ_EFF[wt * 128:(wt + 1) * 128, :])
            if k < 24:
                wt = k - 8
                return (WQ[wt * 128:(wt + 1) * 128, :],
                        WQ_EFF[wt * 128:(wt + 1) * 128, :])
            wt = k - 24
            return (WP[wt * 128:(wt + 1) * 128, :],
                    WP_EFF[wt * 128:(wt + 1) * 128, :])

        def emit_quant_tile(k):
            srcap, dstap = quant_src_dst(k)

            def pload(g, k=k, srcap=srcap):
                QST.wait(g, 16 * max(0, k - 1))
                QBL.inc(g.dma_start(out=qbuf['wbf'][k % 2][:, :], in_=srcap), 16)
            blk.gpsimd(pload)
            emit_quant(k, qbuf['wbf'][k % 2], qbuf['wbf'][k % 2])

            def pstore(g, k=k, dstap=dstap):
                QWE.wait(g, k + 1)
                QST.inc(g.dma_start(out=dstap, in_=qbuf['wbf'][k % 2][:, :]), 16)
            blk.gpsimd(pstore)

        for k in range(8):
            emit_quant_tile(k)
        barrier()

        with ExitStack() as s1b:
            sb1 = lambda n, s, d: s1b.enter_context(nc.sbuf_tensor(n, s, d))
            ps1 = lambda n, s, d: s1b.enter_context(nc.psum_tensor(n, s, d))
            x_f = [sb1(f"x{i}", [128, D], F32) for i in range(3)]
            h1 = [sb1(f"h1{i}", [128, D], BF16) for i in range(2)]
            h1T = [sb1(f"h1T{i}", [128, 16, 128], BF16) for i in range(2)]
            wkvT = sb1("wkvT", [128, 16, 1024], BF16)
            kbf = [sb1(f"kbf{i}", [128, 512], BF16) for i in range(2)]
            krope = [sb1(f"krope{i}", [128, 512], BF16) for i in range(2)]
            ps_tr = [ps1(f"pstr{i}", [128, 1024], BF16) for i in range(2)]
            ps_ktr = [ps1(f"psktr{i}", [128, 1024], BF16) for i in range(2)]
            ps_k = [ps1(f"psk{i}", [128, 512], F32) for i in range(2)]
            ps_v = [ps1(f"psv{i}", [128, 512], F32) for i in range(2)]

            def dve_init(v):
                v.memset(V_aug[:, :, :, 128:129], 1.0)
            blk.vector(dve_init)

            def sync_wkvT(sy):
                QST.wait(sy, 16 * 8)
                for dc in range(16):
                    for half in range(2):
                        WTL.inc(sy.dma_start(
                            out=wkvT[:, dc, half * 512:(half + 1) * 512],
                            in_=WQ_EFF[2048 + half * 512:2048 + (half + 1) * 512,
                                       dc * 128:(dc + 1) * 128],
                            transpose=True), 16)
            blk.sync(sync_wkvT)
            wkvT_t = WTL.n

            def st_load(tt):
                def sy(sy_, tt=tt):
                    if tt >= 3:
                        H1D.wait(sy_, tt - 2)
                    XL.inc(sy_.dma_start(out=x_f[tt % 3][:, :],
                                         in_=X_ALL[tt * 128:(tt + 1) * 128, :]), 16)
                blk.sync(sy)

            def st_tr(tt):
                def pe1(t, tt=tt):
                    H1D.wait(t, tt + 1)
                    if tt >= 2:
                        MMD.wait(t, 2 * (tt - 1))
                    TRE.wait(t, 2 * tt)
                    for half in range(2):
                        last = None
                        for kk in range(8):
                            dc = half * 8 + kk
                            last = t.transpose(
                                ps_tr[half][:, kk * 128:(kk + 1) * 128],
                                h1[tt % 2][:, dc * 128:(dc + 1) * 128],
                                ident[:, :])
                        TRP.inc(last)
                blk.tensor(pe1)

            def st_trev(tt):
                def dve3(v, tt=tt):
                    for half in range(2):
                        TRP.wait(v, 2 * tt + half + 1)
                        dst = h1T[tt % 2][:, half * 8:(half + 1) * 8, :]
                        TRE.inc(v.tensor_copy(
                            dst,
                            ps_tr[half][:, :].rearrange("p (k f) -> p k f",
                                                        k=8)))
                blk.vector(dve3)

            def st_kvmm(tt):
                def pe2(t, tt=tt):
                    if tt == 0:
                        WTL.wait(t, wkvT_t)
                    TRE.wait(t, 2 * (tt + 1))
                    KEV.wait(t, max(0, tt - 1))
                    VEV.wait(t, max(0, tt - 1))
                    last_v = None
                    for dc in range(16):
                        t.matmul(ps_k[tt % 2][:, :], h1T[tt % 2][:, dc, :],
                                 wkvT[:, dc, 0:512],
                                 start=(dc == 0), stop=(dc == 15))
                        last_v = t.matmul(ps_v[tt % 2][:, :], h1T[tt % 2][:, dc, :],
                                          wkvT[:, dc, 512:1024],
                                          start=(dc == 0), stop=(dc == 15))
                    MMD.inc(last_v, 2)
                blk.tensor(pe2)

            def st_kevac(tt):
                def act3(s, tt=tt):
                    MMD.wait(s, 2 * (tt + 1))
                    KRO.wait(s, max(0, tt - 1))
                    KEV.inc(s.activation(kbf[tt % 2][:, :], ps_k[tt % 2][:, :],
                                         AF.Copy))
                blk.scalar(act3)

            def st_kstats(tt):
                def dve4(v, tt=tt):
                    KEV.wait(v, tt + 1)
                    KRS.wait(v, max(0, tt - 1))
                    kseg = kbf[tt % 2][:, :].rearrange("p (h d) -> p h d", h=4)
                    emit_kstats(v, tt % 2, kseg)
                    KTR.wait(v, max(0, tt - 1))
                    k1 = kseg[:, :, 0:64]
                    k2 = kseg[:, :, 64:128]
                    co = cosk[:, tt, :].unsqueeze(1).broadcast_to((128, 4, 64))
                    si = sink[:, tt, :].unsqueeze(1).broadcast_to((128, 4, 64))
                    kr = krope[tt % 2][:, :].rearrange("p (h d) -> p h d", h=4)
                    t1 = st4[:, :].rearrange("p (h d) -> p h d", h=4)[:, :, 0:64]
                    t2 = st4[:, :].rearrange("p (h d) -> p h d", h=4)[:, :, 64:128]
                    v.tensor_tensor(t1, k1, co, ALU.mult)
                    v.tensor_tensor(t2, k2, si, ALU.mult)
                    v.tensor_tensor(kr[:, :, 0:64], t1, t2, ALU.add)
                    v.tensor_tensor(t1, k2, co, ALU.mult)
                    v.tensor_tensor(t2, k1, si, ALU.mult)
                    KRO.inc(v.tensor_tensor(kr[:, :, 64:128], t1, t2,
                                            ALU.subtract))
                    MMD.wait(v, 2 * (tt + 1))
                    last = v.tensor_copy(
                        V_aug[:, :, tt, 0:128].rearrange("p h d -> p h d"),
                        ps_v[tt % 2][:, :].rearrange("p (h d) -> p h d", h=4))
                    VEV.inc(last)
                blk.vector(dve4)

            def st_krstd(tt):
                def act4(s, tt=tt):
                    p2 = tt % 2
                    KBN.wait(s, tt + 1)
                    i = s.activation(st2[:, 4 + 4 * p2:8 + 4 * p2],
                                     st1[:, 4 + 4 * p2:8 + 4 * p2], AF.Ln)
                    asy(s, i)
                    KRS.inc(s.activation(rstdk[:, tt * 4:(tt + 1) * 4],
                                         st2[:, 4 + 4 * p2:8 + 4 * p2],
                                         AF.Exp, scale=-0.5))
                blk.scalar(act4)

            def st_kscale(tt):
                def dvf(v, tt=tt):
                    KRS.wait(v, tt + 1)
                    last = None
                    for h in range(4):
                        last = v.tensor_scalar(
                            krope[tt % 2][:, h * 128:(h + 1) * 128],
                            krope[tt % 2][:, h * 128:(h + 1) * 128],
                            rstdk[:, tt * 4 + h:tt * 4 + h + 1], None, ALU.mult)
                    KSC.inc(last)
                blk.vector(dvf)

            def st_ktr(tt):
                def pe3(t, tt=tt):
                    KSC.wait(t, tt + 1)
                    KTE.wait(t, max(0, tt - 1))
                    last = None
                    for h in range(4):
                        last = t.transpose(ps_ktr[tt % 2][:, h * 128:(h + 1) * 128],
                                           krope[tt % 2][:, h * 128:(h + 1) * 128],
                                           ident[:, :])
                    KTR.inc(last)
                blk.tensor(pe3)

            def st_ktrev(tt):
                def dve5(v, tt=tt):
                    KTR.wait(v, tt + 1)
                    KTE.inc(v.tensor_copy(
                        V_aug[:, 0, 0, 0:1] if False else
                        K_fm[:, :, tt * 128:(tt + 1) * 128].rearrange(
                            "p h d -> p h d"),
                        ps_ktr[tt % 2][:, 0:512].rearrange("p (h d) -> p h d",
                                                           h=4)))
                blk.vector(dve5)

            def in_range(tt):
                return 0 <= tt < NT_ALL

            ln_stash = {}
            for m in range(NT_ALL + 8):
                if m == 0:
                    st_load(0)
                    st_load(1)
                    st_load(2)
                elif in_range(m + 2):
                    st_load(m + 2)
                if in_range(m):
                    lns = make_ln(m, m % 4, x_f[m % 3], h1[m % 2],
                                  extra_dve_wait=(
                                      lambda v, tt=m: XL.wait(v, 16 * (tt + 1))),
                                  extra_act_wait=(
                                      lambda s, tt=m: TRP.wait(s, 2 * (tt - 1))
                                      if tt >= 2 else None))
                else:
                    lns = None
                if in_range(m - 4):
                    st_trev(m - 4)
                if in_range(m - 7):
                    st_ktrev(m - 7)
                if lns:
                    ln_stash[m % 4] = lns
                    blk.vector(lns[0])
                if in_range(m - 1):
                    blk.scalar(ln_stash[(m - 1) % 4][1])
                if in_range(m - 2):
                    blk.vector(ln_stash[(m - 2) % 4][2])
                    blk.scalar(ln_stash[(m - 2) % 4][3])
                if in_range(m - 3):
                    st_tr(m - 3)
                if in_range(m - 4):
                    st_kvmm(m - 4)
                if in_range(m - 6):
                    st_krstd(m - 6)
                if in_range(m - 5):
                    st_kevac(m - 5)
                    st_kstats(m - 5)
                if in_range(m - 6):
                    st_kscale(m - 6)
                    st_ktr(m - 6)
                if m >= 7 and (m - 7) % 2 == 0 and (m - 7) // 2 < 16:
                    emit_quant_tile(8 + (m - 7) // 2)
            barrier()
            if debug:
                def dbgw(g):
                    OST.inc(g.dma_start(out=DBG_W[...], in_=wkvT[...]), 16)
                barrier()
                blk.gpsimd(dbgw)
            barrier()
        sQ.close()
        Q4 = sA.enter_context(
            nc.sbuf_tensor("Q4", [128, NKV * NT_OWN * 4 * 128], BF16))
        if stop_after == 2:
            sA.close()
            return nc

        with ExitStack() as s1c:
            sb1 = lambda n, s, d: s1c.enter_context(nc.sbuf_tensor(n, s, d))
            ps1 = lambda n, s, d: s1c.enter_context(nc.psum_tensor(n, s, d))
            x_f = [sb1(f"xo{i}", [128, D], F32) for i in range(3)]
            h1 = [sb1(f"h1o{i}", [128, D], BF16) for i in range(2)]
            h1ownT = sb1("h1ownT", [128, NT_OWN, 16, 128], BF16)
            wqT = sb1("wqT", [128, 16, 512], BF16)
            qbf = [sb1(f"qbf{i}", [128, 512], BF16) for i in range(2)]
            qrope = [sb1(f"qrope{i}", [128, 512], BF16) for i in range(3)]
            qfin = [sb1(f"qfin{i}", [128, 512], BF16) for i in range(2)]
            ps_tr = [ps1(f"pstrc{i}", [128, 1024], BF16) for i in range(2)]
            ps_qtr = [ps1(f"psqtr{i}", [128, 1024], BF16) for i in range(2)]
            ps_q = [ps1(f"psq{i}", [128, 512], F32) for i in range(2)]

            trp0, tre0 = TRP.n, TRE.n
            ktr0, kte0 = KTR.n, KTE.n
            mmd0, kev0, kbn0, krs0, kro0 = MMD.n, KEV.n, KBN.n, KRS.n, KRO.n
            h1d0, xl0, qed0 = H1D.n, XL.n, QED.n

            def st_load(tt):
                def sy(sy_, tt=tt):
                    if tt >= 3:
                        H1D.wait(sy_, h1d0 + tt - 2)
                    XL.inc(sy_.dma_start(out=x_f[tt % 3][:, :],
                                         in_=X_OWN[tt * 128:(tt + 1) * 128, :]), 16)
                blk.sync(sy)

            def st_tr(tt):
                def pe1(t, tt=tt):
                    H1D.wait(t, h1d0 + tt + 1)
                    TRE.wait(t, tre0 + 2 * tt)
                    for half in range(2):
                        last = None
                        for kk in range(8):
                            dc = half * 8 + kk
                            last = t.transpose(
                                ps_tr[half][:, kk * 128:(kk + 1) * 128],
                                h1[tt % 2][:, dc * 128:(dc + 1) * 128],
                                ident[:, :])
                        TRP.inc(last)
                blk.tensor(pe1)

            def st_trev(tt):
                def dve3(v, tt=tt):
                    for half in range(2):
                        TRP.wait(v, trp0 + 2 * tt + half + 1)
                        dst = h1ownT[:, tt, half * 8:(half + 1) * 8, :]
                        TRE.inc(v.tensor_copy(
                            dst,
                            ps_tr[half][:, :].rearrange("p (k f) -> p k f", k=8)))
                blk.vector(dve3)

            ln_stash = {}
            for m in range(NT_OWN + 4):
                if m == 0:
                    st_load(0)
                if m + 1 < NT_OWN:
                    st_load(m + 1)
                if m < NT_OWN:
                    lns = make_ln(h1d0 + m, m % 4, x_f[m % 3], h1[m % 2],
                                  extra_dve_wait=(
                                      lambda v, tt=m: XL.wait(
                                          v, xl0 + 16 * (tt + 1))),
                                  extra_act_wait=(
                                      lambda s, tt=m: TRP.wait(
                                          s, trp0 + 2 * (tt - 1))
                                      if tt >= 2 else None))
                    ln_stash[m % 4] = lns
                    blk.vector(lns[0])
                if 0 <= m - 1 < NT_OWN:
                    blk.scalar(ln_stash[(m - 1) % 4][1])
                if 0 <= m - 2 < NT_OWN:
                    blk.vector(ln_stash[(m - 2) % 4][2])
                    blk.scalar(ln_stash[(m - 2) % 4][3])
                if 0 <= m - 3 < NT_OWN:
                    st_tr(m - 3)
                if 0 <= m - 4 < NT_OWN:
                    st_trev(m - 4)

            wq_t = {}

            def st_wqload(ob):
                def syw(sy_, ob=ob):
                    QST.wait(sy_, 16 * 24)
                    if ob >= 1:
                        MMD.wait(sy_, mmd0 + 8 * ob)
                    for dc in range(16):
                        WTL.inc(sy_.dma_start(
                            out=wqT[:, dc, :],
                            in_=WQ_EFF[ob * 512:(ob + 1) * 512,
                                       dc * 128:(dc + 1) * 128],
                            transpose=True), 16)
                blk.sync(syw)
                wq_t[ob] = WTL.n

            def st_qmm(n):
                ob, tt = n // 8, n % 8

                def pe2(t, ob=ob, tt=tt, n=n):
                    if tt == 0:
                        WTL.wait(t, wq_t[ob])
                    TRE.wait(t, tre0 + 2 * (tt + 1))
                    KEV.wait(t, kev0 + max(0, n - 1))
                    last = None
                    for dc in range(16):
                        last = t.matmul(ps_q[n % 2][:, :], h1ownT[:, tt, dc, :],
                                        wqT[:, dc, :],
                                        start=(dc == 0), stop=(dc == 15))
                    MMD.inc(last)
                blk.tensor(pe2)

            def st_qevac(n):
                def act3(s, n=n):
                    MMD.wait(s, mmd0 + n + 1)
                    KRO.wait(s, kro0 + max(0, n - 1))
                    KEV.inc(s.activation(qbf[n % 2][:, :], ps_q[n % 2][:, :],
                                         AF.Copy))
                blk.scalar(act3)

            def st_qstats(n):
                ob, tt = n // 8, n % 8

                def dve4(v, ob=ob, tt=tt, n=n):
                    KEV.wait(v, kev0 + n + 1)
                    KRS.wait(v, krs0 + max(0, n - 1))
                    qseg = qbf[n % 2][:, :].rearrange("p (h d) -> p h d", h=4)
                    emit_kstats(v, n % 2, qseg)
                    KTR.wait(v, ktr0 + max(0, n - 2))
                    q1 = qseg[:, :, 0:64]
                    q2 = qseg[:, :, 64:128]
                    co = cosq[:, tt, :].unsqueeze(1).broadcast_to((128, 4, 64))
                    si = sinq[:, tt, :].unsqueeze(1).broadcast_to((128, 4, 64))
                    qr = qrope[n % 3][:, :].rearrange("p (h d) -> p h d", h=4)
                    t1 = st4[:, :].rearrange("p (h d) -> p h d", h=4)[:, :, 0:64]
                    t2 = st4[:, :].rearrange("p (h d) -> p h d", h=4)[:, :, 64:128]
                    v.tensor_tensor(t1, q1, co, ALU.mult)
                    v.tensor_tensor(t2, q2, si, ALU.mult)
                    v.tensor_tensor(qr[:, :, 0:64], t1, t2, ALU.add)
                    v.tensor_tensor(t1, q2, co, ALU.mult)
                    v.tensor_tensor(t2, q1, si, ALU.mult)
                    KRO.inc(v.tensor_tensor(qr[:, :, 64:128], t1, t2,
                                            ALU.subtract))
                blk.vector(dve4)

            def st_qrstd(n):
                def act4(s, n=n):
                    p2 = n % 2
                    p3 = n % 3
                    KBN.wait(s, kbn0 + n + 1)
                    i = s.activation(st2[:, 4 + 4 * p2:8 + 4 * p2],
                                     st1[:, 4 + 4 * p2:8 + 4 * p2], AF.Ln)
                    asy(s, i)
                    KRS.inc(s.activation(st3[:, 4 * p3:4 * p3 + 4],
                                         st2[:, 4 + 4 * p2:8 + 4 * p2],
                                         AF.Exp, scale=-0.5))
                blk.scalar(act4)

            def st_qscale(n):
                ob = n // 8

                def dve5(v, ob=ob, n=n):
                    p3 = n % 3
                    KRS.wait(v, krs0 + n + 1)
                    KTR.wait(v, ktr0 + max(0, n - 1))
                    i = v.tensor_tensor(st3[:, 12:16],
                                        st3[:, 4 * p3:4 * p3 + 4],
                                        qgs[:, ob * 4:(ob + 1) * 4], ALU.mult)
                    vs(v, i)
                    last = None
                    for g in range(4):
                        last = v.tensor_scalar(qfin[n % 2][:, g * 128:(g + 1) * 128],
                                               qrope[n % 3][:, g * 128:(g + 1) * 128],
                                               st3[:, 12 + g:13 + g],
                                               None, ALU.mult)
                    QED.inc(last)
                blk.vector(dve5)

            def st_qtr(n):
                def pe3(t, n=n):
                    QED.wait(t, qed0 + n + 1)
                    KTE.wait(t, kte0 + max(0, n - 1))
                    last = None
                    for g in range(4):
                        last = t.transpose(ps_qtr[n % 2][:, g * 128:(g + 1) * 128],
                                           qfin[n % 2][:, g * 128:(g + 1) * 128],
                                           ident[:, :])
                    KTR.inc(last)
                blk.tensor(pe3)

            def st_qtrev(n):
                ob, tt = n // 8, n % 8

                def dve6(v, ob=ob, tt=tt, n=n):
                    KTR.wait(v, ktr0 + n + 1)
                    off = ((ob * NT_OWN + tt) * 4) * 128
                    KTE.inc(v.tensor_copy(Q4[:, off:off + 512],
                                          ps_qtr[n % 2][:, 0:512]))
                blk.vector(dve6)

            NQ = 32
            for m in range(NQ + 6):
                if m % 8 == 0 and m // 8 < 4:
                    st_wqload(m // 8)
                if 0 <= m - 6 < NQ:
                    st_qtrev(m - 6)
                if m < NQ:
                    st_qmm(m)
                if 0 <= m - 1 < NQ:
                    st_qevac(m - 1)
                if 0 <= m - 2 < NQ:
                    st_qstats(m - 2)
                if 0 <= m - 3 < NQ:
                    st_qrstd(m - 3)
                if 0 <= m - 4 < NQ:
                    st_qscale(m - 4)
                if 0 <= m - 5 < NQ:
                    st_qtr(m - 5)
            if debug:
                def dbgh(g):
                    OST.inc(g.dma_start(out=DBG_H[...], in_=h1ownT[...]), 16)
                barrier()
                blk.gpsimd(dbgh)
            barrier()
        if debug:
            def dbg1(g):
                OST.inc(g.dma_start(out=DBG_K[...], in_=K_fm[...]), 16)
                OST.inc(g.dma_start(out=DBG_V[...], in_=V_aug[...]), 16)
                OST.inc(g.dma_start(out=DBG_Q[...], in_=Q4[...]), 16)
                OST.inc(g.dma_start(out=DBG_R[...], in_=rstdk[...]), 16)
            blk.gpsimd(dbg1)
            barrier()
        if stop_after == 3:
            sA.close()
            return nc

        sB = ExitStack()
        y_nB = sB.enter_context(nc.sbuf_tensor("y_nB", [128, NT_OWN, D], BF16,
                                               side="right"))

        with ExitStack() as s2:
            sb1 = lambda n, s, d: s2.enter_context(nc.sbuf_tensor(n, s, d))
            ps1 = lambda n, s, d: s2.enter_context(nc.psum_tensor(n, s, d))
            alloc_quant_scratch(s2, "b")
            P_buf = sb1("P_buf", [128, 8 * 512], BF16)
            y_tm = [sb1(f"ytm{i}", [128, D], F32) for i in range(2)]
            linv = sb1("linv", [128, 4], F32)
            ps_s = ps1("ps_s", [128, 4 * 512], F32)
            ps_o = [ps1(f"pso{i}", [128, 512], F32) for i in range(4)]

            qkd0, expd0, mskd0, avd0, oev0 = QKD.n, EXPD.n, MSKD.n, AVD.n, OEV.n
            yst0, yrs0, ynd0 = YST.n, YRS.n, YND.n
            trp0, tre0 = TRP.n, TRE.n
            qbases = (QBL.n, QTH.n, QNG.n, QPS.n, QWE.n, QST.n)
            qwe0, qst0, qps0 = QWE.n, QST.n, QPS.n

            nj_of = [4 * tau + 4 for tau in range(NT_OWN)]
            gc_of = [0] * NT_OWN
            dg_of = [0] * NT_OWN
            for tau in range(1, NT_OWN):
                gc_of[tau] = gc_of[tau - 1] + 4 * nj_of[tau - 1]
                dg_of[tau] = dg_of[tau - 1] + 16

            pend_rms = [None]
            pend_ynd = [None]

            for tau in range(ph2_taus):
                nj = nj_of[tau]
                gc_tau = gc_of[tau]
                dg_tau = dg_of[tau]

                LA = 4

                def emit_qk(t, tau, h, j, c):
                    EXPD.wait(t, expd0 + max(0, c // 2 - 1))
                    qoff = (h * NT_OWN + tau) * 4 * 128
                    so = (c % 4) * 512
                    QKD.inc(t.matmul(ps_s[:, so:so + 512],
                                     K_fm[:, h, j * 128:(j + 1) * 128],
                                     Q4[:, qoff:qoff + 512],
                                     start=True, stop=True))

                def emit_av(t, tau, h, j, c, nj, dcnt, s_i):
                    if j == 0 and s_i > 0:
                        OEV.wait(t, oev0 + s_i)
                    if j >= nj - 4:
                        MSKD.wait(t, mskd0 + dcnt)
                    else:
                        EXPD.wait(t, expd0 + c // 2 + 1)
                    last = None
                    po = (c % 8) * 512
                    for g in range(4):
                        last = t.matmul(ps_o[g][:, 0:129],
                                        P_buf[:, po + g * 128:po + (g + 1) * 128],
                                        V_aug[:, h, j, 0:129],
                                        start=(j == 0), stop=(j == nj - 1))
                    AVD.inc(last)

                def pe(t, tau=tau, nj=nj, gc_tau=gc_tau, dg_tau=dg_tau):
                    if tau == 0:
                        KTE.wait(t)
                        VEV.wait(t)
                    items = []
                    dcnt = dg_tau
                    for h in range(4):
                        for j in range(nj):
                            c = gc_tau + h * nj + j
                            if j >= nj - 4:
                                dcnt += 1
                            items.append((h, j, c, dcnt, 4 * tau + h))
                    for k, (h, j, c, dc_, s_i) in enumerate(items):
                        emit_qk(t, tau, h, j, c)
                        if k >= LA:
                            h2_, j2_, c2_, d2_, s2_ = items[k - LA]
                            emit_av(t, tau, h2_, j2_, c2_, nj, d2_, s2_)
                    for k in range(len(items) - LA, len(items)):
                        h2_, j2_, c2_, d2_, s2_ = items[k]
                        emit_av(t, tau, h2_, j2_, c2_, nj, d2_, s2_)
                blk.tensor(pe)

                def act(s, tau=tau, nj=nj, gc_tau=gc_tau, prev=pend_rms[0]):
                    for idx, i in enumerate(range(gc_tau // 2,
                                                  (gc_tau + 4 * nj) // 2)):
                        QKD.wait(s, qkd0 + 2 * i + 2)
                        AVD.wait(s, avd0 + max(0, 2 * i - 6))
                        po = (2 * i % 8) * 512
                        so = (2 * i % 4) * 512
                        EXPD.inc(s.activation(P_buf[:, po:po + 1024],
                                              ps_s[:, so:so + 1024], AF.Exp))
                        if idx == 3 and prev is not None:
                            prev(s)
                blk.scalar(act)

                def mk_rms(tau):
                    p = tau % 2

                    def rms(s, tau=tau, p=p):
                        YST.wait(s, yst0 + tau + 1)
                        i = s.activation(st1[:, 2 * p + 1:2 * p + 2],
                                         st1[:, 2 * p:2 * p + 1], AF.Ln)
                        asy(s, i)
                        YRS.inc(s.activation(st2[:, p:p + 1],
                                             st1[:, 2 * p + 1:2 * p + 2],
                                             AF.Exp, scale=-0.5))
                    return rms
                pend_rms[0] = mk_rms(tau)

                def dve(v, tau=tau, nj=nj, gc_tau=gc_tau, dg_tau=dg_tau,
                        prev=pend_ynd[0]):
                    c = gc_tau
                    dcnt = dg_tau
                    for h in range(4):
                        for j in range(nj):
                            if j >= nj - 4:
                                EXPD.wait(v, expd0 + c // 2 + 1)
                                dcnt += 1
                                po = (c % 8) * 512
                                MSKD.inc(v.tensor_tensor(
                                    P_buf[:, po:po + 512], P_buf[:, po:po + 512],
                                    masks[:, j - (nj - 4), :], ALU.mult))
                            c += 1
                        AVD.wait(v, avd0 + c)
                        i = None
                        for g in range(4):
                            i = v.reciprocal(linv[:, g:g + 1], ps_o[g][:, 128:129])
                        vs(v, i)
                        last = None
                        for g in range(4):
                            hh = 4 * h + g
                            last = v.tensor_scalar(
                                y_tm[tau % 2][:, hh * 128:(hh + 1) * 128],
                                ps_o[g][:, 0:128], linv[:, g:g + 1], None, ALU.mult)
                        OEV.inc(last)
                        if h == 0 and prev is not None:
                            prev(v)
                    p = tau % 2
                    v.wait_ge(OEV.h, OEV.n)
                    for g4 in range(4):
                        i = v.bn_stats(bn6[:, g4, :],
                                       y_tm[tau % 2][:, g4 * 512:(g4 + 1) * 512])
                    vs(v, i)
                    means = bn6[:, 0:4, 1::3]
                    ctvs = bn6[:, 0:4, 2::3]
                    st8v = st8[:, 0:8].rearrange("p (g t) -> p g t", g=4)
                    v.tensor_reduce(stL[:, 1:2], ctvs, AX.XY, ALU.add)
                    i = v.tensor_tensor(st8v, means, means, ALU.mult)
                    vs(v, i)
                    i = v.tensor_reduce(stL[:, 2:3], st8v, AX.XY, ALU.add)
                    vs(v, i)
                    i = v.tensor_scalar(stL[:, 3:4], stL[:, 1:2],
                                        1.0 / 2048, RMS_EPS, ALU.mult, ALU.add)
                    vs(v, i)
                    YST.inc(v.scalar_tensor_tensor(st1[:, 2 * p:2 * p + 1],
                                                   stL[:, 2:3],
                                                   0.125, stL[:, 3:4],
                                                   ALU.mult, ALU.add))
                blk.vector(dve)

                def mk_ynd(tau):
                    p = tau % 2

                    def ynd(v, tau=tau, p=p):
                        YRS.wait(v, yrs0 + tau + 1)
                        YND.inc(v.tensor_scalar(y_nB[:, tau, :],
                                                y_tm[tau % 2][:, :],
                                                st2[:, p:p + 1], None, ALU.mult))
                    return ynd
                pend_ynd[0] = mk_ynd(tau)

                if ph2_quant:
                    for k2 in range(2):
                        emit_quant_tile(24 + tau * 2 + k2)

            if pend_rms[0] is not None:
                blk.scalar(pend_rms[0])
            if pend_ynd[0] is not None:
                blk.vector(pend_ynd[0])
            barrier()
        if debug:
            def dbg2(g):
                OST.inc(g.dma_start(out=DBG_Y[...], in_=y_nB[...]), 16)
            blk.gpsimd(dbg2)
            barrier()
        if stop_after == 4 or ph2_taus < NT_OWN:
            sA.close()
            sB.close()
            return nc

        sA.close()

        sC = ExitStack()
        h2T = sC.enter_context(nc.sbuf_tensor("h2T", [128, 16, OWN], BF16))

        with ExitStack() as s3:
            sb1 = lambda n, s, d: s3.enter_context(nc.sbuf_tensor(n, s, d))
            ps1 = lambda n, s, d: s3.enter_context(nc.psum_tensor(n, s, d))
            wpT = sb1("wpT", [128, 16, D], BF16)
            x_f = [sb1(f"x3{i}", [128, D], F32) for i in range(2)]
            x2 = [sb1(f"x2{i}", [128, D], F32) for i in range(2)]
            h2 = [sb1(f"h2b{i}", [128, D], BF16) for i in range(2)]
            yT = [sb1(f"yT{i}", [128, D], BF16) for i in range(2)]
            ps_p = [ps1(f"psp{i}", [128, 512], F32) for i in range(4)]
            ps_tr = [ps1(f"pstr3{i}", [128, 1024], BF16) for i in range(2)]
            ps_try = ps1("ps_try", [128, D], BF16)

            def syw(sy_):
                QST.wait(sy_)
                for dc in range(16):
                    WTL.inc(sy_.dma_start(out=wpT[:, dc, :],
                                          in_=WP_EFF[0:D, dc * 128:(dc + 1) * 128],
                                          transpose=True), 16)
            blk.sync(syw)
            wpT_t = WTL.n

            xl0, mmd0, pje0, h2d0 = XL.n, MMD.n, PJE.n, H1D.n
            trp0, tre0, ost0 = TRP.n, TRE.n, OST.n
            ytp0, yte0 = YTP.n, YTE.n

            def emit_ytr3(t, tt):
                if tt >= 1:
                    YTE.wait(t, yte0 + tt)
                last = None
                for dc in range(16):
                    last = t.transpose(ps_try[:, dc * 128:(dc + 1) * 128],
                                       y_nB[:, tt, dc * 128:(dc + 1) * 128],
                                       ident[:, :])
                YTP.inc(last)

            def pe_ytr0(t):
                emit_ytr3(t, 0)
            blk.tensor(pe_ytr0)

            for tt in range(NT_OWN):
                bi = tt % 2

                def sy(sy_, tt=tt, bi=bi):
                    if tt >= 2:
                        PJE.wait(sy_, pje0 + 4 * (tt - 1))
                    XL.inc(sy_.dma_start(out=x_f[bi][:, :],
                                         in_=X_OWN[tt * 128:(tt + 1) * 128, :]), 16)
                blk.sync(sy)

                def pe(t, tt=tt, bi=bi):
                    if tt + 1 < NT_OWN:
                        emit_ytr3(t, tt + 1)
                    if tt == 0:
                        WTL.wait(t, wpT_t)
                    if tt >= 1:
                        PJE.wait(t, pje0 + 4 * tt)
                    YTE.wait(t, yte0 + tt + 1)
                    last = None
                    for dc in range(16):
                        for ob in range(4):
                            last = t.matmul(ps_p[ob][:, :],
                                            yT[bi][:, dc * 128:(dc + 1) * 128],
                                            wpT[:, dc, ob * 512:(ob + 1) * 512],
                                            start=(dc == 0), stop=(dc == 15))
                    MMD.inc(last)
                blk.tensor(pe)

                def act_yev(s, tt=tt, bi=bi):
                    YTP.wait(s, ytp0 + tt + 1)
                    if tt >= 2:
                        MMD.wait(s, mmd0 + tt - 1)
                    YTE.inc(s.activation(yT[bi][:, :], ps_try[:, :], AF.Identity))
                blk.scalar(act_yev)

                def dve1(v, tt=tt, bi=bi):
                    MMD.wait(v, mmd0 + tt + 1)
                    XL.wait(v, xl0 + 16 * (tt + 1))
                    if tt >= 2:
                        OST.wait(v, ost0 + 16 * (tt - 1))
                    for ob in range(4):
                        PJE.inc(v.tensor_tensor(x2[bi][:, ob * 512:(ob + 1) * 512],
                                                ps_p[ob][:, :],
                                                x_f[bi][:, ob * 512:(ob + 1) * 512],
                                                ALU.add))
                blk.vector(dve1)

                def h2guard(s, tt=tt):
                    if tt >= 2:
                        TRP.wait(s, trp0 + 4 * (tt - 1))

                def x2w(v):
                    v.wait_ge(PJE.h, PJE.n)
                emit_ln(h2d0 + tt, tt % 4, x2[bi], h2[bi], extra_dve_wait=x2w,
                        extra_act_wait=h2guard)

                def pst(g, tt=tt, bi=bi):
                    H1D.wait(g, h2d0 + tt + 1)
                    OST.inc(g.dma_start(out=OUT[tt * 128:(tt + 1) * 128, :],
                                        in_=x2[bi][:, :]), 16)
                blk.gpsimd(pst)

                emit_tr16(h2[bi],
                          lambda grp, tt=tt: h2T[:, grp * 4:(grp + 1) * 4,
                                                 tt * 128:(tt + 1) * 128],
                          ps_tr, h2d0 + tt, trp0, tre0, 4 * tt)
            barrier()
        if debug:
            def dbg3(g):
                OST.inc(g.dma_start(out=DBG_H2[...], in_=h2T[...]), 16)
            blk.gpsimd(dbg3)
            barrier()
        if stop_after == 5:
            sB.close()
            sC.close()
            return nc

        sB.close()

        with ExitStack() as s4:
            sb0 = lambda n, s, d: s4.enter_context(nc.sbuf_tensor(n, s, d))
            hdn = sb0("hdn", [128, NIC, 1024], BF16)
            dw = [sb0(f"dw{i}", [128, NIC, 128], BF16) for i in range(3)]
            ddb = [sb0(f"ddb{i}", [128, 1024], BF16) for i in range(2)]
            ddf = sb0("ddf", [128, 2, 1024], F32)

            hdn0 = HDN.n
            dwl0, mmdd0 = DWL.n, MMD.n
            dw_t = {}

            DWPRE = {28: (0, 0), 30: (0, 1), 32: (0, 2), 34: (0, 3),
                     36: (1, 0), 38: (1, 1), 40: (1, 2), 42: (1, 3)}

            def dw_chunk(g, oc, c):
                i0, i1 = c * 11, (c + 1) * 11
                DWL.inc(g.dma_start(
                    out=dw[oc % 3][:, i0:i1, :],
                    in_=DWT[oc].rearrange("p (c f) -> p c f", c=NIC)[:, i0:i1, :]),
                    16)
                if c == 3:
                    dw_t[oc] = DWL.n

            with ExitStack() as s4a:
                sb1 = lambda n, s, d: s4a.enter_context(nc.sbuf_tensor(n, s, d))
                ps1 = lambda n, s, d: s4a.enter_context(nc.psum_tensor(n, s, d))
                gw = [sb1(f"gw{i}", [128, 16, 128], BF16) for i in range(2)]
                uw = [sb1(f"uw{i}", [128, 16, 128], BF16) for i in range(2)]
                sgb = [sb1(f"sgb{i}", [128, 1024], BF16) for i in range(2)]
                ubb = [sb1(f"ubb{i}", [128, 1024], BF16) for i in range(2)]
                pool = [ps1(f"pp{i}", [128, 512], F32) for i in range(6)]

                fwl0, fwu0 = FWL.n, FWU.n
                mmg0, gue0, ube0 = MMG.n, GUE.n, UBE.n
                bank_owner = [None] * 6

                for ic in range(NIC):
                    def pload(g, ic=ic):
                        if ic >= 2:
                            MMG.wait(g, mmg0 + 4 * (ic - 2) + 2)
                        FWL.inc(g.dma_start(
                            out=gw[ic % 2][...],
                            in_=GWT[ic].rearrange("p (c f) -> p c f", c=16)), 16)
                        if ic >= 2:
                            MMG.wait(g, mmg0 + 4 * (ic - 2) + 4)
                        FWU.inc(g.dma_start(
                            out=uw[ic % 2][...],
                            in_=UWT[ic].rearrange("p (c f) -> p c f", c=16)), 16)
                        if ic in DWPRE:
                            dw_chunk(g, *DWPRE[ic])
                    blk.gpsimd(pload)

                    owner_snap = [bank_owner[(4 * ic + qi) % 6] for qi in range(4)]

                    def pe(t, ic=ic, owner_snap=owner_snap):
                        FWL.wait(t, fwl0 + 16 * (ic + 1))
                        for qi in range(4):
                            if qi == 2:
                                FWU.wait(t, fwu0 + 16 * (ic + 1))
                            b = (4 * ic + qi) % 6
                            own = owner_snap[qi]
                            if own is not None:
                                own[0].wait(t, own[1])
                            w = (gw if qi < 2 else uw)[ic % 2]
                            half = qi % 2
                            last = None
                            for dc in range(16):
                                last = t.matmul(
                                    pool[b][:, :], w[:, dc, :],
                                    h2T[:, dc, half * 512:(half + 1) * 512],
                                    start=(dc == 0), stop=(dc == 15))
                            MMG.inc(last)
                    blk.tensor(pe)

                    def act(s, ic=ic):
                        if ic >= 2:
                            HDN.wait(s, hdn0 + ic - 1)
                        for half in range(2):
                            MMG.wait(s, mmg0 + 4 * ic + 1 + half)
                            GUE.inc(s.activation(
                                sgb[ic % 2][:, half * 512:(half + 1) * 512],
                                pool[(4 * ic + half) % 6][:, :], AF.Silu,
                                scale=gs2[:, ic:ic + 1]))
                    blk.scalar(act)
                    bank_owner[(4 * ic) % 6] = (GUE, gue0 + 2 * ic + 1)
                    bank_owner[(4 * ic + 1) % 6] = (GUE, gue0 + 2 * ic + 2)

                    def dve(v, ic=ic):
                        for half in range(2):
                            MMG.wait(v, mmg0 + 4 * ic + 3 + half)
                            UBE.inc(v.tensor_scalar(
                                ubb[ic % 2][:, half * 512:(half + 1) * 512],
                                pool[(4 * ic + 2 + half) % 6][:, :],
                                us2[:, ic:ic + 1], None, ALU.mult))
                        UBE.wait(v, ube0 + 2 * ic + 2)
                        GUE.wait(v, gue0 + 2 * ic + 2)
                        HDN.inc(v.tensor_tensor(hdn[:, ic, :], sgb[ic % 2][:, :],
                                                ubb[ic % 2][:, :], ALU.mult))
                    blk.vector(dve)
                    bank_owner[(4 * ic + 2) % 6] = (UBE, ube0 + 2 * ic + 1)
                    bank_owner[(4 * ic + 3) % 6] = (UBE, ube0 + 2 * ic + 2)

            with ExitStack() as s4d:
                ps1 = lambda n, s, d: s4d.enter_context(nc.psum_tensor(n, s, d))
                ps_d = [ps1(f"psd{i}", [128, 1024], F32) for i in range(2)]
                ps_tro = [ps1(f"pstro{i}", [128, 1024], BF16) for i in range(2)]

                dse0, dte0, ktr0 = DSE.n, DTE.n, KTR.n
                oacc0 = OACC.n

                def dwload(oc):
                    def gld(g, oc=oc):
                        if oc >= 3:
                            MMD.wait(g, mmdd0 + oc - 2)
                        DWL.inc(g.dma_start(
                            out=dw[oc % 3][...],
                            in_=DWT[oc].rearrange("p (c f) -> p c f", c=NIC)), 16)
                        dw_t[oc] = DWL.n
                    blk.gpsimd(gld)

                def mmdn(oc):
                    def pe(t, oc=oc):
                        DWL.wait(t, dw_t[oc])
                        if oc == 0:
                            HDN.wait(t, hdn0 + NIC)
                        if oc >= 2:
                            DSE.wait(t, dse0 + oc - 1)
                        last = None
                        for icc in range(NIC):
                            for half in range(2):
                                last = t.matmul(
                                    ps_d[oc % 2][:, half * 512:(half + 1) * 512],
                                    dw[oc % 3][:, icc, :],
                                    hdn[:, icc, half * 512:(half + 1) * 512],
                                    start=(icc == 0), stop=(icc == NIC - 1))
                        MMD.inc(last)
                    blk.tensor(pe)

                def ddbev(oc):
                    def dve1(v, oc=oc):
                        MMD.wait(v, mmdd0 + oc + 1)
                        if oc >= 2:
                            KTR.wait(v, ktr0 + oc - 1)
                        DSE.inc(v.tensor_scalar(ddb[oc % 2][:, :],
                                                ps_d[oc % 2][:, :],
                                                ds2[:, oc:oc + 1], None, ALU.mult))
                    blk.vector(dve1)

                OUT_T = OUT.rearrange("(t p) c -> p t c", p=128)

                def trout(oc):
                    def pe2(t, oc=oc):
                        DSE.wait(t, dse0 + oc + 1)
                        if oc >= 2:
                            DTE.wait(t, dte0 + oc - 1)
                        last = None
                        for tt in range(NT_OWN):
                            last = t.transpose(
                                ps_tro[oc % 2][:, tt * 128:(tt + 1) * 128],
                                ddb[oc % 2][:, tt * 128:(tt + 1) * 128],
                                ident[:, :])
                        KTR.inc(last)
                    blk.tensor(pe2)

                    def dve2(v, oc=oc):
                        KTR.wait(v, ktr0 + oc + 1)
                        if oc >= 2:
                            OACC.wait(v, oacc0 + 16 * (oc - 1))
                        DTE.inc(v.tensor_copy(ddf[:, oc % 2, :],
                                              ps_tro[oc % 2][:, :]))
                    blk.vector(dve2)

                def pacc_emit(oc):
                    def pacc(g, oc=oc):
                        DTE.wait(g, dte0 + oc + 1)
                        OST.wait(g, 16 * 8)
                        OACC.inc(g.dma_start(
                            out=OUT_T[:, :, oc * 128:(oc + 1) * 128],
                            in_=ddf[:, oc % 2, :].rearrange("p (t c) -> p t c",
                                                            t=NT_OWN),
                            accum_op=ALU.add), 16)
                    blk.gpsimd(pacc)

                mmdn(0)
                mmdn(1)
                for oc in range(16):
                    if oc + 2 < 16:
                        dwload(oc + 2)
                    ddbev(oc)
                    trout(oc)
                    if oc + 2 < 16:
                        mmdn(oc + 2)
                    if oc >= 1:
                        pacc_emit(oc - 1)
                pacc_emit(15)

            def fin(g):
                OACC.wait(g)
                OST.wait(g)
            blk.gpsimd(fin)

        sC.close()

    return nc


def _host_prepare(L0_attn_x, L0_attn_w_qkv, L0_attn_w_proj, L0_attn_q_gain,
                  L0_ffn_gate_w, L0_ffn_up_w, L0_ffn_down_w,
                  L0_ffn_gate_s, L0_ffn_up_s, L0_ffn_down_s):
    f32 = np.float32
    x = np.asarray(L0_attn_x, f32)
    wq = np.ascontiguousarray(np.asarray(L0_attn_w_qkv, f32))
    wp = np.ascontiguousarray(np.asarray(L0_attn_w_proj, f32))
    qg = np.asarray(L0_attn_q_gain, f32)
    gwm = np.asarray(L0_ffn_gate_w, f32)
    uwm = np.asarray(L0_ffn_up_w, f32)
    dwm = np.asarray(L0_ffn_down_w, f32)
    gs = np.asarray(L0_ffn_gate_s, f32)
    us = np.asarray(L0_ffn_up_s, f32)
    ds = np.asarray(L0_ffn_down_s, f32)

    gwt = np.ascontiguousarray(
        gwm.reshape(NIC, 128, 16, 128).transpose(0, 3, 2, 1)).reshape(NIC, 128, 2048)
    uwt = np.ascontiguousarray(
        uwm.reshape(NIC, 128, 16, 128).transpose(0, 3, 2, 1)).reshape(NIC, 128, 2048)
    dwt = np.ascontiguousarray(
        dwm.reshape(16, 128, NIC, 128).transpose(0, 3, 2, 1)).reshape(16, 128,
                                                                      NIC * 128)
    gs2 = np.ascontiguousarray(gs.reshape(NIC, 128).T)
    us2 = np.ascontiguousarray(us.reshape(NIC, 128).T)
    ds2 = np.ascontiguousarray(ds.reshape(16, 128).T)
    qg2 = np.ascontiguousarray(np.tile(qg.reshape(1, NH), (128, 1)))

    inv_freq = (1.0 / (10000.0 ** (np.arange(0, HD, 2, dtype=f32) / HD))).astype(f32)
    t_all = np.arange(SEQ, dtype=f32)
    freqs = np.outer(t_all, inv_freq).astype(f32)
    cos_all = np.cos(freqs).astype(BF)
    sin_all = np.sin(freqs).astype(BF)
    cosk = np.ascontiguousarray(cos_all.reshape(NT_ALL, 128, 64).transpose(1, 0, 2))
    sink = np.ascontiguousarray(sin_all.reshape(NT_ALL, 128, 64).transpose(1, 0, 2))
    ident = np.eye(128, dtype=BF)

    per_core = []
    for core in range(8):
        b, lane = core // 4, core % 4
        x_all = np.ascontiguousarray(x[b])
        x_own = np.ascontiguousarray(x[b, lane::4])
        cos_own = np.ascontiguousarray(
            cos_all[lane::4].reshape(NT_OWN, 128, 64).transpose(1, 0, 2))
        sin_own = np.ascontiguousarray(
            sin_all[lane::4].reshape(NT_OWN, 128, 64).transpose(1, 0, 2))
        kk = np.arange(128)[:, None]
        qq = np.arange(128)[None, :]
        m = np.zeros((128, 4, 512), dtype=BF)
        for i in range(4):
            mi = ((128 * i + kk) <= (4 * qq + lane)).astype(BF)
            m[:, i, :] = np.tile(mi, (1, 4))
        per_core.append(dict(
            x_all=x_all, x_own=x_own, wq=wq, wp=wp,
            gwt=gwt, uwt=uwt, dwt=dwt, gs2=gs2, us2=us2, ds2=ds2, qg2=qg2,
            cosk=cosk, sink=sink, cosq=cos_own, sinq=sin_own,
            msk=np.ascontiguousarray(m), idn=ident,
        ))
    return per_core


def kernel(L0_attn_x, L0_attn_w_qkv, L0_attn_w_proj, L0_attn_q_gain,
           L0_ffn_gate_w, L0_ffn_up_w, L0_ffn_down_w,
           L0_ffn_gate_s, L0_ffn_up_s, L0_ffn_down_s):
    in_maps = _host_prepare(L0_attn_x, L0_attn_w_qkv, L0_attn_w_proj,
                            L0_attn_q_gain, L0_ffn_gate_w, L0_ffn_up_w,
                            L0_ffn_down_w, L0_ffn_gate_s, L0_ffn_up_s,
                            L0_ffn_down_s)
    nc = build_program()
    res = run_bass_kernel_spmd(nc, in_maps, core_ids=list(range(8)))
    out = np.empty((B, SEQ, D), np.float32)
    for core in range(8):
        b, lane = core // 4, core % 4
        out[b, lane::4] = res.results[core]["out_own"]
    return out


# version 22
# speedup vs baseline: 1.0318x; 1.0184x over previous
import numpy as np
import ml_dtypes
from contextlib import ExitStack

import concourse.bass as bass
import concourse.mybir as mybir
from concourse.bass_utils import run_bass_kernel_spmd

F32 = mybir.dt.float32
BF16 = mybir.dt.bfloat16
AF = mybir.ActivationFunctionType
ALU = mybir.AluOpType
AX = mybir.AxisListType
BF = ml_dtypes.bfloat16

B, SEQ, D = 2, 4096, 2048
NH, NKV, HD = 16, 4, 128
OQKV = 3072
INTER = 5632
NIC = INTER // 128
OWN = SEQ // 4
NT_ALL = SEQ // 128
NT_OWN = OWN // 128
RMS_EPS = 1.1920929e-07
LN_EPS = 1e-5
THR_C = 0.5009765625
INV_SQRT_HD = 0.08838834764831843


class Sem:
    def __init__(self, stack, nc, name):
        self.h = stack.enter_context(nc.semaphore(name))
        self.n = 0

    def inc(self, instr, k=1):
        instr.then_inc(self.h, k)
        self.n += k
        return instr

    def wait(self, eng, n=None):
        v = self.n if n is None else n
        if v > 0:
            eng.wait_ge(self.h, v)


def build_program(stop_after=None, ph2_taus=NT_OWN, ph2_quant=True, ph2_ytr=True, ph2_tail=True, debug=False):
    nc = bass.Bass("TRN2", target_bir_lowering=False)

    X_ALL = nc.dram_tensor("x_all", [SEQ, D], F32, kind="ExternalInput").ap()
    X_OWN = nc.dram_tensor("x_own", [OWN, D], F32, kind="ExternalInput").ap()
    WQ = nc.dram_tensor("wq", [OQKV, D], F32, kind="ExternalInput").ap()
    WP = nc.dram_tensor("wp", [D, D], F32, kind="ExternalInput").ap()
    GWT = nc.dram_tensor("gwt", [NIC, 128, 2048], F32, kind="ExternalInput").ap()
    UWT = nc.dram_tensor("uwt", [NIC, 128, 2048], F32, kind="ExternalInput").ap()
    DWT = nc.dram_tensor("dwt", [16, 128, NIC * 128], F32, kind="ExternalInput").ap()
    GS2 = nc.dram_tensor("gs2", [128, NIC], F32, kind="ExternalInput").ap()
    US2 = nc.dram_tensor("us2", [128, NIC], F32, kind="ExternalInput").ap()
    DS2 = nc.dram_tensor("ds2", [128, 16], F32, kind="ExternalInput").ap()
    QG2 = nc.dram_tensor("qg2", [128, 16], F32, kind="ExternalInput").ap()
    COSK = nc.dram_tensor("cosk", [128, NT_ALL, 64], BF16, kind="ExternalInput").ap()
    SINK = nc.dram_tensor("sink", [128, NT_ALL, 64], BF16, kind="ExternalInput").ap()
    COSQ = nc.dram_tensor("cosq", [128, NT_OWN, 64], BF16, kind="ExternalInput").ap()
    SINQ = nc.dram_tensor("sinq", [128, NT_OWN, 64], BF16, kind="ExternalInput").ap()
    MSK = nc.dram_tensor("msk", [128, 4, 512], BF16, kind="ExternalInput").ap()
    IDN = nc.dram_tensor("idn", [128, 128], BF16, kind="ExternalInput").ap()

    OUT = nc.dram_tensor("out_own", [OWN, D], F32, kind="ExternalOutput").ap()
    if debug:
        DBG_W = nc.dram_tensor("dbg_w", [128, 16, 1024], BF16, kind="ExternalOutput").ap()
        DBG_H = nc.dram_tensor("dbg_h", [128, NT_OWN, 16, 128], BF16, kind="ExternalOutput").ap()
        DBG_K = nc.dram_tensor("dbg_k", [128, NKV, SEQ], BF16, kind="ExternalOutput").ap()
        DBG_V = nc.dram_tensor("dbg_v", [128, NKV, NT_ALL, 132], BF16, kind="ExternalOutput").ap()
        DBG_Q = nc.dram_tensor("dbg_q", [128, NKV * NT_OWN * 4 * 128], BF16, kind="ExternalOutput").ap()
        DBG_R = nc.dram_tensor("dbg_r", [128, NT_ALL * 4], F32, kind="ExternalOutput").ap()
        DBG_Y = nc.dram_tensor("dbg_y", [128, NT_OWN, D], BF16, kind="ExternalOutput").ap()
        DBG_H2 = nc.dram_tensor("dbg_h2", [128, 16, OWN], BF16, kind="ExternalOutput").ap()

    WQ_EFF = nc.dram_tensor("wq_eff", [OQKV, D], BF16, kind="Internal").ap()
    WP_EFF = nc.dram_tensor("wp_eff", [D, D], BF16, kind="Internal").ap()

    st = ExitStack()
    with st:
        sb = lambda n, s, d: st.enter_context(nc.sbuf_tensor(n, s, d))
        ident = sb("ident", [128, 128], BF16)
        masks = sb("masks", [128, 4, 512], BF16)
        cosk = sb("cosk_t", [128, NT_ALL, 64], BF16)
        sink = sb("sink_t", [128, NT_ALL, 64], BF16)
        cosq = sb("cosq_t", [128, NT_OWN, 64], BF16)
        sinq = sb("sinq_t", [128, NT_OWN, 64], BF16)
        gs2 = sb("gs2_t", [128, NIC], F32)
        us2 = sb("us2_t", [128, NIC], F32)
        ds2 = sb("ds2_t", [128, 16], F32)
        qg2 = sb("qg2_t", [128, 16], F32)
        qgs = sb("qgs_t", [128, 16], F32)
        fence = sb("fence", [1, 16], F32)
        bn6 = sb("bn6", [128, 24, 6], F32)
        bn2 = sb("bn2", [128, 16, 2], F32)
        stV = sb("stV", [128, 4], F32)
        stLN = sb("stLN", [128, 4], F32)
        stR = sb("stR", [128, 4], F32)
        stMR = sb("stMR", [128, 4], F32)
        st1 = sb("st1", [128, 16], F32)
        st2 = sb("st2", [128, 16], F32)
        st3 = sb("st3", [128, 16], F32)
        st4 = sb("st4", [128, 512], BF16)
        st5 = sb("st5", [128, 16], F32)
        st6 = sb("st6", [128, 16], F32)
        st7 = sb("st7", [128, 16], F32)
        stb = sb("stb", [128, 16], BF16)
        st8 = sb("st8", [128, 16], F32)
        stL = sb("stL", [128, 8], F32)
        stK = sb("stK", [128, 16], F32)
        rstdk = sb("rstdk", [128, NT_ALL * 4], F32)

        sems = {}

        def mksem(name):
            sems[name] = Sem(st, nc, name)
            return sems[name]

        CONST = mksem("CONST")
        XL = mksem("XL")
        WTL = mksem("WTL")
        QBL = mksem("QBL")
        QST = mksem("QST")
        FWL = mksem("FWL")
        OST = mksem("OST")
        OACC = mksem("OACC")
        BAR = mksem("BAR")
        LNS = mksem("LNS")
        LNR = mksem("LNR")
        MRD = mksem("MRD")
        H1D = mksem("H1D")
        TRP = mksem("TRP")
        TRE = mksem("TRE")
        MMD = mksem("MMD")
        KEV = mksem("KEV")
        KBN = mksem("KBN")
        KRS = mksem("KRS")
        KRO = mksem("KRO")
        VEV = mksem("VEV")
        KTR = mksem("KTR")
        KTE = mksem("KTE")
        QED = mksem("QED")
        QTH = mksem("QTH")
        QNG = mksem("QNG")
        QPS = mksem("QPS")
        QWE = mksem("QWE")
        QKD = mksem("QKD")
        EXPD = mksem("EXPD")
        MSKD = mksem("MSKD")
        AVD = mksem("AVD")
        OEV = mksem("OEV")
        YST = mksem("YST")
        YRS = mksem("YRS")
        YND = mksem("YND")
        PJE = mksem("PJE")
        H2D = mksem("H2D")
        GUE = mksem("GUE")
        HDN = mksem("HDN")
        DSE = mksem("DSE")
        DTE = mksem("DTE")
        MMG = mksem("MMG")
        UBE = mksem("UBE")
        FWU = mksem("FWU")
        DWL = mksem("DWL")
        VSY = mksem("VSY")
        ASY = mksem("ASY")
        QAB = mksem("QAB")
        KSC = mksem("KSC")
        YTP = mksem("YTP")
        YTE = mksem("YTE")

        blk = st.enter_context(nc.Block())

        fence_ops = {
            "vector": lambda v: v.memset(fence[0:1, 0:1], 0.0),
            "scalar": lambda s: s.memzero(fence[0:1, 2:3]),
            "gpsimd": lambda g: g.memset(fence[0:1, 4:5], 0.0),
            "tensor": lambda t: t.nop(),
        }

        def vs(v, instr):
            VSY.inc(instr)
            v.wait_ge(VSY.h, VSY.n)

        def asy(s, instr):
            ASY.inc(instr)
            s.wait_ge(ASY.h, ASY.n)

        def barrier():
            for e in ("vector", "scalar", "gpsimd", "tensor"):
                def emit_f(eng, e=e):
                    BAR.inc(fence_ops[e](eng))
                getattr(blk, e)(emit_f)
            tgt = BAR.n
            dma_targets = [(s, s.n) for s in (CONST, XL, WTL, QBL, QST, FWL,
                                              FWU, DWL, OST, OACC)]

            def waits(eng):
                eng.wait_ge(BAR.h, tgt)
                for s_, v in dma_targets:
                    if v > 0:
                        eng.wait_ge(s_.h, v)
            for e in ("sync", "gpsimd", "vector", "scalar", "tensor"):
                getattr(blk, e)(waits)

        LNG = [0, 8, 12, 16]

        def make_ln(idx, p, x_t, h_t, extra_dve_wait=None, extra_act_wait=None):
            g0 = LNG[p]

            def dve1(v, idx=idx, p=p, g0=g0):
                if extra_dve_wait is not None:
                    extra_dve_wait(v)
                for g4 in range(4):
                    i = v.bn_stats(bn6[:, g0 + g4, :],
                                   x_t[:, g4 * 512:(g4 + 1) * 512])
                vs(v, i)
                means = bn6[:, g0:g0 + 4, 1::3]
                ctvs = bn6[:, g0:g0 + 4, 2::3]
                st8v = st8[:, 0:8].rearrange("p (g t) -> p g t", g=4)
                v.tensor_reduce(stL[:, 0:1], means, AX.XY, ALU.add)
                v.tensor_reduce(stL[:, 1:2], ctvs, AX.XY, ALU.add)
                i = v.tensor_tensor(st8v, means, means, ALU.mult)
                vs(v, i)
                v.tensor_reduce(stL[:, 2:3], st8v, AX.XY, ALU.add)
                v.tensor_scalar(bn2[:, p, 0:1], stL[:, 0:1], 0.125, None,
                                ALU.mult)
                v.scalar_tensor_tensor(stL[:, 3:4], stL[:, 0:1],
                                       -0.125 * 0.125, stL[:, 0:1],
                                       ALU.mult, ALU.mult)
                i = v.tensor_scalar(stL[:, 4:5], stL[:, 1:2], 1.0 / 2048,
                                    LN_EPS, ALU.mult, ALU.add)
                vs(v, i)
                i = v.scalar_tensor_tensor(stL[:, 5:6], stL[:, 2:3], 0.125,
                                           stL[:, 3:4], ALU.mult, ALU.add)
                vs(v, i)
                LNS.inc(v.scalar_tensor_tensor(stV[:, p:p + 1], stL[:, 5:6],
                                               1.0, stL[:, 4:5],
                                               ALU.mult, ALU.add))

            def act1(s, idx=idx, p=p):
                LNS.wait(s, idx + 1)
                i = s.activation(stLN[:, p:p + 1], stV[:, p:p + 1], AF.Ln)
                asy(s, i)
                LNR.inc(s.activation(stR[:, p:p + 1], stLN[:, p:p + 1], AF.Exp,
                                     scale=-0.5))

            def dve2(v, idx=idx, p=p):
                LNR.wait(v, idx + 1)
                MRD.inc(v.scalar_tensor_tensor(stMR[:, p:p + 1], bn2[:, p, 0:1],
                                               -1.0, stR[:, p:p + 1],
                                               ALU.mult, ALU.mult))

            def act2(s, idx=idx, p=p):
                MRD.wait(s, idx + 1)
                if extra_act_wait is not None:
                    extra_act_wait(s)
                H1D.inc(s.activation(h_t[:, :], x_t[:, :], AF.Identity,
                                     bias=stMR[:, p:p + 1],
                                     scale=stR[:, p:p + 1]))
            return dve1, act1, dve2, act2

        def emit_ln(idx, p, x_t, h_t, extra_dve_wait=None, extra_act_wait=None):
            d1, a1, d2, a2 = make_ln(idx, p, x_t, h_t, extra_dve_wait,
                                     extra_act_wait)
            blk.vector(d1)
            blk.scalar(a1)
            blk.vector(d2)
            blk.scalar(a2)

        def emit_kstats(v, p2, seg):
            kb = 4 if p2 == 0 else 20
            mp = 8 * p2
            for h4 in range(4):
                i = v.bn_stats(bn6[:, kb + h4, :], seg[:, h4, :])
            vs(v, i)
            means = bn6[:, kb:kb + 4, 1::3]
            ctvs = bn6[:, kb:kb + 4, 2::3]
            st8v = st8[:, mp:mp + 8].rearrange("p (g t) -> p g t", g=4)
            v.tensor_reduce(stK[:, mp:mp + 4], ctvs, AX.X, ALU.add)
            i = v.tensor_tensor(st8v, means, means, ALU.mult)
            vs(v, i)
            i = v.tensor_reduce(stK[:, mp + 4:mp + 8], st8v, AX.X, ALU.add)
            vs(v, i)
            i = v.scalar_tensor_tensor(st8[:, mp:mp + 4], stK[:, mp + 4:mp + 8],
                                       64.0, stK[:, mp:mp + 4],
                                       ALU.mult, ALU.add)
            vs(v, i)
            KBN.inc(v.tensor_scalar(st1[:, 4 + 4 * p2:8 + 4 * p2],
                                    st8[:, mp:mp + 4], 1.0 / 128, RMS_EPS,
                                    ALU.mult, ALU.add))

        def emit_tr16(src_t, dst_fn, ps_tr, h1d_idx, trp_base, tre_base, gbase,
                      pe_extra=None):
            def pe1(t):
                H1D.wait(t, h1d_idx + 1)
                if pe_extra is not None:
                    pe_extra(t)
                for grp in range(4):
                    gi = gbase + grp
                    TRE.wait(t, tre_base + max(0, gi - 1))
                    last = None
                    for k in range(4):
                        dc = grp * 4 + k
                        last = t.transpose(ps_tr[gi % 2][:, k * 128:(k + 1) * 128],
                                           src_t[:, dc * 128:(dc + 1) * 128],
                                           ident[:, :])
                    TRP.inc(last)
            blk.tensor(pe1)

            def dve3(v):
                for grp in range(4):
                    gi = gbase + grp
                    TRP.wait(v, trp_base + gi + 1)
                    TRE.inc(v.tensor_copy(
                        dst_fn(grp),
                        ps_tr[gi % 2][:, 0:512].rearrange("p (k f) -> p k f", k=4)))
            blk.vector(dve3)

        def emit_quant(k, w_bf, weff_dst):
            def qact(s, k=k):
                QBL.wait(s, 16 * (k + 1))
                QWE.wait(s, k)
                s.activation(qbuf['aw'][k % 2][:, :], w_bf[:, :], AF.Abs)
                QAB.inc(s.activation(qbuf['sg'][k % 2][:, :], w_bf[:, :], AF.Sign))
            blk.scalar(qact)

            def qdve(v, k=k):
                QBL.wait(v, 16 * (k + 1))
                i = v.tensor_reduce(st5[:, :],
                                    w_bf[:, :].rearrange("p (g d) -> p g d", g=16),
                                    AX.X, ALU.add, apply_absolute_value=True)
                vs(v, i)
                i = v.tensor_scalar(stb[:, :], st5[:, :], 1.0 / 128.0, None,
                                    ALU.mult)
                vs(v, i)
                i = v.tensor_copy(st6[:, :], stb[:, :])
                vs(v, i)
                i = v.tensor_scalar(st7[:, :], st6[:, :], THR_C, None, ALU.mult)
                vs(v, i)
                i = v.tensor_copy(qbuf['thrf'][:, :, :],
                                  st7[:, :].unsqueeze(2).broadcast_to((128, 16, 128)))
                v.tensor_copy(qbuf['scf'][:, :, :],
                              stb[:, :].unsqueeze(2).broadcast_to((128, 16, 128)))
                vs(v, i)
                QAB.wait(v, k + 1)
                awv = qbuf['aw'][k % 2][:, :].rearrange("p (g d) -> p g d", g=16)
                i = v.tensor_tensor(awv, awv, qbuf['thrf'][:, :, :], ALU.is_gt)
                vs(v, i)
                i = v.tensor_tensor(qbuf['aw'][k % 2][:, :], qbuf['aw'][k % 2][:, :],
                                    qbuf['sg'][k % 2][:, :], ALU.mult)
                vs(v, i)
                QWE.inc(v.tensor_tensor(
                    weff_dst[:, :].rearrange("p (g d) -> p g d", g=16),
                    qbuf['aw'][k % 2][:, :].rearrange("p (g d) -> p g d", g=16),
                    qbuf['scf'][:, :, :], ALU.mult))
            blk.vector(qdve)

        def ph0_sync(sy):
            for dst, src in ((ident, IDN), (masks, MSK), (cosk, COSK),
                             (sink, SINK), (cosq, COSQ), (sinq, SINQ),
                             (gs2, GS2), (us2, US2), (ds2, DS2), (qg2, QG2)):
                CONST.inc(sy.dma_start(out=dst[...], in_=src[...]), 16)
        blk.sync(ph0_sync)

        def ph0_dve(v):
            v.memset(fence[0:1, 0:2], 0.0)
            CONST.wait(v)
            v.tensor_scalar(qgs[:, :], qg2[:, :], INV_SQRT_HD, None, ALU.mult)
        blk.vector(ph0_dve)

        def ph0_act(s):
            s.memzero(fence[0:1, 2:4])
        blk.scalar(ph0_act)

        def ph0_pool(g):
            g.memset(fence[0:1, 4:6], 0.0)
        blk.gpsimd(ph0_pool)
        barrier()
        if stop_after == 0:
            return nc

        sA = ExitStack()
        K_fm = sA.enter_context(nc.sbuf_tensor("K_fm", [128, NKV, SEQ], BF16))
        V_aug = sA.enter_context(nc.sbuf_tensor("V_aug", [128, NKV, NT_ALL, 132], BF16))

        qbuf = {}

        def alloc_quant_scratch(scope, tag):
            qbuf['aw'] = [scope.enter_context(
                nc.sbuf_tensor(f"qaw{tag}", [128, D], BF16))] * 2
            qbuf['sg'] = [scope.enter_context(
                nc.sbuf_tensor(f"qsg{tag}", [128, D], BF16))] * 2
            qbuf['wbf'] = [scope.enter_context(
                nc.sbuf_tensor(f"qwbf{tag}{i}", [128, D], BF16)) for i in range(2)]
            qbuf['thrf'] = scope.enter_context(
                nc.sbuf_tensor(f"qthrf{tag}", [128, 16, 128], F32))
            qbuf['scf'] = scope.enter_context(
                nc.sbuf_tensor(f"qscf{tag}", [128, 16, 128], BF16))

        sQ = ExitStack()
        alloc_quant_scratch(sQ, "a")

        def quant_src_dst(k):
            if k < 8:
                wt = 16 + k
                return (WQ[wt * 128:(wt + 1) * 128, :],
                        WQ_EFF[wt * 128:(wt + 1) * 128, :])
            if k < 24:
                wt = k - 8
                return (WQ[wt * 128:(wt + 1) * 128, :],
                        WQ_EFF[wt * 128:(wt + 1) * 128, :])
            wt = k - 24
            return (WP[wt * 128:(wt + 1) * 128, :],
                    WP_EFF[wt * 128:(wt + 1) * 128, :])

        def emit_quant_tile(k):
            srcap, dstap = quant_src_dst(k)

            def pload(g, k=k, srcap=srcap):
                QST.wait(g, 16 * max(0, k - 1))
                QBL.inc(g.dma_start(out=qbuf['wbf'][k % 2][:, :], in_=srcap), 16)
            blk.gpsimd(pload)
            emit_quant(k, qbuf['wbf'][k % 2], qbuf['wbf'][k % 2])

            def pstore(g, k=k, dstap=dstap):
                QWE.wait(g, k + 1)
                QST.inc(g.dma_start(out=dstap, in_=qbuf['wbf'][k % 2][:, :]), 16)
            blk.gpsimd(pstore)

        for k in range(8):
            emit_quant_tile(k)
        barrier()

        with ExitStack() as s1b:
            sb1 = lambda n, s, d: s1b.enter_context(nc.sbuf_tensor(n, s, d))
            ps1 = lambda n, s, d: s1b.enter_context(nc.psum_tensor(n, s, d))
            x_f = [sb1(f"x{i}", [128, D], F32) for i in range(3)]
            h1 = [sb1(f"h1{i}", [128, D], BF16) for i in range(2)]
            h1T = [sb1(f"h1T{i}", [128, 16, 128], BF16) for i in range(2)]
            wkvT = sb1("wkvT", [128, 16, 1024], BF16)
            kbf = [sb1(f"kbf{i}", [128, 512], BF16) for i in range(2)]
            krope = [sb1(f"krope{i}", [128, 512], BF16) for i in range(2)]
            ps_tr = [ps1(f"pstr{i}", [128, 1024], BF16) for i in range(2)]
            ps_ktr = [ps1(f"psktr{i}", [128, 1024], BF16) for i in range(2)]
            ps_k = [ps1(f"psk{i}", [128, 512], F32) for i in range(2)]
            ps_v = [ps1(f"psv{i}", [128, 512], F32) for i in range(2)]

            def dve_init(v):
                v.memset(V_aug[:, :, :, 128:129], 1.0)
            blk.vector(dve_init)

            def sync_wkvT(sy):
                QST.wait(sy, 16 * 8)
                for dc in range(16):
                    for half in range(2):
                        WTL.inc(sy.dma_start(
                            out=wkvT[:, dc, half * 512:(half + 1) * 512],
                            in_=WQ_EFF[2048 + half * 512:2048 + (half + 1) * 512,
                                       dc * 128:(dc + 1) * 128],
                            transpose=True), 16)
            blk.sync(sync_wkvT)
            wkvT_t = WTL.n

            def st_load(tt):
                def sy(sy_, tt=tt):
                    if tt >= 3:
                        H1D.wait(sy_, tt - 2)
                    XL.inc(sy_.dma_start(out=x_f[tt % 3][:, :],
                                         in_=X_ALL[tt * 128:(tt + 1) * 128, :]), 16)
                blk.sync(sy)

            def st_tr(tt):
                def pe1(t, tt=tt):
                    H1D.wait(t, tt + 1)
                    if tt >= 2:
                        MMD.wait(t, 2 * (tt - 1))
                    TRE.wait(t, 2 * tt)
                    for half in range(2):
                        last = None
                        for kk in range(8):
                            dc = half * 8 + kk
                            last = t.transpose(
                                ps_tr[half][:, kk * 128:(kk + 1) * 128],
                                h1[tt % 2][:, dc * 128:(dc + 1) * 128],
                                ident[:, :])
                        TRP.inc(last)
                blk.tensor(pe1)

            def st_trev(tt):
                def dve3(v, tt=tt):
                    for half in range(2):
                        TRP.wait(v, 2 * tt + half + 1)
                        dst = h1T[tt % 2][:, half * 8:(half + 1) * 8, :]
                        TRE.inc(v.tensor_copy(
                            dst,
                            ps_tr[half][:, :].rearrange("p (k f) -> p k f",
                                                        k=8)))
                blk.vector(dve3)

            def st_kvmm(tt):
                def pe2(t, tt=tt):
                    if tt == 0:
                        WTL.wait(t, wkvT_t)
                    TRE.wait(t, 2 * (tt + 1))
                    KEV.wait(t, max(0, tt - 1))
                    VEV.wait(t, max(0, tt - 1))
                    last_v = None
                    for dc in range(16):
                        t.matmul(ps_k[tt % 2][:, :], h1T[tt % 2][:, dc, :],
                                 wkvT[:, dc, 0:512],
                                 start=(dc == 0), stop=(dc == 15))
                        last_v = t.matmul(ps_v[tt % 2][:, :], h1T[tt % 2][:, dc, :],
                                          wkvT[:, dc, 512:1024],
                                          start=(dc == 0), stop=(dc == 15))
                    MMD.inc(last_v, 2)
                blk.tensor(pe2)

            def st_kevac(tt):
                def act3(s, tt=tt):
                    MMD.wait(s, 2 * (tt + 1))
                    KRO.wait(s, max(0, tt - 1))
                    KEV.inc(s.activation(kbf[tt % 2][:, :], ps_k[tt % 2][:, :],
                                         AF.Copy))
                blk.scalar(act3)

            def st_kstats(tt):
                def dve4(v, tt=tt):
                    KEV.wait(v, tt + 1)
                    KRS.wait(v, max(0, tt - 1))
                    kseg = kbf[tt % 2][:, :].rearrange("p (h d) -> p h d", h=4)
                    emit_kstats(v, tt % 2, kseg)
                    KTR.wait(v, max(0, tt - 1))
                    k1 = kseg[:, :, 0:64]
                    k2 = kseg[:, :, 64:128]
                    co = cosk[:, tt, :].unsqueeze(1).broadcast_to((128, 4, 64))
                    si = sink[:, tt, :].unsqueeze(1).broadcast_to((128, 4, 64))
                    kr = krope[tt % 2][:, :].rearrange("p (h d) -> p h d", h=4)
                    t1 = st4[:, :].rearrange("p (h d) -> p h d", h=4)[:, :, 0:64]
                    t2 = st4[:, :].rearrange("p (h d) -> p h d", h=4)[:, :, 64:128]
                    v.tensor_tensor(t1, k1, co, ALU.mult)
                    v.tensor_tensor(t2, k2, si, ALU.mult)
                    v.tensor_tensor(kr[:, :, 0:64], t1, t2, ALU.add)
                    v.tensor_tensor(t1, k2, co, ALU.mult)
                    v.tensor_tensor(t2, k1, si, ALU.mult)
                    KRO.inc(v.tensor_tensor(kr[:, :, 64:128], t1, t2,
                                            ALU.subtract))
                    MMD.wait(v, 2 * (tt + 1))
                    last = v.tensor_copy(
                        V_aug[:, :, tt, 0:128].rearrange("p h d -> p h d"),
                        ps_v[tt % 2][:, :].rearrange("p (h d) -> p h d", h=4))
                    VEV.inc(last)
                blk.vector(dve4)

            def st_krstd(tt):
                def act4(s, tt=tt):
                    p2 = tt % 2
                    KBN.wait(s, tt + 1)
                    i = s.activation(st2[:, 4 + 4 * p2:8 + 4 * p2],
                                     st1[:, 4 + 4 * p2:8 + 4 * p2], AF.Ln)
                    asy(s, i)
                    KRS.inc(s.activation(rstdk[:, tt * 4:(tt + 1) * 4],
                                         st2[:, 4 + 4 * p2:8 + 4 * p2],
                                         AF.Exp, scale=-0.5))
                blk.scalar(act4)

            def st_kscale(tt):
                def dvf(v, tt=tt):
                    KRS.wait(v, tt + 1)
                    last = None
                    for h in range(4):
                        last = v.tensor_scalar(
                            krope[tt % 2][:, h * 128:(h + 1) * 128],
                            krope[tt % 2][:, h * 128:(h + 1) * 128],
                            rstdk[:, tt * 4 + h:tt * 4 + h + 1], None, ALU.mult)
                    KSC.inc(last)
                blk.vector(dvf)

            def st_ktr(tt):
                def pe3(t, tt=tt):
                    KSC.wait(t, tt + 1)
                    KTE.wait(t, max(0, tt - 1))
                    last = None
                    for h in range(4):
                        last = t.transpose(ps_ktr[tt % 2][:, h * 128:(h + 1) * 128],
                                           krope[tt % 2][:, h * 128:(h + 1) * 128],
                                           ident[:, :])
                    KTR.inc(last)
                blk.tensor(pe3)

            def st_ktrev(tt):
                def dve5(v, tt=tt):
                    KTR.wait(v, tt + 1)
                    KTE.inc(v.tensor_copy(
                        V_aug[:, 0, 0, 0:1] if False else
                        K_fm[:, :, tt * 128:(tt + 1) * 128].rearrange(
                            "p h d -> p h d"),
                        ps_ktr[tt % 2][:, 0:512].rearrange("p (h d) -> p h d",
                                                           h=4)))
                blk.vector(dve5)

            def in_range(tt):
                return 0 <= tt < NT_ALL

            ln_stash = {}
            for m in range(NT_ALL + 8):
                if m == 0:
                    st_load(0)
                    st_load(1)
                    st_load(2)
                elif in_range(m + 2):
                    st_load(m + 2)
                if in_range(m):
                    lns = make_ln(m, m % 4, x_f[m % 3], h1[m % 2],
                                  extra_dve_wait=(
                                      lambda v, tt=m: XL.wait(v, 16 * (tt + 1))),
                                  extra_act_wait=(
                                      lambda s, tt=m: TRP.wait(s, 2 * (tt - 1))
                                      if tt >= 2 else None))
                else:
                    lns = None
                if in_range(m - 4):
                    st_trev(m - 4)
                if in_range(m - 7):
                    st_ktrev(m - 7)
                if lns:
                    ln_stash[m % 4] = lns
                    blk.vector(lns[0])
                if in_range(m - 1):
                    blk.scalar(ln_stash[(m - 1) % 4][1])
                if in_range(m - 2):
                    blk.vector(ln_stash[(m - 2) % 4][2])
                    blk.scalar(ln_stash[(m - 2) % 4][3])
                if in_range(m - 3):
                    st_tr(m - 3)
                if in_range(m - 4):
                    st_kvmm(m - 4)
                if in_range(m - 6):
                    st_krstd(m - 6)
                if in_range(m - 5):
                    st_kevac(m - 5)
                    st_kstats(m - 5)
                if in_range(m - 6):
                    st_kscale(m - 6)
                    st_ktr(m - 6)
                if m >= 7 and (m - 7) % 2 == 0 and (m - 7) // 2 < 16:
                    emit_quant_tile(8 + (m - 7) // 2)
            barrier()
            if debug:
                def dbgw(g):
                    OST.inc(g.dma_start(out=DBG_W[...], in_=wkvT[...]), 16)
                barrier()
                blk.gpsimd(dbgw)
            barrier()
        sQ.close()
        Q4 = sA.enter_context(
            nc.sbuf_tensor("Q4", [128, NKV * NT_OWN * 4 * 128], BF16))
        if stop_after == 2:
            sA.close()
            return nc

        with ExitStack() as s1c:
            sb1 = lambda n, s, d: s1c.enter_context(nc.sbuf_tensor(n, s, d))
            ps1 = lambda n, s, d: s1c.enter_context(nc.psum_tensor(n, s, d))
            x_f = [sb1(f"xo{i}", [128, D], F32) for i in range(3)]
            h1 = [sb1(f"h1o{i}", [128, D], BF16) for i in range(2)]
            h1ownT = sb1("h1ownT", [128, NT_OWN, 16, 128], BF16)
            wqT = sb1("wqT", [128, 16, 512], BF16)
            qbf = [sb1(f"qbf{i}", [128, 512], BF16) for i in range(2)]
            qrope = [sb1(f"qrope{i}", [128, 512], BF16) for i in range(3)]
            qfin = [sb1(f"qfin{i}", [128, 512], BF16) for i in range(2)]
            ps_tr = [ps1(f"pstrc{i}", [128, 1024], BF16) for i in range(2)]
            ps_qtr = [ps1(f"psqtr{i}", [128, 1024], BF16) for i in range(2)]
            ps_q = [ps1(f"psq{i}", [128, 512], F32) for i in range(2)]

            trp0, tre0 = TRP.n, TRE.n
            ktr0, kte0 = KTR.n, KTE.n
            mmd0, kev0, kbn0, krs0, kro0 = MMD.n, KEV.n, KBN.n, KRS.n, KRO.n
            h1d0, xl0, qed0 = H1D.n, XL.n, QED.n

            def st_load(tt):
                def sy(sy_, tt=tt):
                    if tt >= 3:
                        H1D.wait(sy_, h1d0 + tt - 2)
                    XL.inc(sy_.dma_start(out=x_f[tt % 3][:, :],
                                         in_=X_OWN[tt * 128:(tt + 1) * 128, :]), 16)
                blk.sync(sy)

            def st_tr(tt):
                def pe1(t, tt=tt):
                    H1D.wait(t, h1d0 + tt + 1)
                    TRE.wait(t, tre0 + 2 * tt)
                    for half in range(2):
                        last = None
                        for kk in range(8):
                            dc = half * 8 + kk
                            last = t.transpose(
                                ps_tr[half][:, kk * 128:(kk + 1) * 128],
                                h1[tt % 2][:, dc * 128:(dc + 1) * 128],
                                ident[:, :])
                        TRP.inc(last)
                blk.tensor(pe1)

            def st_trev(tt):
                def dve3(v, tt=tt):
                    for half in range(2):
                        TRP.wait(v, trp0 + 2 * tt + half + 1)
                        dst = h1ownT[:, tt, half * 8:(half + 1) * 8, :]
                        TRE.inc(v.tensor_copy(
                            dst,
                            ps_tr[half][:, :].rearrange("p (k f) -> p k f", k=8)))
                blk.vector(dve3)

            ln_stash = {}
            for m in range(NT_OWN + 4):
                if m == 0:
                    st_load(0)
                if m + 1 < NT_OWN:
                    st_load(m + 1)
                if m < NT_OWN:
                    lns = make_ln(h1d0 + m, m % 4, x_f[m % 3], h1[m % 2],
                                  extra_dve_wait=(
                                      lambda v, tt=m: XL.wait(
                                          v, xl0 + 16 * (tt + 1))),
                                  extra_act_wait=(
                                      lambda s, tt=m: TRP.wait(
                                          s, trp0 + 2 * (tt - 1))
                                      if tt >= 2 else None))
                    ln_stash[m % 4] = lns
                    blk.vector(lns[0])
                if 0 <= m - 1 < NT_OWN:
                    blk.scalar(ln_stash[(m - 1) % 4][1])
                if 0 <= m - 2 < NT_OWN:
                    blk.vector(ln_stash[(m - 2) % 4][2])
                    blk.scalar(ln_stash[(m - 2) % 4][3])
                if 0 <= m - 3 < NT_OWN:
                    st_tr(m - 3)
                if 0 <= m - 4 < NT_OWN:
                    st_trev(m - 4)

            wq_t = {}

            def st_wqload(ob):
                def syw(sy_, ob=ob):
                    QST.wait(sy_, 16 * 24)
                    if ob >= 1:
                        MMD.wait(sy_, mmd0 + 8 * ob)
                    for dc in range(16):
                        WTL.inc(sy_.dma_start(
                            out=wqT[:, dc, :],
                            in_=WQ_EFF[ob * 512:(ob + 1) * 512,
                                       dc * 128:(dc + 1) * 128],
                            transpose=True), 16)
                blk.sync(syw)
                wq_t[ob] = WTL.n

            def st_qmm(n):
                ob, tt = n // 8, n % 8

                def pe2(t, ob=ob, tt=tt, n=n):
                    if tt == 0:
                        WTL.wait(t, wq_t[ob])
                    TRE.wait(t, tre0 + 2 * (tt + 1))
                    KEV.wait(t, kev0 + max(0, n - 1))
                    last = None
                    for dc in range(16):
                        last = t.matmul(ps_q[n % 2][:, :], h1ownT[:, tt, dc, :],
                                        wqT[:, dc, :],
                                        start=(dc == 0), stop=(dc == 15))
                    MMD.inc(last)
                blk.tensor(pe2)

            def st_qevac(n):
                def act3(s, n=n):
                    MMD.wait(s, mmd0 + n + 1)
                    KRO.wait(s, kro0 + max(0, n - 1))
                    KEV.inc(s.activation(qbf[n % 2][:, :], ps_q[n % 2][:, :],
                                         AF.Copy))
                blk.scalar(act3)

            def st_qstats(n):
                ob, tt = n // 8, n % 8

                def dve4(v, ob=ob, tt=tt, n=n):
                    KEV.wait(v, kev0 + n + 1)
                    KRS.wait(v, krs0 + max(0, n - 1))
                    qseg = qbf[n % 2][:, :].rearrange("p (h d) -> p h d", h=4)
                    emit_kstats(v, n % 2, qseg)
                    KTR.wait(v, ktr0 + max(0, n - 2))
                    q1 = qseg[:, :, 0:64]
                    q2 = qseg[:, :, 64:128]
                    co = cosq[:, tt, :].unsqueeze(1).broadcast_to((128, 4, 64))
                    si = sinq[:, tt, :].unsqueeze(1).broadcast_to((128, 4, 64))
                    qr = qrope[n % 3][:, :].rearrange("p (h d) -> p h d", h=4)
                    t1 = st4[:, :].rearrange("p (h d) -> p h d", h=4)[:, :, 0:64]
                    t2 = st4[:, :].rearrange("p (h d) -> p h d", h=4)[:, :, 64:128]
                    v.tensor_tensor(t1, q1, co, ALU.mult)
                    v.tensor_tensor(t2, q2, si, ALU.mult)
                    v.tensor_tensor(qr[:, :, 0:64], t1, t2, ALU.add)
                    v.tensor_tensor(t1, q2, co, ALU.mult)
                    v.tensor_tensor(t2, q1, si, ALU.mult)
                    KRO.inc(v.tensor_tensor(qr[:, :, 64:128], t1, t2,
                                            ALU.subtract))
                blk.vector(dve4)

            def st_qrstd(n):
                def act4(s, n=n):
                    p2 = n % 2
                    p3 = n % 3
                    KBN.wait(s, kbn0 + n + 1)
                    i = s.activation(st2[:, 4 + 4 * p2:8 + 4 * p2],
                                     st1[:, 4 + 4 * p2:8 + 4 * p2], AF.Ln)
                    asy(s, i)
                    KRS.inc(s.activation(st3[:, 4 * p3:4 * p3 + 4],
                                         st2[:, 4 + 4 * p2:8 + 4 * p2],
                                         AF.Exp, scale=-0.5))
                blk.scalar(act4)

            def st_qscale(n):
                ob = n // 8

                def dve5(v, ob=ob, n=n):
                    p3 = n % 3
                    KRS.wait(v, krs0 + n + 1)
                    KTR.wait(v, ktr0 + max(0, n - 1))
                    i = v.tensor_tensor(st3[:, 12:16],
                                        st3[:, 4 * p3:4 * p3 + 4],
                                        qgs[:, ob * 4:(ob + 1) * 4], ALU.mult)
                    vs(v, i)
                    last = None
                    for g in range(4):
                        last = v.tensor_scalar(qfin[n % 2][:, g * 128:(g + 1) * 128],
                                               qrope[n % 3][:, g * 128:(g + 1) * 128],
                                               st3[:, 12 + g:13 + g],
                                               None, ALU.mult)
                    QED.inc(last)
                blk.vector(dve5)

            def st_qtr(n):
                def pe3(t, n=n):
                    QED.wait(t, qed0 + n + 1)
                    KTE.wait(t, kte0 + max(0, n - 1))
                    last = None
                    for g in range(4):
                        last = t.transpose(ps_qtr[n % 2][:, g * 128:(g + 1) * 128],
                                           qfin[n % 2][:, g * 128:(g + 1) * 128],
                                           ident[:, :])
                    KTR.inc(last)
                blk.tensor(pe3)

            def st_qtrev(n):
                ob, tt = n // 8, n % 8

                def dve6(v, ob=ob, tt=tt, n=n):
                    KTR.wait(v, ktr0 + n + 1)
                    off = ((ob * NT_OWN + tt) * 4) * 128
                    KTE.inc(v.tensor_copy(Q4[:, off:off + 512],
                                          ps_qtr[n % 2][:, 0:512]))
                blk.vector(dve6)

            NQ = 32
            for m in range(NQ + 6):
                if m % 8 == 0 and m // 8 < 4:
                    st_wqload(m // 8)
                if 0 <= m - 6 < NQ:
                    st_qtrev(m - 6)
                if m < NQ:
                    st_qmm(m)
                if 0 <= m - 1 < NQ:
                    st_qevac(m - 1)
                if 0 <= m - 2 < NQ:
                    st_qstats(m - 2)
                if 0 <= m - 3 < NQ:
                    st_qrstd(m - 3)
                if 0 <= m - 4 < NQ:
                    st_qscale(m - 4)
                if 0 <= m - 5 < NQ:
                    st_qtr(m - 5)
            if debug:
                def dbgh(g):
                    OST.inc(g.dma_start(out=DBG_H[...], in_=h1ownT[...]), 16)
                barrier()
                blk.gpsimd(dbgh)
            barrier()
        if debug:
            def dbg1(g):
                OST.inc(g.dma_start(out=DBG_K[...], in_=K_fm[...]), 16)
                OST.inc(g.dma_start(out=DBG_V[...], in_=V_aug[...]), 16)
                OST.inc(g.dma_start(out=DBG_Q[...], in_=Q4[...]), 16)
                OST.inc(g.dma_start(out=DBG_R[...], in_=rstdk[...]), 16)
            blk.gpsimd(dbg1)
            barrier()
        if stop_after == 3:
            sA.close()
            return nc

        sB = ExitStack()
        y_nB = sB.enter_context(nc.sbuf_tensor("y_nB", [128, NT_OWN, D], BF16,
                                               side="right"))

        with ExitStack() as s2:
            sb1 = lambda n, s, d: s2.enter_context(nc.sbuf_tensor(n, s, d))
            ps1 = lambda n, s, d: s2.enter_context(nc.psum_tensor(n, s, d))
            alloc_quant_scratch(s2, "b")
            P_buf = sb1("P_buf", [128, 8 * 512], BF16)
            y_tm = [sb1(f"ytm{i}", [128, D], F32) for i in range(2)]
            linv = sb1("linv", [128, 4], F32)
            ps_s = ps1("ps_s", [128, 4 * 512], F32)
            ps_o = [ps1(f"pso{i}", [128, 512], F32) for i in range(4)]

            qkd0, expd0, mskd0, avd0, oev0 = QKD.n, EXPD.n, MSKD.n, AVD.n, OEV.n
            yst0, yrs0, ynd0 = YST.n, YRS.n, YND.n
            trp0, tre0 = TRP.n, TRE.n
            qbases = (QBL.n, QTH.n, QNG.n, QPS.n, QWE.n, QST.n)
            qwe0, qst0, qps0 = QWE.n, QST.n, QPS.n

            nj_of = [4 * tau + 4 for tau in range(NT_OWN)]
            gc_of = [0] * NT_OWN
            dg_of = [0] * NT_OWN
            for tau in range(1, NT_OWN):
                gc_of[tau] = gc_of[tau - 1] + 4 * nj_of[tau - 1]
                dg_of[tau] = dg_of[tau - 1] + 16

            pend_rms = [None]
            pend_ynd = [None]

            for tau in range(ph2_taus):
                nj = nj_of[tau]
                gc_tau = gc_of[tau]
                dg_tau = dg_of[tau]

                LA = 4

                def emit_qk(t, tau, h, j, c):
                    EXPD.wait(t, expd0 + max(0, c // 2 - 1))
                    qoff = (h * NT_OWN + tau) * 4 * 128
                    so = (c % 4) * 512
                    QKD.inc(t.matmul(ps_s[:, so:so + 512],
                                     K_fm[:, h, j * 128:(j + 1) * 128],
                                     Q4[:, qoff:qoff + 512],
                                     start=True, stop=True))

                def emit_av(t, tau, h, j, c, nj, dcnt, s_i):
                    if j == 0 and s_i > 0:
                        OEV.wait(t, oev0 + s_i)
                    if j >= nj - 4:
                        MSKD.wait(t, mskd0 + dcnt)
                    else:
                        EXPD.wait(t, expd0 + c // 2 + 1)
                    last = None
                    po = (c % 8) * 512
                    for g in range(4):
                        last = t.matmul(ps_o[g][:, 0:129],
                                        P_buf[:, po + g * 128:po + (g + 1) * 128],
                                        V_aug[:, h, j, 0:129],
                                        start=(j == 0), stop=(j == nj - 1))
                    AVD.inc(last)

                def pe(t, tau=tau, nj=nj, gc_tau=gc_tau, dg_tau=dg_tau):
                    if tau == 0:
                        KTE.wait(t)
                        VEV.wait(t)
                    items = []
                    dcnt = dg_tau
                    for h in range(4):
                        for j in range(nj):
                            c = gc_tau + h * nj + j
                            if j >= nj - 4:
                                dcnt += 1
                            items.append((h, j, c, dcnt, 4 * tau + h))
                    for k, (h, j, c, dc_, s_i) in enumerate(items):
                        emit_qk(t, tau, h, j, c)
                        if k >= LA:
                            h2_, j2_, c2_, d2_, s2_ = items[k - LA]
                            emit_av(t, tau, h2_, j2_, c2_, nj, d2_, s2_)
                    for k in range(len(items) - LA, len(items)):
                        h2_, j2_, c2_, d2_, s2_ = items[k]
                        emit_av(t, tau, h2_, j2_, c2_, nj, d2_, s2_)
                blk.tensor(pe)

                def act(s, tau=tau, nj=nj, gc_tau=gc_tau, prev=pend_rms[0]):
                    for idx, i in enumerate(range(gc_tau // 2,
                                                  (gc_tau + 4 * nj) // 2)):
                        QKD.wait(s, qkd0 + 2 * i + 2)
                        AVD.wait(s, avd0 + max(0, 2 * i - 6))
                        po = (2 * i % 8) * 512
                        so = (2 * i % 4) * 512
                        EXPD.inc(s.activation(P_buf[:, po:po + 1024],
                                              ps_s[:, so:so + 1024], AF.Exp))
                        if idx == 3 and prev is not None:
                            prev(s)
                blk.scalar(act)

                def mk_rms(tau):
                    p = tau % 2

                    def rms(s, tau=tau, p=p):
                        YST.wait(s, yst0 + tau + 1)
                        i = s.activation(st1[:, 2 * p + 1:2 * p + 2],
                                         st1[:, 2 * p:2 * p + 1], AF.Ln)
                        asy(s, i)
                        YRS.inc(s.activation(st2[:, p:p + 1],
                                             st1[:, 2 * p + 1:2 * p + 2],
                                             AF.Exp, scale=-0.5))
                    return rms
                pend_rms[0] = mk_rms(tau)

                def dve(v, tau=tau, nj=nj, gc_tau=gc_tau, dg_tau=dg_tau,
                        prev=pend_ynd[0]):
                    c = gc_tau
                    dcnt = dg_tau
                    for h in range(4):
                        for j in range(nj):
                            if j >= nj - 4:
                                EXPD.wait(v, expd0 + c // 2 + 1)
                                dcnt += 1
                                po = (c % 8) * 512
                                MSKD.inc(v.tensor_tensor(
                                    P_buf[:, po:po + 512], P_buf[:, po:po + 512],
                                    masks[:, j - (nj - 4), :], ALU.mult))
                            c += 1
                        AVD.wait(v, avd0 + c)
                        i = None
                        for g in range(4):
                            i = v.reciprocal(linv[:, g:g + 1], ps_o[g][:, 128:129])
                        vs(v, i)
                        last = None
                        for g in range(4):
                            hh = 4 * h + g
                            last = v.tensor_scalar(
                                y_tm[tau % 2][:, hh * 128:(hh + 1) * 128],
                                ps_o[g][:, 0:128], linv[:, g:g + 1], None, ALU.mult)
                        OEV.inc(last)
                        if h == 0 and prev is not None:
                            prev(v)
                    p = tau % 2
                    v.wait_ge(OEV.h, OEV.n)
                    for g4 in range(4):
                        i = v.bn_stats(bn6[:, g4, :],
                                       y_tm[tau % 2][:, g4 * 512:(g4 + 1) * 512])
                    vs(v, i)
                    means = bn6[:, 0:4, 1::3]
                    ctvs = bn6[:, 0:4, 2::3]
                    st8v = st8[:, 0:8].rearrange("p (g t) -> p g t", g=4)
                    v.tensor_reduce(stL[:, 1:2], ctvs, AX.XY, ALU.add)
                    i = v.tensor_tensor(st8v, means, means, ALU.mult)
                    vs(v, i)
                    i = v.tensor_reduce(stL[:, 2:3], st8v, AX.XY, ALU.add)
                    vs(v, i)
                    i = v.tensor_scalar(stL[:, 3:4], stL[:, 1:2],
                                        1.0 / 2048, RMS_EPS, ALU.mult, ALU.add)
                    vs(v, i)
                    YST.inc(v.scalar_tensor_tensor(st1[:, 2 * p:2 * p + 1],
                                                   stL[:, 2:3],
                                                   0.125, stL[:, 3:4],
                                                   ALU.mult, ALU.add))
                blk.vector(dve)

                def mk_ynd(tau):
                    p = tau % 2

                    def ynd(v, tau=tau, p=p):
                        YRS.wait(v, yrs0 + tau + 1)
                        YND.inc(v.tensor_scalar(y_nB[:, tau, :],
                                                y_tm[tau % 2][:, :],
                                                st2[:, p:p + 1], None, ALU.mult))
                    return ynd
                pend_ynd[0] = mk_ynd(tau)

                if ph2_quant:
                    for k2 in range(2):
                        emit_quant_tile(24 + tau * 2 + k2)

            if pend_rms[0] is not None:
                blk.scalar(pend_rms[0])
            if pend_ynd[0] is not None:
                blk.vector(pend_ynd[0])
            barrier()
        if debug:
            def dbg2(g):
                OST.inc(g.dma_start(out=DBG_Y[...], in_=y_nB[...]), 16)
            blk.gpsimd(dbg2)
            barrier()
        if stop_after == 4 or ph2_taus < NT_OWN:
            sA.close()
            sB.close()
            return nc

        sA.close()

        sC = ExitStack()
        h2T = sC.enter_context(nc.sbuf_tensor("h2T", [128, 16, OWN], BF16))

        with ExitStack() as s3:
            sb1 = lambda n, s, d: s3.enter_context(nc.sbuf_tensor(n, s, d))
            ps1 = lambda n, s, d: s3.enter_context(nc.psum_tensor(n, s, d))
            wpT = sb1("wpT", [128, 16, D], BF16)
            x_f = [sb1(f"x3{i}", [128, D], F32) for i in range(2)]
            x2 = [sb1(f"x2{i}", [128, D], F32) for i in range(2)]
            h2 = [sb1(f"h2b{i}", [128, D], BF16) for i in range(2)]
            yT = [sb1(f"yT{i}", [128, D], BF16) for i in range(2)]
            ps_p = [ps1(f"psp{i}", [128, 512], F32) for i in range(4)]
            ps_tr = [ps1(f"pstr3{i}", [128, 1024], BF16) for i in range(2)]
            ps_try = ps1("ps_try", [128, D], BF16)

            def syw(sy_):
                QST.wait(sy_)
                for dc in range(16):
                    WTL.inc(sy_.dma_start(out=wpT[:, dc, :],
                                          in_=WP_EFF[0:D, dc * 128:(dc + 1) * 128],
                                          transpose=True), 16)
            blk.sync(syw)
            wpT_t = WTL.n

            xl0, mmd0, pje0, h2d0 = XL.n, MMD.n, PJE.n, H1D.n
            trp0, tre0, ost0 = TRP.n, TRE.n, OST.n
            ytp0, yte0 = YTP.n, YTE.n

            def emit_ytr3(t, tt):
                if tt >= 1:
                    YTE.wait(t, yte0 + tt)
                last = None
                for dc in range(16):
                    last = t.transpose(ps_try[:, dc * 128:(dc + 1) * 128],
                                       y_nB[:, tt, dc * 128:(dc + 1) * 128],
                                       ident[:, :])
                YTP.inc(last)

            def pe_ytr0(t):
                emit_ytr3(t, 0)
            blk.tensor(pe_ytr0)

            for tt in range(NT_OWN):
                bi = tt % 2

                def sy(sy_, tt=tt, bi=bi):
                    if tt >= 2:
                        PJE.wait(sy_, pje0 + 4 * (tt - 1))
                    XL.inc(sy_.dma_start(out=x_f[bi][:, :],
                                         in_=X_OWN[tt * 128:(tt + 1) * 128, :]), 16)
                blk.sync(sy)

                def pe(t, tt=tt, bi=bi):
                    if tt + 1 < NT_OWN:
                        emit_ytr3(t, tt + 1)
                    if tt == 0:
                        WTL.wait(t, wpT_t)
                    if tt >= 1:
                        PJE.wait(t, pje0 + 4 * tt)
                    YTE.wait(t, yte0 + tt + 1)
                    last = None
                    for dc in range(16):
                        for ob in range(4):
                            last = t.matmul(ps_p[ob][:, :],
                                            yT[bi][:, dc * 128:(dc + 1) * 128],
                                            wpT[:, dc, ob * 512:(ob + 1) * 512],
                                            start=(dc == 0), stop=(dc == 15))
                    MMD.inc(last)
                blk.tensor(pe)

                def act_yev(s, tt=tt, bi=bi):
                    YTP.wait(s, ytp0 + tt + 1)
                    if tt >= 2:
                        MMD.wait(s, mmd0 + tt - 1)
                    YTE.inc(s.activation(yT[bi][:, :], ps_try[:, :], AF.Identity))
                blk.scalar(act_yev)

                def dve1(v, tt=tt, bi=bi):
                    MMD.wait(v, mmd0 + tt + 1)
                    XL.wait(v, xl0 + 16 * (tt + 1))
                    if tt >= 2:
                        OST.wait(v, ost0 + 16 * (tt - 1))
                    for ob in range(4):
                        PJE.inc(v.tensor_tensor(x2[bi][:, ob * 512:(ob + 1) * 512],
                                                ps_p[ob][:, :],
                                                x_f[bi][:, ob * 512:(ob + 1) * 512],
                                                ALU.add))
                blk.vector(dve1)

                def h2guard(s, tt=tt):
                    if tt >= 2:
                        TRP.wait(s, trp0 + 4 * (tt - 1))

                def x2w(v):
                    v.wait_ge(PJE.h, PJE.n)
                emit_ln(h2d0 + tt, tt % 4, x2[bi], h2[bi], extra_dve_wait=x2w,
                        extra_act_wait=h2guard)

                def pst(g, tt=tt, bi=bi):
                    H1D.wait(g, h2d0 + tt + 1)
                    OST.inc(g.dma_start(out=OUT[tt * 128:(tt + 1) * 128, :],
                                        in_=x2[bi][:, :]), 16)
                blk.gpsimd(pst)

                emit_tr16(h2[bi],
                          lambda grp, tt=tt: h2T[:, grp * 4:(grp + 1) * 4,
                                                 tt * 128:(tt + 1) * 128],
                          ps_tr, h2d0 + tt, trp0, tre0, 4 * tt)
            barrier()
        if debug:
            def dbg3(g):
                OST.inc(g.dma_start(out=DBG_H2[...], in_=h2T[...]), 16)
            blk.gpsimd(dbg3)
            barrier()
        if stop_after == 5:
            sB.close()
            sC.close()
            return nc

        sB.close()

        with ExitStack() as s4:
            sb0 = lambda n, s, d: s4.enter_context(nc.sbuf_tensor(n, s, d))
            hdn = sb0("hdn", [128, NIC, 1024], BF16)
            dw = [sb0(f"dw{i}", [128, NIC, 128], BF16) for i in range(3)]
            ddb = [sb0(f"ddb{i}", [128, 1024], BF16) for i in range(2)]
            ddf = sb0("ddf", [128, 2, 1024], F32)

            hdn0 = HDN.n
            dwl0, mmdd0 = DWL.n, MMD.n
            dw_t = {}

            DWPRE = {28: (0, 0), 30: (0, 1), 32: (0, 2), 34: (0, 3),
                     36: (1, 0), 38: (1, 1), 40: (1, 2), 42: (1, 3)}

            def dw_chunk(g, oc, c):
                i0, i1 = c * 11, (c + 1) * 11
                DWL.inc(g.dma_start(
                    out=dw[oc % 3][:, i0:i1, :],
                    in_=DWT[oc].rearrange("p (c f) -> p c f", c=NIC)[:, i0:i1, :]),
                    16)
                if c == 3:
                    dw_t[oc] = DWL.n

            with ExitStack() as s4a:
                sb1 = lambda n, s, d: s4a.enter_context(nc.sbuf_tensor(n, s, d))
                ps1 = lambda n, s, d: s4a.enter_context(nc.psum_tensor(n, s, d))
                gw = [sb1(f"gw{i}", [128, 16, 128], BF16) for i in range(2)]
                uw = [sb1(f"uw{i}", [128, 16, 128], BF16) for i in range(2)]
                sgb = [sb1(f"sgb{i}", [128, 1024], BF16) for i in range(2)]
                ubb = [sb1(f"ubb{i}", [128, 1024], BF16) for i in range(2)]
                pool = [ps1(f"pp{i}", [128, 512], F32) for i in range(6)]

                fwl0, fwu0 = FWL.n, FWU.n
                mmg0, gue0, ube0 = MMG.n, GUE.n, UBE.n
                bank_owner = [None] * 6

                for ic in range(NIC):
                    def pload(g, ic=ic):
                        if ic >= 2:
                            MMG.wait(g, mmg0 + 4 * (ic - 2) + 2)
                        FWL.inc(g.dma_start(
                            out=gw[ic % 2][...],
                            in_=GWT[ic].rearrange("p (c f) -> p c f", c=16)), 16)
                        if ic >= 2:
                            MMG.wait(g, mmg0 + 4 * (ic - 2) + 4)
                        FWU.inc(g.dma_start(
                            out=uw[ic % 2][...],
                            in_=UWT[ic].rearrange("p (c f) -> p c f", c=16)), 16)
                        if ic in DWPRE:
                            dw_chunk(g, *DWPRE[ic])
                    blk.gpsimd(pload)

                    owner_snap = [bank_owner[(4 * ic + qi) % 6] for qi in range(4)]

                    def pe(t, ic=ic, owner_snap=owner_snap):
                        FWL.wait(t, fwl0 + 16 * (ic + 1))
                        for qi in range(4):
                            if qi == 2:
                                FWU.wait(t, fwu0 + 16 * (ic + 1))
                            b = (4 * ic + qi) % 6
                            own = owner_snap[qi]
                            if own is not None:
                                own[0].wait(t, own[1])
                            w = (gw if qi < 2 else uw)[ic % 2]
                            half = qi % 2
                            last = None
                            for dc in range(16):
                                last = t.matmul(
                                    pool[b][:, :], w[:, dc, :],
                                    h2T[:, dc, half * 512:(half + 1) * 512],
                                    start=(dc == 0), stop=(dc == 15))
                            MMG.inc(last)
                    blk.tensor(pe)

                    def act(s, ic=ic):
                        if ic >= 2:
                            HDN.wait(s, hdn0 + ic - 1)
                        for half in range(2):
                            MMG.wait(s, mmg0 + 4 * ic + 1 + half)
                            GUE.inc(s.activation(
                                sgb[ic % 2][:, half * 512:(half + 1) * 512],
                                pool[(4 * ic + half) % 6][:, :], AF.Silu,
                                scale=gs2[:, ic:ic + 1]))
                    blk.scalar(act)
                    bank_owner[(4 * ic) % 6] = (GUE, gue0 + 2 * ic + 1)
                    bank_owner[(4 * ic + 1) % 6] = (GUE, gue0 + 2 * ic + 2)

                    def dve(v, ic=ic):
                        for half in range(2):
                            MMG.wait(v, mmg0 + 4 * ic + 3 + half)
                            UBE.inc(v.tensor_scalar(
                                ubb[ic % 2][:, half * 512:(half + 1) * 512],
                                pool[(4 * ic + 2 + half) % 6][:, :],
                                us2[:, ic:ic + 1], None, ALU.mult))
                        UBE.wait(v, ube0 + 2 * ic + 2)
                        GUE.wait(v, gue0 + 2 * ic + 2)
                        HDN.inc(v.tensor_tensor(hdn[:, ic, :], sgb[ic % 2][:, :],
                                                ubb[ic % 2][:, :], ALU.mult))
                    blk.vector(dve)
                    bank_owner[(4 * ic + 2) % 6] = (UBE, ube0 + 2 * ic + 1)
                    bank_owner[(4 * ic + 3) % 6] = (UBE, ube0 + 2 * ic + 2)

            with ExitStack() as s4d:
                ps1 = lambda n, s, d: s4d.enter_context(nc.psum_tensor(n, s, d))
                ps_d = [ps1(f"psd{i}", [128, 1024], F32) for i in range(2)]
                ps_tro = [ps1(f"pstro{i}", [128, 1024], BF16) for i in range(2)]

                dse0, dte0, ktr0 = DSE.n, DTE.n, KTR.n
                oacc0 = OACC.n

                def dwload(oc):
                    def gld(g, oc=oc):
                        if oc >= 3:
                            MMD.wait(g, mmdd0 + oc - 2)
                        DWL.inc(g.dma_start(
                            out=dw[oc % 3][...],
                            in_=DWT[oc].rearrange("p (c f) -> p c f", c=NIC)), 16)
                        dw_t[oc] = DWL.n
                    blk.gpsimd(gld)

                def mmdn(oc):
                    def pe(t, oc=oc):
                        DWL.wait(t, dw_t[oc])
                        if oc == 0:
                            HDN.wait(t, hdn0 + NIC)
                        if oc >= 2:
                            DSE.wait(t, dse0 + oc - 1)
                        last = None
                        for icc in range(NIC):
                            for half in range(2):
                                last = t.matmul(
                                    ps_d[oc % 2][:, half * 512:(half + 1) * 512],
                                    dw[oc % 3][:, icc, :],
                                    hdn[:, icc, half * 512:(half + 1) * 512],
                                    start=(icc == 0), stop=(icc == NIC - 1))
                        MMD.inc(last)
                    blk.tensor(pe)

                def ddbev(oc):
                    def dve1(v, oc=oc):
                        MMD.wait(v, mmdd0 + oc + 1)
                        if oc >= 2:
                            KTR.wait(v, ktr0 + oc - 1)
                        DSE.inc(v.tensor_scalar(ddb[oc % 2][:, :],
                                                ps_d[oc % 2][:, :],
                                                ds2[:, oc:oc + 1], None, ALU.mult))
                    blk.vector(dve1)

                OUT_T = OUT.rearrange("(t p) c -> p t c", p=128)

                def trout(oc):
                    def pe2(t, oc=oc):
                        DSE.wait(t, dse0 + oc + 1)
                        if oc >= 2:
                            DTE.wait(t, dte0 + oc - 1)
                        last = None
                        for tt in range(NT_OWN):
                            last = t.transpose(
                                ps_tro[oc % 2][:, tt * 128:(tt + 1) * 128],
                                ddb[oc % 2][:, tt * 128:(tt + 1) * 128],
                                ident[:, :])
                        KTR.inc(last)
                    blk.tensor(pe2)

                    def dve2(v, oc=oc):
                        KTR.wait(v, ktr0 + oc + 1)
                        if oc >= 2:
                            OACC.wait(v, oacc0 + 16 * (oc - 1))
                        DTE.inc(v.tensor_copy(ddf[:, oc % 2, :],
                                              ps_tro[oc % 2][:, :]))
                    blk.vector(dve2)

                def pacc_emit(oc):
                    def pacc(g, oc=oc):
                        DTE.wait(g, dte0 + oc + 1)
                        OST.wait(g, 16 * 8)
                        OACC.inc(g.dma_start(
                            out=OUT_T[:, :, oc * 128:(oc + 1) * 128],
                            in_=ddf[:, oc % 2, :].rearrange("p (t c) -> p t c",
                                                            t=NT_OWN),
                            accum_op=ALU.add), 16)
                    blk.gpsimd(pacc)

                mmdn(0)
                mmdn(1)
                for oc in range(16):
                    if oc + 2 < 16:
                        dwload(oc + 2)
                    ddbev(oc)
                    trout(oc)
                    if oc + 2 < 16:
                        mmdn(oc + 2)
                    if oc >= 1:
                        pacc_emit(oc - 1)
                pacc_emit(15)

            def fin(g):
                OACC.wait(g)
                OST.wait(g)
            blk.gpsimd(fin)

        sC.close()

    return nc


def _host_prepare(L0_attn_x, L0_attn_w_qkv, L0_attn_w_proj, L0_attn_q_gain,
                  L0_ffn_gate_w, L0_ffn_up_w, L0_ffn_down_w,
                  L0_ffn_gate_s, L0_ffn_up_s, L0_ffn_down_s):
    f32 = np.float32
    x = np.asarray(L0_attn_x, f32)
    wq = np.ascontiguousarray(np.asarray(L0_attn_w_qkv, f32))
    wp = np.ascontiguousarray(np.asarray(L0_attn_w_proj, f32))
    qg = np.asarray(L0_attn_q_gain, f32)
    gwm = np.asarray(L0_ffn_gate_w, f32)
    uwm = np.asarray(L0_ffn_up_w, f32)
    dwm = np.asarray(L0_ffn_down_w, f32)
    gs = np.asarray(L0_ffn_gate_s, f32)
    us = np.asarray(L0_ffn_up_s, f32)
    ds = np.asarray(L0_ffn_down_s, f32)

    gwt = np.ascontiguousarray(
        gwm.reshape(NIC, 128, 16, 128).transpose(0, 3, 2, 1)).reshape(NIC, 128, 2048)
    uwt = np.ascontiguousarray(
        uwm.reshape(NIC, 128, 16, 128).transpose(0, 3, 2, 1)).reshape(NIC, 128, 2048)
    dwt = np.ascontiguousarray(
        dwm.reshape(16, 128, NIC, 128).transpose(0, 3, 2, 1)).reshape(16, 128,
                                                                      NIC * 128)
    gs2 = np.ascontiguousarray(gs.reshape(NIC, 128).T)
    us2 = np.ascontiguousarray(us.reshape(NIC, 128).T)
    ds2 = np.ascontiguousarray(ds.reshape(16, 128).T)
    qg2 = np.ascontiguousarray(np.tile(qg.reshape(1, NH), (128, 1)))

    inv_freq = (1.0 / (10000.0 ** (np.arange(0, HD, 2, dtype=f32) / HD))).astype(f32)
    t_all = np.arange(SEQ, dtype=f32)
    freqs = np.outer(t_all, inv_freq).astype(f32)
    cos_all = np.cos(freqs).astype(BF)
    sin_all = np.sin(freqs).astype(BF)
    cosk = np.ascontiguousarray(cos_all.reshape(NT_ALL, 128, 64).transpose(1, 0, 2))
    sink = np.ascontiguousarray(sin_all.reshape(NT_ALL, 128, 64).transpose(1, 0, 2))
    ident = np.eye(128, dtype=BF)

    per_core = []
    for core in range(8):
        b, lane = core // 4, core % 4
        x_all = np.ascontiguousarray(x[b])
        x_own = np.ascontiguousarray(x[b, lane::4])
        cos_own = np.ascontiguousarray(
            cos_all[lane::4].reshape(NT_OWN, 128, 64).transpose(1, 0, 2))
        sin_own = np.ascontiguousarray(
            sin_all[lane::4].reshape(NT_OWN, 128, 64).transpose(1, 0, 2))
        kk = np.arange(128)[:, None]
        qq = np.arange(128)[None, :]
        m = np.zeros((128, 4, 512), dtype=BF)
        for i in range(4):
            mi = ((128 * i + kk) <= (4 * qq + lane)).astype(BF)
            m[:, i, :] = np.tile(mi, (1, 4))
        per_core.append(dict(
            x_all=x_all, x_own=x_own, wq=wq, wp=wp,
            gwt=gwt, uwt=uwt, dwt=dwt, gs2=gs2, us2=us2, ds2=ds2, qg2=qg2,
            cosk=cosk, sink=sink, cosq=cos_own, sinq=sin_own,
            msk=np.ascontiguousarray(m), idn=ident,
        ))
    return per_core


def kernel(L0_attn_x, L0_attn_w_qkv, L0_attn_w_proj, L0_attn_q_gain,
           L0_ffn_gate_w, L0_ffn_up_w, L0_ffn_down_w,
           L0_ffn_gate_s, L0_ffn_up_s, L0_ffn_down_s):
    in_maps = _host_prepare(L0_attn_x, L0_attn_w_qkv, L0_attn_w_proj,
                            L0_attn_q_gain, L0_ffn_gate_w, L0_ffn_up_w,
                            L0_ffn_down_w, L0_ffn_gate_s, L0_ffn_up_s,
                            L0_ffn_down_s)
    nc = build_program()
    res = run_bass_kernel_spmd(nc, in_maps, core_ids=list(range(8)))
    out = np.empty((B, SEQ, D), np.float32)
    for core in range(8):
        b, lane = core // 4, core % 4
        out[b, lane::4] = res.results[core]["out_own"]
    return out


# version 23 (final)
# speedup vs baseline: 1.0320x; 1.0001x over previous
import numpy as np
import ml_dtypes
from contextlib import ExitStack

import concourse.bass as bass
import concourse.mybir as mybir
from concourse.bass_utils import run_bass_kernel_spmd

F32 = mybir.dt.float32
BF16 = mybir.dt.bfloat16
AF = mybir.ActivationFunctionType
ALU = mybir.AluOpType
AX = mybir.AxisListType
BF = ml_dtypes.bfloat16

B, SEQ, D = 2, 4096, 2048
NH, NKV, HD = 16, 4, 128
OQKV = 3072
INTER = 5632
NIC = INTER // 128
OWN = SEQ // 4
NT_ALL = SEQ // 128
NT_OWN = OWN // 128
RMS_EPS = 1.1920929e-07
LN_EPS = 1e-5
THR_C = 0.5009765625
INV_SQRT_HD = 0.08838834764831843


class Sem:
    def __init__(self, stack, nc, name):
        self.h = stack.enter_context(nc.semaphore(name))
        self.n = 0

    def inc(self, instr, k=1):
        instr.then_inc(self.h, k)
        self.n += k
        return instr

    def wait(self, eng, n=None):
        v = self.n if n is None else n
        if v > 0:
            eng.wait_ge(self.h, v)


def build_program(stop_after=None, ph2_taus=NT_OWN, ph2_quant=True, ph2_ytr=True, ph2_tail=True, debug=False):
    nc = bass.Bass("TRN2", target_bir_lowering=False)

    X_ALL = nc.dram_tensor("x_all", [SEQ, D], F32, kind="ExternalInput").ap()
    X_OWN = nc.dram_tensor("x_own", [OWN, D], F32, kind="ExternalInput").ap()
    WQ = nc.dram_tensor("wq", [OQKV, D], F32, kind="ExternalInput").ap()
    WP = nc.dram_tensor("wp", [D, D], F32, kind="ExternalInput").ap()
    GWT = nc.dram_tensor("gwt", [NIC, 128, 2048], F32, kind="ExternalInput").ap()
    UWT = nc.dram_tensor("uwt", [NIC, 128, 2048], F32, kind="ExternalInput").ap()
    DWT = nc.dram_tensor("dwt", [16, 128, NIC * 128], F32, kind="ExternalInput").ap()
    GS2 = nc.dram_tensor("gs2", [128, NIC], F32, kind="ExternalInput").ap()
    US2 = nc.dram_tensor("us2", [128, NIC], F32, kind="ExternalInput").ap()
    DS2 = nc.dram_tensor("ds2", [128, 16], F32, kind="ExternalInput").ap()
    QG2 = nc.dram_tensor("qg2", [128, 16], F32, kind="ExternalInput").ap()
    COSK = nc.dram_tensor("cosk", [128, NT_ALL, 64], BF16, kind="ExternalInput").ap()
    SINK = nc.dram_tensor("sink", [128, NT_ALL, 64], BF16, kind="ExternalInput").ap()
    COSQ = nc.dram_tensor("cosq", [128, NT_OWN, 64], BF16, kind="ExternalInput").ap()
    SINQ = nc.dram_tensor("sinq", [128, NT_OWN, 64], BF16, kind="ExternalInput").ap()
    MSK = nc.dram_tensor("msk", [128, 4, 512], BF16, kind="ExternalInput").ap()
    IDN = nc.dram_tensor("idn", [128, 128], BF16, kind="ExternalInput").ap()

    OUT = nc.dram_tensor("out_own", [OWN, D], F32, kind="ExternalOutput").ap()
    if debug:
        DBG_W = nc.dram_tensor("dbg_w", [128, 16, 1024], BF16, kind="ExternalOutput").ap()
        DBG_H = nc.dram_tensor("dbg_h", [128, NT_OWN, 16, 128], BF16, kind="ExternalOutput").ap()
        DBG_K = nc.dram_tensor("dbg_k", [128, NKV, SEQ], BF16, kind="ExternalOutput").ap()
        DBG_V = nc.dram_tensor("dbg_v", [128, NKV, NT_ALL, 132], BF16, kind="ExternalOutput").ap()
        DBG_Q = nc.dram_tensor("dbg_q", [128, NKV * NT_OWN * 4 * 128], BF16, kind="ExternalOutput").ap()
        DBG_R = nc.dram_tensor("dbg_r", [128, NT_ALL * 4], F32, kind="ExternalOutput").ap()
        DBG_Y = nc.dram_tensor("dbg_y", [128, NT_OWN, D], BF16, kind="ExternalOutput").ap()
        DBG_H2 = nc.dram_tensor("dbg_h2", [128, 16, OWN], BF16, kind="ExternalOutput").ap()

    WQ_EFF = nc.dram_tensor("wq_eff", [OQKV, D], BF16, kind="Internal").ap()
    WP_EFF = nc.dram_tensor("wp_eff", [D, D], BF16, kind="Internal").ap()

    st = ExitStack()
    with st:
        sb = lambda n, s, d: st.enter_context(nc.sbuf_tensor(n, s, d))
        ident = sb("ident", [128, 128], BF16)
        masks = sb("masks", [128, 4, 512], BF16)
        cosk = sb("cosk_t", [128, NT_ALL, 64], BF16)
        sink = sb("sink_t", [128, NT_ALL, 64], BF16)
        cosq = sb("cosq_t", [128, NT_OWN, 64], BF16)
        sinq = sb("sinq_t", [128, NT_OWN, 64], BF16)
        gs2 = sb("gs2_t", [128, NIC], F32)
        us2 = sb("us2_t", [128, NIC], F32)
        ds2 = sb("ds2_t", [128, 16], F32)
        qg2 = sb("qg2_t", [128, 16], F32)
        qgs = sb("qgs_t", [128, 16], F32)
        fence = sb("fence", [1, 16], F32)
        bn6 = sb("bn6", [128, 24, 6], F32)
        bn2 = sb("bn2", [128, 16, 2], F32)
        stV = sb("stV", [128, 4], F32)
        stLN = sb("stLN", [128, 4], F32)
        stR = sb("stR", [128, 4], F32)
        stMR = sb("stMR", [128, 4], F32)
        st1 = sb("st1", [128, 16], F32)
        st2 = sb("st2", [128, 16], F32)
        st3 = sb("st3", [128, 16], F32)
        st4 = sb("st4", [128, 512], BF16)
        st5 = sb("st5", [128, 16], F32)
        st6 = sb("st6", [128, 16], F32)
        st7 = sb("st7", [128, 16], F32)
        stb = sb("stb", [128, 16], BF16)
        st8 = sb("st8", [128, 16], F32)
        stL = sb("stL", [128, 8], F32)
        stK = sb("stK", [128, 16], F32)
        rstdk = sb("rstdk", [128, NT_ALL * 4], F32)

        sems = {}

        def mksem(name):
            sems[name] = Sem(st, nc, name)
            return sems[name]

        CONST = mksem("CONST")
        XL = mksem("XL")
        WTL = mksem("WTL")
        QBL = mksem("QBL")
        QST = mksem("QST")
        FWL = mksem("FWL")
        OST = mksem("OST")
        OACC = mksem("OACC")
        BAR = mksem("BAR")
        LNS = mksem("LNS")
        LNR = mksem("LNR")
        MRD = mksem("MRD")
        H1D = mksem("H1D")
        TRP = mksem("TRP")
        TRE = mksem("TRE")
        MMD = mksem("MMD")
        KEV = mksem("KEV")
        KBN = mksem("KBN")
        KRS = mksem("KRS")
        KRO = mksem("KRO")
        VEV = mksem("VEV")
        KTR = mksem("KTR")
        KTE = mksem("KTE")
        QED = mksem("QED")
        QTH = mksem("QTH")
        QNG = mksem("QNG")
        QPS = mksem("QPS")
        QWE = mksem("QWE")
        QKD = mksem("QKD")
        EXPD = mksem("EXPD")
        MSKD = mksem("MSKD")
        AVD = mksem("AVD")
        OEV = mksem("OEV")
        YST = mksem("YST")
        YRS = mksem("YRS")
        YND = mksem("YND")
        PJE = mksem("PJE")
        H2D = mksem("H2D")
        GUE = mksem("GUE")
        HDN = mksem("HDN")
        DSE = mksem("DSE")
        DTE = mksem("DTE")
        MMG = mksem("MMG")
        UBE = mksem("UBE")
        FWU = mksem("FWU")
        DWL = mksem("DWL")
        VSY = mksem("VSY")
        ASY = mksem("ASY")
        QAB = mksem("QAB")
        KSC = mksem("KSC")
        YTP = mksem("YTP")
        YTE = mksem("YTE")

        blk = st.enter_context(nc.Block())

        fence_ops = {
            "vector": lambda v: v.memset(fence[0:1, 0:1], 0.0),
            "scalar": lambda s: s.memzero(fence[0:1, 2:3]),
            "gpsimd": lambda g: g.memset(fence[0:1, 4:5], 0.0),
            "tensor": lambda t: t.nop(),
        }

        def vs(v, instr):
            VSY.inc(instr)
            v.wait_ge(VSY.h, VSY.n)

        def asy(s, instr):
            ASY.inc(instr)
            s.wait_ge(ASY.h, ASY.n)

        def barrier():
            for e in ("vector", "scalar", "gpsimd", "tensor"):
                def emit_f(eng, e=e):
                    BAR.inc(fence_ops[e](eng))
                getattr(blk, e)(emit_f)
            tgt = BAR.n
            dma_targets = [(s, s.n) for s in (CONST, XL, WTL, QBL, QST, FWL,
                                              FWU, DWL, OST, OACC)]

            def waits(eng):
                eng.wait_ge(BAR.h, tgt)
                for s_, v in dma_targets:
                    if v > 0:
                        eng.wait_ge(s_.h, v)
            for e in ("sync", "gpsimd", "vector", "scalar", "tensor"):
                getattr(blk, e)(waits)

        LNG = [0, 8, 12, 16]

        def make_ln(idx, p, x_t, h_t, extra_dve_wait=None, extra_act_wait=None):
            g0 = LNG[p]

            def dve1(v, idx=idx, p=p, g0=g0):
                if extra_dve_wait is not None:
                    extra_dve_wait(v)
                for g4 in range(4):
                    i = v.bn_stats(bn6[:, g0 + g4, :],
                                   x_t[:, g4 * 512:(g4 + 1) * 512])
                vs(v, i)
                means = bn6[:, g0:g0 + 4, 1::3]
                ctvs = bn6[:, g0:g0 + 4, 2::3]
                st8v = st8[:, 0:8].rearrange("p (g t) -> p g t", g=4)
                v.tensor_reduce(stL[:, 0:1], means, AX.XY, ALU.add)
                v.tensor_reduce(stL[:, 1:2], ctvs, AX.XY, ALU.add)
                i = v.tensor_tensor(st8v, means, means, ALU.mult)
                vs(v, i)
                v.tensor_reduce(stL[:, 2:3], st8v, AX.XY, ALU.add)
                v.tensor_scalar(bn2[:, p, 0:1], stL[:, 0:1], 0.125, None,
                                ALU.mult)
                v.scalar_tensor_tensor(stL[:, 3:4], stL[:, 0:1],
                                       -0.125 * 0.125, stL[:, 0:1],
                                       ALU.mult, ALU.mult)
                i = v.tensor_scalar(stL[:, 4:5], stL[:, 1:2], 1.0 / 2048,
                                    LN_EPS, ALU.mult, ALU.add)
                vs(v, i)
                i = v.scalar_tensor_tensor(stL[:, 5:6], stL[:, 2:3], 0.125,
                                           stL[:, 3:4], ALU.mult, ALU.add)
                vs(v, i)
                LNS.inc(v.scalar_tensor_tensor(stV[:, p:p + 1], stL[:, 5:6],
                                               1.0, stL[:, 4:5],
                                               ALU.mult, ALU.add))

            def act1(s, idx=idx, p=p):
                LNS.wait(s, idx + 1)
                i = s.activation(stLN[:, p:p + 1], stV[:, p:p + 1], AF.Ln)
                asy(s, i)
                LNR.inc(s.activation(stR[:, p:p + 1], stLN[:, p:p + 1], AF.Exp,
                                     scale=-0.5))

            def dve2(v, idx=idx, p=p):
                LNR.wait(v, idx + 1)
                MRD.inc(v.scalar_tensor_tensor(stMR[:, p:p + 1], bn2[:, p, 0:1],
                                               -1.0, stR[:, p:p + 1],
                                               ALU.mult, ALU.mult))

            def act2(s, idx=idx, p=p):
                MRD.wait(s, idx + 1)
                if extra_act_wait is not None:
                    extra_act_wait(s)
                H1D.inc(s.activation(h_t[:, :], x_t[:, :], AF.Identity,
                                     bias=stMR[:, p:p + 1],
                                     scale=stR[:, p:p + 1]))
            return dve1, act1, dve2, act2

        def emit_ln(idx, p, x_t, h_t, extra_dve_wait=None, extra_act_wait=None):
            d1, a1, d2, a2 = make_ln(idx, p, x_t, h_t, extra_dve_wait,
                                     extra_act_wait)
            blk.vector(d1)
            blk.scalar(a1)
            blk.vector(d2)
            blk.scalar(a2)

        def emit_kstats(v, p2, seg):
            kb = 4 if p2 == 0 else 20
            mp = 8 * p2
            for h4 in range(4):
                i = v.bn_stats(bn6[:, kb + h4, :], seg[:, h4, :])
            vs(v, i)
            means = bn6[:, kb:kb + 4, 1::3]
            ctvs = bn6[:, kb:kb + 4, 2::3]
            st8v = st8[:, mp:mp + 8].rearrange("p (g t) -> p g t", g=4)
            v.tensor_reduce(stK[:, mp:mp + 4], ctvs, AX.X, ALU.add)
            i = v.tensor_tensor(st8v, means, means, ALU.mult)
            vs(v, i)
            i = v.tensor_reduce(stK[:, mp + 4:mp + 8], st8v, AX.X, ALU.add)
            vs(v, i)
            i = v.scalar_tensor_tensor(st8[:, mp:mp + 4], stK[:, mp + 4:mp + 8],
                                       64.0, stK[:, mp:mp + 4],
                                       ALU.mult, ALU.add)
            vs(v, i)
            KBN.inc(v.tensor_scalar(st1[:, 4 + 4 * p2:8 + 4 * p2],
                                    st8[:, mp:mp + 4], 1.0 / 128, RMS_EPS,
                                    ALU.mult, ALU.add))

        def emit_tr16(src_t, dst_fn, ps_tr, h1d_idx, trp_base, tre_base, gbase,
                      pe_extra=None):
            def pe1(t):
                H1D.wait(t, h1d_idx + 1)
                if pe_extra is not None:
                    pe_extra(t)
                for grp in range(4):
                    gi = gbase + grp
                    TRE.wait(t, tre_base + max(0, gi - 1))
                    last = None
                    for k in range(4):
                        dc = grp * 4 + k
                        last = t.transpose(ps_tr[gi % 2][:, k * 128:(k + 1) * 128],
                                           src_t[:, dc * 128:(dc + 1) * 128],
                                           ident[:, :])
                    TRP.inc(last)
            blk.tensor(pe1)

            def dve3(v):
                for grp in range(4):
                    gi = gbase + grp
                    TRP.wait(v, trp_base + gi + 1)
                    TRE.inc(v.tensor_copy(
                        dst_fn(grp),
                        ps_tr[gi % 2][:, 0:512].rearrange("p (k f) -> p k f", k=4)))
            blk.vector(dve3)

        def emit_quant(k, w_bf, weff_dst):
            def qact(s, k=k):
                QBL.wait(s, 16 * (k + 1))
                QWE.wait(s, k)
                s.activation(qbuf['aw'][k % 2][:, :], w_bf[:, :], AF.Abs)
                QAB.inc(s.activation(qbuf['sg'][k % 2][:, :], w_bf[:, :], AF.Sign))
            blk.scalar(qact)

            def qdve(v, k=k):
                QBL.wait(v, 16 * (k + 1))
                i = v.tensor_reduce(st5[:, :],
                                    w_bf[:, :].rearrange("p (g d) -> p g d", g=16),
                                    AX.X, ALU.add, apply_absolute_value=True)
                vs(v, i)
                i = v.tensor_scalar(stb[:, :], st5[:, :], 1.0 / 128.0, None,
                                    ALU.mult)
                vs(v, i)
                i = v.tensor_copy(st6[:, :], stb[:, :])
                vs(v, i)
                i = v.tensor_scalar(st7[:, :], st6[:, :], THR_C, None, ALU.mult)
                vs(v, i)
                i = v.tensor_copy(qbuf['thrf'][:, :, :],
                                  st7[:, :].unsqueeze(2).broadcast_to((128, 16, 128)))
                v.tensor_copy(qbuf['scf'][:, :, :],
                              stb[:, :].unsqueeze(2).broadcast_to((128, 16, 128)))
                vs(v, i)
                QAB.wait(v, k + 1)
                awv = qbuf['aw'][k % 2][:, :].rearrange("p (g d) -> p g d", g=16)
                i = v.tensor_tensor(awv, awv, qbuf['thrf'][:, :, :], ALU.is_gt)
                vs(v, i)
                i = v.tensor_tensor(qbuf['aw'][k % 2][:, :], qbuf['aw'][k % 2][:, :],
                                    qbuf['sg'][k % 2][:, :], ALU.mult)
                vs(v, i)
                QWE.inc(v.tensor_tensor(
                    weff_dst[:, :].rearrange("p (g d) -> p g d", g=16),
                    qbuf['aw'][k % 2][:, :].rearrange("p (g d) -> p g d", g=16),
                    qbuf['scf'][:, :, :], ALU.mult))
            blk.vector(qdve)

        def ph0_sync(sy):
            for dst, src in ((ident, IDN), (masks, MSK), (cosk, COSK),
                             (sink, SINK), (cosq, COSQ), (sinq, SINQ),
                             (gs2, GS2), (us2, US2), (ds2, DS2), (qg2, QG2)):
                CONST.inc(sy.dma_start(out=dst[...], in_=src[...]), 16)
        blk.sync(ph0_sync)

        def ph0_dve(v):
            CONST.wait(v)
            v.tensor_scalar(qgs[:, :], qg2[:, :], INV_SQRT_HD, None, ALU.mult)
        blk.vector(ph0_dve)
        barrier()
        if stop_after == 0:
            return nc

        sA = ExitStack()
        K_fm = sA.enter_context(nc.sbuf_tensor("K_fm", [128, NKV, SEQ], BF16))
        V_aug = sA.enter_context(nc.sbuf_tensor("V_aug", [128, NKV, NT_ALL, 132], BF16))

        qbuf = {}

        def alloc_quant_scratch(scope, tag):
            qbuf['aw'] = [scope.enter_context(
                nc.sbuf_tensor(f"qaw{tag}", [128, D], BF16))] * 2
            qbuf['sg'] = [scope.enter_context(
                nc.sbuf_tensor(f"qsg{tag}", [128, D], BF16))] * 2
            qbuf['wbf'] = [scope.enter_context(
                nc.sbuf_tensor(f"qwbf{tag}{i}", [128, D], BF16)) for i in range(2)]
            qbuf['thrf'] = scope.enter_context(
                nc.sbuf_tensor(f"qthrf{tag}", [128, 16, 128], F32))
            qbuf['scf'] = scope.enter_context(
                nc.sbuf_tensor(f"qscf{tag}", [128, 16, 128], BF16))

        sQ = ExitStack()
        alloc_quant_scratch(sQ, "a")

        def quant_src_dst(k):
            if k < 8:
                wt = 16 + k
                return (WQ[wt * 128:(wt + 1) * 128, :],
                        WQ_EFF[wt * 128:(wt + 1) * 128, :])
            if k < 24:
                wt = k - 8
                return (WQ[wt * 128:(wt + 1) * 128, :],
                        WQ_EFF[wt * 128:(wt + 1) * 128, :])
            wt = k - 24
            return (WP[wt * 128:(wt + 1) * 128, :],
                    WP_EFF[wt * 128:(wt + 1) * 128, :])

        def emit_quant_tile(k):
            srcap, dstap = quant_src_dst(k)

            def pload(g, k=k, srcap=srcap):
                QST.wait(g, 16 * max(0, k - 1))
                QBL.inc(g.dma_start(out=qbuf['wbf'][k % 2][:, :], in_=srcap), 16)
            blk.gpsimd(pload)
            emit_quant(k, qbuf['wbf'][k % 2], qbuf['wbf'][k % 2])

            def pstore(g, k=k, dstap=dstap):
                QWE.wait(g, k + 1)
                QST.inc(g.dma_start(out=dstap, in_=qbuf['wbf'][k % 2][:, :]), 16)
            blk.gpsimd(pstore)

        for k in range(8):
            emit_quant_tile(k)
        barrier()

        with ExitStack() as s1b:
            sb1 = lambda n, s, d: s1b.enter_context(nc.sbuf_tensor(n, s, d))
            ps1 = lambda n, s, d: s1b.enter_context(nc.psum_tensor(n, s, d))
            x_f = [sb1(f"x{i}", [128, D], F32) for i in range(3)]
            h1 = [sb1(f"h1{i}", [128, D], BF16) for i in range(2)]
            h1T = [sb1(f"h1T{i}", [128, 16, 128], BF16) for i in range(2)]
            wkvT = sb1("wkvT", [128, 16, 1024], BF16)
            kbf = [sb1(f"kbf{i}", [128, 512], BF16) for i in range(2)]
            krope = [sb1(f"krope{i}", [128, 512], BF16) for i in range(2)]
            ps_tr = [ps1(f"pstr{i}", [128, 1024], BF16) for i in range(2)]
            ps_ktr = [ps1(f"psktr{i}", [128, 1024], BF16) for i in range(2)]
            ps_k = [ps1(f"psk{i}", [128, 512], F32) for i in range(2)]
            ps_v = [ps1(f"psv{i}", [128, 512], F32) for i in range(2)]

            def dve_init(v):
                v.memset(V_aug[:, :, :, 128:129], 1.0)
            blk.vector(dve_init)

            def sync_wkvT(sy):
                QST.wait(sy, 16 * 8)
                for dc in range(16):
                    for half in range(2):
                        WTL.inc(sy.dma_start(
                            out=wkvT[:, dc, half * 512:(half + 1) * 512],
                            in_=WQ_EFF[2048 + half * 512:2048 + (half + 1) * 512,
                                       dc * 128:(dc + 1) * 128],
                            transpose=True), 16)
            blk.sync(sync_wkvT)
            wkvT_t = WTL.n

            def st_load(tt):
                def sy(sy_, tt=tt):
                    if tt >= 3:
                        H1D.wait(sy_, tt - 2)
                    XL.inc(sy_.dma_start(out=x_f[tt % 3][:, :],
                                         in_=X_ALL[tt * 128:(tt + 1) * 128, :]), 16)
                blk.sync(sy)

            def st_tr(tt):
                def pe1(t, tt=tt):
                    H1D.wait(t, tt + 1)
                    if tt >= 2:
                        MMD.wait(t, 2 * (tt - 1))
                    TRE.wait(t, 2 * tt)
                    for half in range(2):
                        last = None
                        for kk in range(8):
                            dc = half * 8 + kk
                            last = t.transpose(
                                ps_tr[half][:, kk * 128:(kk + 1) * 128],
                                h1[tt % 2][:, dc * 128:(dc + 1) * 128],
                                ident[:, :])
                        TRP.inc(last)
                blk.tensor(pe1)

            def st_trev(tt):
                def dve3(v, tt=tt):
                    for half in range(2):
                        TRP.wait(v, 2 * tt + half + 1)
                        dst = h1T[tt % 2][:, half * 8:(half + 1) * 8, :]
                        TRE.inc(v.tensor_copy(
                            dst,
                            ps_tr[half][:, :].rearrange("p (k f) -> p k f",
                                                        k=8)))
                blk.vector(dve3)

            def st_kvmm(tt):
                def pe2(t, tt=tt):
                    if tt == 0:
                        WTL.wait(t, wkvT_t)
                    TRE.wait(t, 2 * (tt + 1))
                    KEV.wait(t, max(0, tt - 1))
                    VEV.wait(t, max(0, tt - 1))
                    last_v = None
                    for dc in range(16):
                        t.matmul(ps_k[tt % 2][:, :], h1T[tt % 2][:, dc, :],
                                 wkvT[:, dc, 0:512],
                                 start=(dc == 0), stop=(dc == 15))
                        last_v = t.matmul(ps_v[tt % 2][:, :], h1T[tt % 2][:, dc, :],
                                          wkvT[:, dc, 512:1024],
                                          start=(dc == 0), stop=(dc == 15))
                    MMD.inc(last_v, 2)
                blk.tensor(pe2)

            def st_kevac(tt):
                def act3(s, tt=tt):
                    MMD.wait(s, 2 * (tt + 1))
                    KRO.wait(s, max(0, tt - 1))
                    KEV.inc(s.activation(kbf[tt % 2][:, :], ps_k[tt % 2][:, :],
                                         AF.Copy))
                blk.scalar(act3)

            def st_kstats(tt):
                def dve4(v, tt=tt):
                    KEV.wait(v, tt + 1)
                    KRS.wait(v, max(0, tt - 1))
                    kseg = kbf[tt % 2][:, :].rearrange("p (h d) -> p h d", h=4)
                    emit_kstats(v, tt % 2, kseg)
                    KTR.wait(v, max(0, tt - 1))
                    k1 = kseg[:, :, 0:64]
                    k2 = kseg[:, :, 64:128]
                    co = cosk[:, tt, :].unsqueeze(1).broadcast_to((128, 4, 64))
                    si = sink[:, tt, :].unsqueeze(1).broadcast_to((128, 4, 64))
                    kr = krope[tt % 2][:, :].rearrange("p (h d) -> p h d", h=4)
                    t1 = st4[:, :].rearrange("p (h d) -> p h d", h=4)[:, :, 0:64]
                    t2 = st4[:, :].rearrange("p (h d) -> p h d", h=4)[:, :, 64:128]
                    v.tensor_tensor(t1, k1, co, ALU.mult)
                    v.tensor_tensor(t2, k2, si, ALU.mult)
                    v.tensor_tensor(kr[:, :, 0:64], t1, t2, ALU.add)
                    v.tensor_tensor(t1, k2, co, ALU.mult)
                    v.tensor_tensor(t2, k1, si, ALU.mult)
                    KRO.inc(v.tensor_tensor(kr[:, :, 64:128], t1, t2,
                                            ALU.subtract))
                    MMD.wait(v, 2 * (tt + 1))
                    last = v.tensor_copy(
                        V_aug[:, :, tt, 0:128].rearrange("p h d -> p h d"),
                        ps_v[tt % 2][:, :].rearrange("p (h d) -> p h d", h=4))
                    VEV.inc(last)
                blk.vector(dve4)

            def st_krstd(tt):
                def act4(s, tt=tt):
                    p2 = tt % 2
                    KBN.wait(s, tt + 1)
                    i = s.activation(st2[:, 4 + 4 * p2:8 + 4 * p2],
                                     st1[:, 4 + 4 * p2:8 + 4 * p2], AF.Ln)
                    asy(s, i)
                    KRS.inc(s.activation(rstdk[:, tt * 4:(tt + 1) * 4],
                                         st2[:, 4 + 4 * p2:8 + 4 * p2],
                                         AF.Exp, scale=-0.5))
                blk.scalar(act4)

            def st_kscale(tt):
                def dvf(v, tt=tt):
                    KRS.wait(v, tt + 1)
                    last = None
                    for h in range(4):
                        last = v.tensor_scalar(
                            krope[tt % 2][:, h * 128:(h + 1) * 128],
                            krope[tt % 2][:, h * 128:(h + 1) * 128],
                            rstdk[:, tt * 4 + h:tt * 4 + h + 1], None, ALU.mult)
                    KSC.inc(last)
                blk.vector(dvf)

            def st_ktr(tt):
                def pe3(t, tt=tt):
                    KSC.wait(t, tt + 1)
                    KTE.wait(t, max(0, tt - 1))
                    last = None
                    for h in range(4):
                        last = t.transpose(ps_ktr[tt % 2][:, h * 128:(h + 1) * 128],
                                           krope[tt % 2][:, h * 128:(h + 1) * 128],
                                           ident[:, :])
                    KTR.inc(last)
                blk.tensor(pe3)

            def st_ktrev(tt):
                def dve5(v, tt=tt):
                    KTR.wait(v, tt + 1)
                    KTE.inc(v.tensor_copy(
                        V_aug[:, 0, 0, 0:1] if False else
                        K_fm[:, :, tt * 128:(tt + 1) * 128].rearrange(
                            "p h d -> p h d"),
                        ps_ktr[tt % 2][:, 0:512].rearrange("p (h d) -> p h d",
                                                           h=4)))
                blk.vector(dve5)

            def in_range(tt):
                return 0 <= tt < NT_ALL

            ln_stash = {}
            for m in range(NT_ALL + 8):
                if m == 0:
                    st_load(0)
                    st_load(1)
                    st_load(2)
                elif in_range(m + 2):
                    st_load(m + 2)
                if in_range(m):
                    lns = make_ln(m, m % 4, x_f[m % 3], h1[m % 2],
                                  extra_dve_wait=(
                                      lambda v, tt=m: XL.wait(v, 16 * (tt + 1))),
                                  extra_act_wait=(
                                      lambda s, tt=m: TRP.wait(s, 2 * (tt - 1))
                                      if tt >= 2 else None))
                else:
                    lns = None
                if in_range(m - 4):
                    st_trev(m - 4)
                if in_range(m - 7):
                    st_ktrev(m - 7)
                if lns:
                    ln_stash[m % 4] = lns
                    blk.vector(lns[0])
                if in_range(m - 1):
                    blk.scalar(ln_stash[(m - 1) % 4][1])
                if in_range(m - 2):
                    blk.vector(ln_stash[(m - 2) % 4][2])
                    blk.scalar(ln_stash[(m - 2) % 4][3])
                if in_range(m - 3):
                    st_tr(m - 3)
                if in_range(m - 4):
                    st_kvmm(m - 4)
                if in_range(m - 6):
                    st_krstd(m - 6)
                if in_range(m - 5):
                    st_kevac(m - 5)
                    st_kstats(m - 5)
                if in_range(m - 6):
                    st_kscale(m - 6)
                    st_ktr(m - 6)
                if m >= 7 and (m - 7) % 2 == 0 and (m - 7) // 2 < 16:
                    emit_quant_tile(8 + (m - 7) // 2)
            barrier()
            if debug:
                def dbgw(g):
                    OST.inc(g.dma_start(out=DBG_W[...], in_=wkvT[...]), 16)
                barrier()
                blk.gpsimd(dbgw)
            barrier()
        sQ.close()
        Q4 = sA.enter_context(
            nc.sbuf_tensor("Q4", [128, NKV * NT_OWN * 4 * 128], BF16))
        if stop_after == 2:
            sA.close()
            return nc

        with ExitStack() as s1c:
            sb1 = lambda n, s, d: s1c.enter_context(nc.sbuf_tensor(n, s, d))
            ps1 = lambda n, s, d: s1c.enter_context(nc.psum_tensor(n, s, d))
            x_f = [sb1(f"xo{i}", [128, D], F32) for i in range(3)]
            h1 = [sb1(f"h1o{i}", [128, D], BF16) for i in range(2)]
            h1ownT = sb1("h1ownT", [128, NT_OWN, 16, 128], BF16)
            wqT = sb1("wqT", [128, 16, 512], BF16)
            qbf = [sb1(f"qbf{i}", [128, 512], BF16) for i in range(2)]
            qrope = [sb1(f"qrope{i}", [128, 512], BF16) for i in range(3)]
            qfin = [sb1(f"qfin{i}", [128, 512], BF16) for i in range(2)]
            ps_tr = [ps1(f"pstrc{i}", [128, 1024], BF16) for i in range(2)]
            ps_qtr = [ps1(f"psqtr{i}", [128, 1024], BF16) for i in range(2)]
            ps_q = [ps1(f"psq{i}", [128, 512], F32) for i in range(2)]

            trp0, tre0 = TRP.n, TRE.n
            ktr0, kte0 = KTR.n, KTE.n
            mmd0, kev0, kbn0, krs0, kro0 = MMD.n, KEV.n, KBN.n, KRS.n, KRO.n
            h1d0, xl0, qed0 = H1D.n, XL.n, QED.n

            def st_load(tt):
                def sy(sy_, tt=tt):
                    if tt >= 3:
                        H1D.wait(sy_, h1d0 + tt - 2)
                    XL.inc(sy_.dma_start(out=x_f[tt % 3][:, :],
                                         in_=X_OWN[tt * 128:(tt + 1) * 128, :]), 16)
                blk.sync(sy)

            def st_tr(tt):
                def pe1(t, tt=tt):
                    H1D.wait(t, h1d0 + tt + 1)
                    TRE.wait(t, tre0 + 2 * tt)
                    for half in range(2):
                        last = None
                        for kk in range(8):
                            dc = half * 8 + kk
                            last = t.transpose(
                                ps_tr[half][:, kk * 128:(kk + 1) * 128],
                                h1[tt % 2][:, dc * 128:(dc + 1) * 128],
                                ident[:, :])
                        TRP.inc(last)
                blk.tensor(pe1)

            def st_trev(tt):
                def dve3(v, tt=tt):
                    for half in range(2):
                        TRP.wait(v, trp0 + 2 * tt + half + 1)
                        dst = h1ownT[:, tt, half * 8:(half + 1) * 8, :]
                        TRE.inc(v.tensor_copy(
                            dst,
                            ps_tr[half][:, :].rearrange("p (k f) -> p k f", k=8)))
                blk.vector(dve3)

            ln_stash = {}
            for m in range(NT_OWN + 4):
                if m == 0:
                    st_load(0)
                if m + 1 < NT_OWN:
                    st_load(m + 1)
                if m < NT_OWN:
                    lns = make_ln(h1d0 + m, m % 4, x_f[m % 3], h1[m % 2],
                                  extra_dve_wait=(
                                      lambda v, tt=m: XL.wait(
                                          v, xl0 + 16 * (tt + 1))),
                                  extra_act_wait=(
                                      lambda s, tt=m: TRP.wait(
                                          s, trp0 + 2 * (tt - 1))
                                      if tt >= 2 else None))
                    ln_stash[m % 4] = lns
                    blk.vector(lns[0])
                if 0 <= m - 1 < NT_OWN:
                    blk.scalar(ln_stash[(m - 1) % 4][1])
                if 0 <= m - 2 < NT_OWN:
                    blk.vector(ln_stash[(m - 2) % 4][2])
                    blk.scalar(ln_stash[(m - 2) % 4][3])
                if 0 <= m - 3 < NT_OWN:
                    st_tr(m - 3)
                if 0 <= m - 4 < NT_OWN:
                    st_trev(m - 4)

            wq_t = {}

            def st_wqload(ob):
                def syw(sy_, ob=ob):
                    QST.wait(sy_, 16 * 24)
                    if ob >= 1:
                        MMD.wait(sy_, mmd0 + 8 * ob)
                    for dc in range(16):
                        WTL.inc(sy_.dma_start(
                            out=wqT[:, dc, :],
                            in_=WQ_EFF[ob * 512:(ob + 1) * 512,
                                       dc * 128:(dc + 1) * 128],
                            transpose=True), 16)
                blk.sync(syw)
                wq_t[ob] = WTL.n

            def st_qmm(n):
                ob, tt = n // 8, n % 8

                def pe2(t, ob=ob, tt=tt, n=n):
                    if tt == 0:
                        WTL.wait(t, wq_t[ob])
                    TRE.wait(t, tre0 + 2 * (tt + 1))
                    KEV.wait(t, kev0 + max(0, n - 1))
                    last = None
                    for dc in range(16):
                        last = t.matmul(ps_q[n % 2][:, :], h1ownT[:, tt, dc, :],
                                        wqT[:, dc, :],
                                        start=(dc == 0), stop=(dc == 15))
                    MMD.inc(last)
                blk.tensor(pe2)

            def st_qevac(n):
                def act3(s, n=n):
                    MMD.wait(s, mmd0 + n + 1)
                    KRO.wait(s, kro0 + max(0, n - 1))
                    KEV.inc(s.activation(qbf[n % 2][:, :], ps_q[n % 2][:, :],
                                         AF.Copy))
                blk.scalar(act3)

            def st_qstats(n):
                ob, tt = n // 8, n % 8

                def dve4(v, ob=ob, tt=tt, n=n):
                    KEV.wait(v, kev0 + n + 1)
                    KRS.wait(v, krs0 + max(0, n - 1))
                    qseg = qbf[n % 2][:, :].rearrange("p (h d) -> p h d", h=4)
                    emit_kstats(v, n % 2, qseg)
                    KTR.wait(v, ktr0 + max(0, n - 2))
                    q1 = qseg[:, :, 0:64]
                    q2 = qseg[:, :, 64:128]
                    co = cosq[:, tt, :].unsqueeze(1).broadcast_to((128, 4, 64))
                    si = sinq[:, tt, :].unsqueeze(1).broadcast_to((128, 4, 64))
                    qr = qrope[n % 3][:, :].rearrange("p (h d) -> p h d", h=4)
                    t1 = st4[:, :].rearrange("p (h d) -> p h d", h=4)[:, :, 0:64]
                    t2 = st4[:, :].rearrange("p (h d) -> p h d", h=4)[:, :, 64:128]
                    v.tensor_tensor(t1, q1, co, ALU.mult)
                    v.tensor_tensor(t2, q2, si, ALU.mult)
                    v.tensor_tensor(qr[:, :, 0:64], t1, t2, ALU.add)
                    v.tensor_tensor(t1, q2, co, ALU.mult)
                    v.tensor_tensor(t2, q1, si, ALU.mult)
                    KRO.inc(v.tensor_tensor(qr[:, :, 64:128], t1, t2,
                                            ALU.subtract))
                blk.vector(dve4)

            def st_qrstd(n):
                def act4(s, n=n):
                    p2 = n % 2
                    p3 = n % 3
                    KBN.wait(s, kbn0 + n + 1)
                    i = s.activation(st2[:, 4 + 4 * p2:8 + 4 * p2],
                                     st1[:, 4 + 4 * p2:8 + 4 * p2], AF.Ln)
                    asy(s, i)
                    KRS.inc(s.activation(st3[:, 4 * p3:4 * p3 + 4],
                                         st2[:, 4 + 4 * p2:8 + 4 * p2],
                                         AF.Exp, scale=-0.5))
                blk.scalar(act4)

            def st_qscale(n):
                ob = n // 8

                def dve5(v, ob=ob, n=n):
                    p3 = n % 3
                    KRS.wait(v, krs0 + n + 1)
                    KTR.wait(v, ktr0 + max(0, n - 1))
                    i = v.tensor_tensor(st3[:, 12:16],
                                        st3[:, 4 * p3:4 * p3 + 4],
                                        qgs[:, ob * 4:(ob + 1) * 4], ALU.mult)
                    vs(v, i)
                    last = None
                    for g in range(4):
                        last = v.tensor_scalar(qfin[n % 2][:, g * 128:(g + 1) * 128],
                                               qrope[n % 3][:, g * 128:(g + 1) * 128],
                                               st3[:, 12 + g:13 + g],
                                               None, ALU.mult)
                    QED.inc(last)
                blk.vector(dve5)

            def st_qtr(n):
                def pe3(t, n=n):
                    QED.wait(t, qed0 + n + 1)
                    KTE.wait(t, kte0 + max(0, n - 1))
                    last = None
                    for g in range(4):
                        last = t.transpose(ps_qtr[n % 2][:, g * 128:(g + 1) * 128],
                                           qfin[n % 2][:, g * 128:(g + 1) * 128],
                                           ident[:, :])
                    KTR.inc(last)
                blk.tensor(pe3)

            def st_qtrev(n):
                ob, tt = n // 8, n % 8

                def dve6(v, ob=ob, tt=tt, n=n):
                    KTR.wait(v, ktr0 + n + 1)
                    off = ((ob * NT_OWN + tt) * 4) * 128
                    KTE.inc(v.tensor_copy(Q4[:, off:off + 512],
                                          ps_qtr[n % 2][:, 0:512]))
                blk.vector(dve6)

            NQ = 32
            for m in range(NQ + 6):
                if m % 8 == 0 and m // 8 < 4:
                    st_wqload(m // 8)
                if 0 <= m - 6 < NQ:
                    st_qtrev(m - 6)
                if m < NQ:
                    st_qmm(m)
                if 0 <= m - 1 < NQ:
                    st_qevac(m - 1)
                if 0 <= m - 2 < NQ:
                    st_qstats(m - 2)
                if 0 <= m - 3 < NQ:
                    st_qrstd(m - 3)
                if 0 <= m - 4 < NQ:
                    st_qscale(m - 4)
                if 0 <= m - 5 < NQ:
                    st_qtr(m - 5)
            if debug:
                def dbgh(g):
                    OST.inc(g.dma_start(out=DBG_H[...], in_=h1ownT[...]), 16)
                barrier()
                blk.gpsimd(dbgh)
            barrier()
        if debug:
            def dbg1(g):
                OST.inc(g.dma_start(out=DBG_K[...], in_=K_fm[...]), 16)
                OST.inc(g.dma_start(out=DBG_V[...], in_=V_aug[...]), 16)
                OST.inc(g.dma_start(out=DBG_Q[...], in_=Q4[...]), 16)
                OST.inc(g.dma_start(out=DBG_R[...], in_=rstdk[...]), 16)
            blk.gpsimd(dbg1)
            barrier()
        if stop_after == 3:
            sA.close()
            return nc

        sB = ExitStack()
        y_nB = sB.enter_context(nc.sbuf_tensor("y_nB", [128, NT_OWN, D], BF16,
                                               side="right"))

        with ExitStack() as s2:
            sb1 = lambda n, s, d: s2.enter_context(nc.sbuf_tensor(n, s, d))
            ps1 = lambda n, s, d: s2.enter_context(nc.psum_tensor(n, s, d))
            alloc_quant_scratch(s2, "b")
            P_buf = sb1("P_buf", [128, 8 * 512], BF16)
            y_tm = [sb1(f"ytm{i}", [128, D], F32) for i in range(2)]
            linv = sb1("linv", [128, 4], F32)
            ps_s = ps1("ps_s", [128, 4 * 512], F32)
            ps_o = [ps1(f"pso{i}", [128, 512], F32) for i in range(4)]

            qkd0, expd0, mskd0, avd0, oev0 = QKD.n, EXPD.n, MSKD.n, AVD.n, OEV.n
            yst0, yrs0, ynd0 = YST.n, YRS.n, YND.n
            trp0, tre0 = TRP.n, TRE.n
            qbases = (QBL.n, QTH.n, QNG.n, QPS.n, QWE.n, QST.n)
            qwe0, qst0, qps0 = QWE.n, QST.n, QPS.n

            nj_of = [4 * tau + 4 for tau in range(NT_OWN)]
            gc_of = [0] * NT_OWN
            dg_of = [0] * NT_OWN
            for tau in range(1, NT_OWN):
                gc_of[tau] = gc_of[tau - 1] + 4 * nj_of[tau - 1]
                dg_of[tau] = dg_of[tau - 1] + 16

            pend_rms = [None]
            pend_ynd = [None]

            for tau in range(ph2_taus):
                nj = nj_of[tau]
                gc_tau = gc_of[tau]
                dg_tau = dg_of[tau]

                LA = 4

                def emit_qk(t, tau, h, j, c):
                    EXPD.wait(t, expd0 + max(0, c // 2 - 1))
                    qoff = (h * NT_OWN + tau) * 4 * 128
                    so = (c % 4) * 512
                    QKD.inc(t.matmul(ps_s[:, so:so + 512],
                                     K_fm[:, h, j * 128:(j + 1) * 128],
                                     Q4[:, qoff:qoff + 512],
                                     start=True, stop=True))

                def emit_av(t, tau, h, j, c, nj, dcnt, s_i):
                    if j == 0 and s_i > 0:
                        OEV.wait(t, oev0 + s_i)
                    if j >= nj - 4:
                        MSKD.wait(t, mskd0 + dcnt)
                    else:
                        EXPD.wait(t, expd0 + c // 2 + 1)
                    last = None
                    po = (c % 8) * 512
                    for g in range(4):
                        last = t.matmul(ps_o[g][:, 0:129],
                                        P_buf[:, po + g * 128:po + (g + 1) * 128],
                                        V_aug[:, h, j, 0:129],
                                        start=(j == 0), stop=(j == nj - 1))
                    AVD.inc(last)

                def pe(t, tau=tau, nj=nj, gc_tau=gc_tau, dg_tau=dg_tau):
                    if tau == 0:
                        KTE.wait(t)
                        VEV.wait(t)
                    items = []
                    dcnt = dg_tau
                    for h in range(4):
                        for j in range(nj):
                            c = gc_tau + h * nj + j
                            if j >= nj - 4:
                                dcnt += 1
                            items.append((h, j, c, dcnt, 4 * tau + h))
                    for k, (h, j, c, dc_, s_i) in enumerate(items):
                        emit_qk(t, tau, h, j, c)
                        if k >= LA:
                            h2_, j2_, c2_, d2_, s2_ = items[k - LA]
                            emit_av(t, tau, h2_, j2_, c2_, nj, d2_, s2_)
                    for k in range(len(items) - LA, len(items)):
                        h2_, j2_, c2_, d2_, s2_ = items[k]
                        emit_av(t, tau, h2_, j2_, c2_, nj, d2_, s2_)
                blk.tensor(pe)

                def act(s, tau=tau, nj=nj, gc_tau=gc_tau, prev=pend_rms[0]):
                    for idx, i in enumerate(range(gc_tau // 2,
                                                  (gc_tau + 4 * nj) // 2)):
                        QKD.wait(s, qkd0 + 2 * i + 2)
                        AVD.wait(s, avd0 + max(0, 2 * i - 6))
                        po = (2 * i % 8) * 512
                        so = (2 * i % 4) * 512
                        EXPD.inc(s.activation(P_buf[:, po:po + 1024],
                                              ps_s[:, so:so + 1024], AF.Exp))
                        if idx == 3 and prev is not None:
                            prev(s)
                blk.scalar(act)

                def mk_rms(tau):
                    p = tau % 2

                    def rms(s, tau=tau, p=p):
                        YST.wait(s, yst0 + tau + 1)
                        i = s.activation(st1[:, 2 * p + 1:2 * p + 2],
                                         st1[:, 2 * p:2 * p + 1], AF.Ln)
                        asy(s, i)
                        YRS.inc(s.activation(st2[:, p:p + 1],
                                             st1[:, 2 * p + 1:2 * p + 2],
                                             AF.Exp, scale=-0.5))
                    return rms
                pend_rms[0] = mk_rms(tau)

                def dve(v, tau=tau, nj=nj, gc_tau=gc_tau, dg_tau=dg_tau,
                        prev=pend_ynd[0]):
                    c = gc_tau
                    dcnt = dg_tau
                    for h in range(4):
                        for j in range(nj):
                            if j >= nj - 4:
                                EXPD.wait(v, expd0 + c // 2 + 1)
                                dcnt += 1
                                po = (c % 8) * 512
                                MSKD.inc(v.tensor_tensor(
                                    P_buf[:, po:po + 512], P_buf[:, po:po + 512],
                                    masks[:, j - (nj - 4), :], ALU.mult))
                            c += 1
                        AVD.wait(v, avd0 + c)
                        i = None
                        for g in range(4):
                            i = v.reciprocal(linv[:, g:g + 1], ps_o[g][:, 128:129])
                        vs(v, i)
                        last = None
                        for g in range(4):
                            hh = 4 * h + g
                            last = v.tensor_scalar(
                                y_tm[tau % 2][:, hh * 128:(hh + 1) * 128],
                                ps_o[g][:, 0:128], linv[:, g:g + 1], None, ALU.mult)
                        OEV.inc(last)
                        if h == 0 and prev is not None:
                            prev(v)
                    p = tau % 2
                    v.wait_ge(OEV.h, OEV.n)
                    for g4 in range(4):
                        i = v.bn_stats(bn6[:, g4, :],
                                       y_tm[tau % 2][:, g4 * 512:(g4 + 1) * 512])
                    vs(v, i)
                    means = bn6[:, 0:4, 1::3]
                    ctvs = bn6[:, 0:4, 2::3]
                    st8v = st8[:, 0:8].rearrange("p (g t) -> p g t", g=4)
                    v.tensor_reduce(stL[:, 1:2], ctvs, AX.XY, ALU.add)
                    i = v.tensor_tensor(st8v, means, means, ALU.mult)
                    vs(v, i)
                    i = v.tensor_reduce(stL[:, 2:3], st8v, AX.XY, ALU.add)
                    vs(v, i)
                    i = v.tensor_scalar(stL[:, 3:4], stL[:, 1:2],
                                        1.0 / 2048, RMS_EPS, ALU.mult, ALU.add)
                    vs(v, i)
                    YST.inc(v.scalar_tensor_tensor(st1[:, 2 * p:2 * p + 1],
                                                   stL[:, 2:3],
                                                   0.125, stL[:, 3:4],
                                                   ALU.mult, ALU.add))
                blk.vector(dve)

                def mk_ynd(tau):
                    p = tau % 2

                    def ynd(v, tau=tau, p=p):
                        YRS.wait(v, yrs0 + tau + 1)
                        YND.inc(v.tensor_scalar(y_nB[:, tau, :],
                                                y_tm[tau % 2][:, :],
                                                st2[:, p:p + 1], None, ALU.mult))
                    return ynd
                pend_ynd[0] = mk_ynd(tau)

                if ph2_quant:
                    for k2 in range(2):
                        emit_quant_tile(24 + tau * 2 + k2)

            if pend_rms[0] is not None:
                blk.scalar(pend_rms[0])
            if pend_ynd[0] is not None:
                blk.vector(pend_ynd[0])
            barrier()
        if debug:
            def dbg2(g):
                OST.inc(g.dma_start(out=DBG_Y[...], in_=y_nB[...]), 16)
            blk.gpsimd(dbg2)
            barrier()
        if stop_after == 4 or ph2_taus < NT_OWN:
            sA.close()
            sB.close()
            return nc

        sA.close()

        sC = ExitStack()
        h2T = sC.enter_context(nc.sbuf_tensor("h2T", [128, 16, OWN], BF16))

        with ExitStack() as s3:
            sb1 = lambda n, s, d: s3.enter_context(nc.sbuf_tensor(n, s, d))
            ps1 = lambda n, s, d: s3.enter_context(nc.psum_tensor(n, s, d))
            wpT = sb1("wpT", [128, 16, D], BF16)
            x_f = [sb1(f"x3{i}", [128, D], F32) for i in range(2)]
            x2 = [sb1(f"x2{i}", [128, D], F32) for i in range(2)]
            h2 = [sb1(f"h2b{i}", [128, D], BF16) for i in range(2)]
            yT = [sb1(f"yT{i}", [128, D], BF16) for i in range(2)]
            ps_p = [ps1(f"psp{i}", [128, 512], F32) for i in range(4)]
            ps_tr = [ps1(f"pstr3{i}", [128, 1024], BF16) for i in range(2)]
            ps_try = ps1("ps_try", [128, D], BF16)

            def syw(sy_):
                QST.wait(sy_)
                for dc in range(16):
                    WTL.inc(sy_.dma_start(out=wpT[:, dc, :],
                                          in_=WP_EFF[0:D, dc * 128:(dc + 1) * 128],
                                          transpose=True), 16)
            blk.sync(syw)
            wpT_t = WTL.n

            xl0, mmd0, pje0, h2d0 = XL.n, MMD.n, PJE.n, H1D.n
            trp0, tre0, ost0 = TRP.n, TRE.n, OST.n
            ytp0, yte0 = YTP.n, YTE.n

            def emit_ytr3(t, tt):
                if tt >= 1:
                    YTE.wait(t, yte0 + tt)
                last = None
                for dc in range(16):
                    last = t.transpose(ps_try[:, dc * 128:(dc + 1) * 128],
                                       y_nB[:, tt, dc * 128:(dc + 1) * 128],
                                       ident[:, :])
                YTP.inc(last)

            def pe_ytr0(t):
                emit_ytr3(t, 0)
            blk.tensor(pe_ytr0)

            for tt in range(NT_OWN):
                bi = tt % 2

                def sy(sy_, tt=tt, bi=bi):
                    if tt >= 2:
                        PJE.wait(sy_, pje0 + 4 * (tt - 1))
                    XL.inc(sy_.dma_start(out=x_f[bi][:, :],
                                         in_=X_OWN[tt * 128:(tt + 1) * 128, :]), 16)
                blk.sync(sy)

                def pe(t, tt=tt, bi=bi):
                    if tt + 1 < NT_OWN:
                        emit_ytr3(t, tt + 1)
                    if tt == 0:
                        WTL.wait(t, wpT_t)
                    if tt >= 1:
                        PJE.wait(t, pje0 + 4 * tt)
                    YTE.wait(t, yte0 + tt + 1)
                    last = None
                    for dc in range(16):
                        for ob in range(4):
                            last = t.matmul(ps_p[ob][:, :],
                                            yT[bi][:, dc * 128:(dc + 1) * 128],
                                            wpT[:, dc, ob * 512:(ob + 1) * 512],
                                            start=(dc == 0), stop=(dc == 15))
                    MMD.inc(last)
                blk.tensor(pe)

                def act_yev(s, tt=tt, bi=bi):
                    YTP.wait(s, ytp0 + tt + 1)
                    if tt >= 2:
                        MMD.wait(s, mmd0 + tt - 1)
                    YTE.inc(s.activation(yT[bi][:, :], ps_try[:, :], AF.Identity))
                blk.scalar(act_yev)

                def dve1(v, tt=tt, bi=bi):
                    MMD.wait(v, mmd0 + tt + 1)
                    XL.wait(v, xl0 + 16 * (tt + 1))
                    if tt >= 2:
                        OST.wait(v, ost0 + 16 * (tt - 1))
                    for ob in range(4):
                        PJE.inc(v.tensor_tensor(x2[bi][:, ob * 512:(ob + 1) * 512],
                                                ps_p[ob][:, :],
                                                x_f[bi][:, ob * 512:(ob + 1) * 512],
                                                ALU.add))
                blk.vector(dve1)

                def h2guard(s, tt=tt):
                    if tt >= 2:
                        TRP.wait(s, trp0 + 4 * (tt - 1))

                def x2w(v):
                    v.wait_ge(PJE.h, PJE.n)
                emit_ln(h2d0 + tt, tt % 4, x2[bi], h2[bi], extra_dve_wait=x2w,
                        extra_act_wait=h2guard)

                def pst(g, tt=tt, bi=bi):
                    H1D.wait(g, h2d0 + tt + 1)
                    OST.inc(g.dma_start(out=OUT[tt * 128:(tt + 1) * 128, :],
                                        in_=x2[bi][:, :]), 16)
                blk.gpsimd(pst)

                emit_tr16(h2[bi],
                          lambda grp, tt=tt: h2T[:, grp * 4:(grp + 1) * 4,
                                                 tt * 128:(tt + 1) * 128],
                          ps_tr, h2d0 + tt, trp0, tre0, 4 * tt)
            barrier()
        if debug:
            def dbg3(g):
                OST.inc(g.dma_start(out=DBG_H2[...], in_=h2T[...]), 16)
            blk.gpsimd(dbg3)
            barrier()
        if stop_after == 5:
            sB.close()
            sC.close()
            return nc

        sB.close()

        with ExitStack() as s4:
            sb0 = lambda n, s, d: s4.enter_context(nc.sbuf_tensor(n, s, d))
            hdn = sb0("hdn", [128, NIC, 1024], BF16)
            dw = [sb0(f"dw{i}", [128, NIC, 128], BF16) for i in range(3)]
            ddb = [sb0(f"ddb{i}", [128, 1024], BF16) for i in range(2)]
            ddf = sb0("ddf", [128, 2, 1024], F32)

            hdn0 = HDN.n
            dwl0, mmdd0 = DWL.n, MMD.n
            dw_t = {}

            DWPRE = {28: (0, 0), 30: (0, 1), 32: (0, 2), 34: (0, 3),
                     36: (1, 0), 38: (1, 1), 40: (1, 2), 42: (1, 3)}

            def dw_chunk(g, oc, c):
                i0, i1 = c * 11, (c + 1) * 11
                DWL.inc(g.dma_start(
                    out=dw[oc % 3][:, i0:i1, :],
                    in_=DWT[oc].rearrange("p (c f) -> p c f", c=NIC)[:, i0:i1, :]),
                    16)
                if c == 3:
                    dw_t[oc] = DWL.n

            with ExitStack() as s4a:
                sb1 = lambda n, s, d: s4a.enter_context(nc.sbuf_tensor(n, s, d))
                ps1 = lambda n, s, d: s4a.enter_context(nc.psum_tensor(n, s, d))
                gw = [sb1(f"gw{i}", [128, 16, 128], BF16) for i in range(2)]
                uw = [sb1(f"uw{i}", [128, 16, 128], BF16) for i in range(2)]
                sgb = [sb1(f"sgb{i}", [128, 1024], BF16) for i in range(2)]
                ubb = [sb1(f"ubb{i}", [128, 1024], BF16) for i in range(2)]
                pool = [ps1(f"pp{i}", [128, 512], F32) for i in range(6)]

                fwl0, fwu0 = FWL.n, FWU.n
                mmg0, gue0, ube0 = MMG.n, GUE.n, UBE.n
                bank_owner = [None] * 6

                for ic in range(NIC):
                    def pload(g, ic=ic):
                        if ic >= 2:
                            MMG.wait(g, mmg0 + 4 * (ic - 2) + 2)
                        FWL.inc(g.dma_start(
                            out=gw[ic % 2][...],
                            in_=GWT[ic].rearrange("p (c f) -> p c f", c=16)), 16)
                        if ic >= 2:
                            MMG.wait(g, mmg0 + 4 * (ic - 2) + 4)
                        FWU.inc(g.dma_start(
                            out=uw[ic % 2][...],
                            in_=UWT[ic].rearrange("p (c f) -> p c f", c=16)), 16)
                        if ic in DWPRE:
                            dw_chunk(g, *DWPRE[ic])
                    blk.gpsimd(pload)

                    owner_snap = [bank_owner[(4 * ic + qi) % 6] for qi in range(4)]

                    def pe(t, ic=ic, owner_snap=owner_snap):
                        FWL.wait(t, fwl0 + 16 * (ic + 1))
                        for qi in range(4):
                            if qi == 2:
                                FWU.wait(t, fwu0 + 16 * (ic + 1))
                            b = (4 * ic + qi) % 6
                            own = owner_snap[qi]
                            if own is not None:
                                own[0].wait(t, own[1])
                            w = (gw if qi < 2 else uw)[ic % 2]
                            half = qi % 2
                            last = None
                            for dc in range(16):
                                last = t.matmul(
                                    pool[b][:, :], w[:, dc, :],
                                    h2T[:, dc, half * 512:(half + 1) * 512],
                                    start=(dc == 0), stop=(dc == 15))
                            MMG.inc(last)
                    blk.tensor(pe)

                    def act(s, ic=ic):
                        if ic >= 2:
                            HDN.wait(s, hdn0 + ic - 1)
                        for half in range(2):
                            MMG.wait(s, mmg0 + 4 * ic + 1 + half)
                            GUE.inc(s.activation(
                                sgb[ic % 2][:, half * 512:(half + 1) * 512],
                                pool[(4 * ic + half) % 6][:, :], AF.Silu,
                                scale=gs2[:, ic:ic + 1]))
                    blk.scalar(act)
                    bank_owner[(4 * ic) % 6] = (GUE, gue0 + 2 * ic + 1)
                    bank_owner[(4 * ic + 1) % 6] = (GUE, gue0 + 2 * ic + 2)

                    def dve(v, ic=ic):
                        for half in range(2):
                            MMG.wait(v, mmg0 + 4 * ic + 3 + half)
                            UBE.inc(v.tensor_scalar(
                                ubb[ic % 2][:, half * 512:(half + 1) * 512],
                                pool[(4 * ic + 2 + half) % 6][:, :],
                                us2[:, ic:ic + 1], None, ALU.mult))
                        UBE.wait(v, ube0 + 2 * ic + 2)
                        GUE.wait(v, gue0 + 2 * ic + 2)
                        HDN.inc(v.tensor_tensor(hdn[:, ic, :], sgb[ic % 2][:, :],
                                                ubb[ic % 2][:, :], ALU.mult))
                    blk.vector(dve)
                    bank_owner[(4 * ic + 2) % 6] = (UBE, ube0 + 2 * ic + 1)
                    bank_owner[(4 * ic + 3) % 6] = (UBE, ube0 + 2 * ic + 2)

            with ExitStack() as s4d:
                ps1 = lambda n, s, d: s4d.enter_context(nc.psum_tensor(n, s, d))
                ps_d = [ps1(f"psd{i}", [128, 1024], F32) for i in range(2)]
                ps_tro = [ps1(f"pstro{i}", [128, 1024], BF16) for i in range(2)]

                dse0, dte0, ktr0 = DSE.n, DTE.n, KTR.n
                oacc0 = OACC.n

                def dwload(oc):
                    def gld(g, oc=oc):
                        if oc >= 3:
                            MMD.wait(g, mmdd0 + oc - 2)
                        DWL.inc(g.dma_start(
                            out=dw[oc % 3][...],
                            in_=DWT[oc].rearrange("p (c f) -> p c f", c=NIC)), 16)
                        dw_t[oc] = DWL.n
                    blk.gpsimd(gld)

                def mmdn(oc):
                    def pe(t, oc=oc):
                        DWL.wait(t, dw_t[oc])
                        if oc == 0:
                            HDN.wait(t, hdn0 + NIC)
                        if oc >= 2:
                            DSE.wait(t, dse0 + oc - 1)
                        last = None
                        for icc in range(NIC):
                            for half in range(2):
                                last = t.matmul(
                                    ps_d[oc % 2][:, half * 512:(half + 1) * 512],
                                    dw[oc % 3][:, icc, :],
                                    hdn[:, icc, half * 512:(half + 1) * 512],
                                    start=(icc == 0), stop=(icc == NIC - 1))
                        MMD.inc(last)
                    blk.tensor(pe)

                def ddbev(oc):
                    def dve1(v, oc=oc):
                        MMD.wait(v, mmdd0 + oc + 1)
                        if oc >= 2:
                            KTR.wait(v, ktr0 + oc - 1)
                        DSE.inc(v.tensor_scalar(ddb[oc % 2][:, :],
                                                ps_d[oc % 2][:, :],
                                                ds2[:, oc:oc + 1], None, ALU.mult))
                    blk.vector(dve1)

                OUT_T = OUT.rearrange("(t p) c -> p t c", p=128)

                def trout(oc):
                    def pe2(t, oc=oc):
                        DSE.wait(t, dse0 + oc + 1)
                        if oc >= 2:
                            DTE.wait(t, dte0 + oc - 1)
                        last = None
                        for tt in range(NT_OWN):
                            last = t.transpose(
                                ps_tro[oc % 2][:, tt * 128:(tt + 1) * 128],
                                ddb[oc % 2][:, tt * 128:(tt + 1) * 128],
                                ident[:, :])
                        KTR.inc(last)
                    blk.tensor(pe2)

                    def dve2(v, oc=oc):
                        KTR.wait(v, ktr0 + oc + 1)
                        if oc >= 2:
                            OACC.wait(v, oacc0 + 16 * (oc - 1))
                        DTE.inc(v.tensor_copy(ddf[:, oc % 2, :],
                                              ps_tro[oc % 2][:, :]))
                    blk.vector(dve2)

                def pacc_emit(oc):
                    def pacc(g, oc=oc):
                        DTE.wait(g, dte0 + oc + 1)
                        OST.wait(g, 16 * 8)
                        OACC.inc(g.dma_start(
                            out=OUT_T[:, :, oc * 128:(oc + 1) * 128],
                            in_=ddf[:, oc % 2, :].rearrange("p (t c) -> p t c",
                                                            t=NT_OWN),
                            accum_op=ALU.add), 16)
                    blk.gpsimd(pacc)

                mmdn(0)
                mmdn(1)
                for oc in range(16):
                    if oc + 2 < 16:
                        dwload(oc + 2)
                    ddbev(oc)
                    trout(oc)
                    if oc + 2 < 16:
                        mmdn(oc + 2)
                    if oc >= 1:
                        pacc_emit(oc - 1)
                pacc_emit(15)

            def fin(g):
                OACC.wait(g)
                OST.wait(g)
            blk.gpsimd(fin)

        sC.close()

    return nc


def _host_prepare(L0_attn_x, L0_attn_w_qkv, L0_attn_w_proj, L0_attn_q_gain,
                  L0_ffn_gate_w, L0_ffn_up_w, L0_ffn_down_w,
                  L0_ffn_gate_s, L0_ffn_up_s, L0_ffn_down_s):
    f32 = np.float32
    x = np.asarray(L0_attn_x, f32)
    wq = np.ascontiguousarray(np.asarray(L0_attn_w_qkv, f32))
    wp = np.ascontiguousarray(np.asarray(L0_attn_w_proj, f32))
    qg = np.asarray(L0_attn_q_gain, f32)
    gwm = np.asarray(L0_ffn_gate_w, f32)
    uwm = np.asarray(L0_ffn_up_w, f32)
    dwm = np.asarray(L0_ffn_down_w, f32)
    gs = np.asarray(L0_ffn_gate_s, f32)
    us = np.asarray(L0_ffn_up_s, f32)
    ds = np.asarray(L0_ffn_down_s, f32)

    gwt = np.ascontiguousarray(
        gwm.reshape(NIC, 128, 16, 128).transpose(0, 3, 2, 1)).reshape(NIC, 128, 2048)
    uwt = np.ascontiguousarray(
        uwm.reshape(NIC, 128, 16, 128).transpose(0, 3, 2, 1)).reshape(NIC, 128, 2048)
    dwt = np.ascontiguousarray(
        dwm.reshape(16, 128, NIC, 128).transpose(0, 3, 2, 1)).reshape(16, 128,
                                                                      NIC * 128)
    gs2 = np.ascontiguousarray(gs.reshape(NIC, 128).T)
    us2 = np.ascontiguousarray(us.reshape(NIC, 128).T)
    ds2 = np.ascontiguousarray(ds.reshape(16, 128).T)
    qg2 = np.ascontiguousarray(np.tile(qg.reshape(1, NH), (128, 1)))

    inv_freq = (1.0 / (10000.0 ** (np.arange(0, HD, 2, dtype=f32) / HD))).astype(f32)
    t_all = np.arange(SEQ, dtype=f32)
    freqs = np.outer(t_all, inv_freq).astype(f32)
    cos_all = np.cos(freqs).astype(BF)
    sin_all = np.sin(freqs).astype(BF)
    cosk = np.ascontiguousarray(cos_all.reshape(NT_ALL, 128, 64).transpose(1, 0, 2))
    sink = np.ascontiguousarray(sin_all.reshape(NT_ALL, 128, 64).transpose(1, 0, 2))
    ident = np.eye(128, dtype=BF)

    per_core = []
    for core in range(8):
        b, lane = core // 4, core % 4
        x_all = np.ascontiguousarray(x[b])
        x_own = np.ascontiguousarray(x[b, lane::4])
        cos_own = np.ascontiguousarray(
            cos_all[lane::4].reshape(NT_OWN, 128, 64).transpose(1, 0, 2))
        sin_own = np.ascontiguousarray(
            sin_all[lane::4].reshape(NT_OWN, 128, 64).transpose(1, 0, 2))
        kk = np.arange(128)[:, None]
        qq = np.arange(128)[None, :]
        m = np.zeros((128, 4, 512), dtype=BF)
        for i in range(4):
            mi = ((128 * i + kk) <= (4 * qq + lane)).astype(BF)
            m[:, i, :] = np.tile(mi, (1, 4))
        per_core.append(dict(
            x_all=x_all, x_own=x_own, wq=wq, wp=wp,
            gwt=gwt, uwt=uwt, dwt=dwt, gs2=gs2, us2=us2, ds2=ds2, qg2=qg2,
            cosk=cosk, sink=sink, cosq=cos_own, sinq=sin_own,
            msk=np.ascontiguousarray(m), idn=ident,
        ))
    return per_core


def kernel(L0_attn_x, L0_attn_w_qkv, L0_attn_w_proj, L0_attn_q_gain,
           L0_ffn_gate_w, L0_ffn_up_w, L0_ffn_down_w,
           L0_ffn_gate_s, L0_ffn_up_s, L0_ffn_down_s):
    in_maps = _host_prepare(L0_attn_x, L0_attn_w_qkv, L0_attn_w_proj,
                            L0_attn_q_gain, L0_ffn_gate_w, L0_ffn_up_w,
                            L0_ffn_down_w, L0_ffn_gate_s, L0_ffn_up_s,
                            L0_ffn_down_s)
    nc = build_program()
    res = run_bass_kernel_spmd(nc, in_maps, core_ids=list(range(8)))
    out = np.empty((B, SEQ, D), np.float32)
    for core in range(8):
        b, lane = core // 4, core % 4
        out[b, lane::4] = res.results[core]["out_own"]
    return out
